# version 01
# baseline (speedup 1.0000x reference)
import numpy as np
import ml_dtypes
import concourse.bass as bass
import concourse.mybir as mybir
from contextlib import ExitStack

F32 = mybir.dt.float32
BF16 = mybir.dt.bfloat16
FP8 = mybir.dt.float8e4
AF = mybir.ActivationFunctionType
OP = mybir.AluOpType

NCORES = 8
T = 1024
D = 2048
FC = 16
J = 8192
JC = 64
NH, HD, RD = 16, 128, 64
KV, CC = 512, 4
QC, QCC = 1024, 8
SK, KC = 8192, 64
SCALE = float(1.0 / np.sqrt(HD + RD))
C44715 = float(np.float32(ml_dtypes.bfloat16(0.044715)))
SQ2PI = float(np.float32(np.sqrt(2 / np.pi).astype(ml_dtypes.bfloat16)))
EPS = 1e-5


class Prog:
    def __init__(self, nc, ctx):
        self.nc, self.ctx = nc, ctx
        self.sems, self.cnt = {}, {}

    def sem(self, name):
        if name not in self.sems:
            self.sems[name] = self.ctx.enter_context(self.nc.semaphore(name))
            self.cnt[name] = 0
        return self.sems[name]

    def inc(self, instr, name, n=1):
        instr.then_inc(self.sem(name), n)
        self.cnt[name] += n
        return instr

    def val(self, name):
        self.sem(name)
        return self.cnt[name]

    def wait(self, eng, name, val=None):
        eng.wait_ge(self.sem(name), self.cnt[name] if val is None else val)

    def ser_begin(self):
        self._ser = []

    def ser(self, eng_name, fn, inc=1):
        self._ser.append((eng_name, fn, inc))

    def ser_emit(self, block, sem_name, pred=None):
        bodies = {"sync": block.sync, "tensor": block.tensor,
                  "vector": block.vector, "scalar": block.scalar,
                  "gpsimd": block.gpsimd}
        ops = self._ser
        sm = self.sem(sem_name)
        starts = []
        v = self.cnt[sem_name]
        for (_, _, inc) in ops:
            starts.append(v)
            v += inc
        self.cnt[sem_name] = v
        for ename in ("sync", "tensor", "vector", "scalar", "gpsimd"):
            todo = [(i, f, inc) for i, (en, f, inc) in enumerate(ops)
                    if en == ename]
            if not todo:
                continue

            def mk(todo_list):
                def _body(eng):
                    if pred is not None:
                        eng.wait_ge(self.sem(pred[0]), pred[1])
                    for i, f, inc in todo_list:
                        eng.wait_ge(sm, starts[i])
                        f(eng).then_inc(sm, inc)
                return _body
            bodies[ename](mk(todo))
        return (sem_name, v)


def build(stop_after=None):
    nc = bass.Bass("TRN2", target_bir_lowering=False)
    ctx = ExitStack()
    P = Prog(nc, ctx)

    def dri(name, shape, dtype=F32):
        return nc.dram_tensor(name, shape, dtype, kind="ExternalInput").ap()

    x_in = dri("x", [T, D])
    w_kvd = dri("w_kvd", [KV, D])
    b_kvd_c = dri("b_kvd_c", [CC, 128])
    w_ku = dri("w_ku", [D, KV])
    b_ku = dri("b_ku", [FC, 128])
    w_vu = dri("w_vu", [D, KV])
    b_vu = dri("b_vu", [1, D])
    w_kr = dri("w_kr", [QC, KV])
    b_kr = dri("b_kr", [8, 128])
    w_qd = dri("w_qd", [QC, D])
    b_qd = dri("b_qd", [QCC, 128])
    w_qu = dri("w_qu", [D, QC])
    b_qu = dri("b_qu", [FC, 128])
    w_qr = dri("w_qr", [QC, QC])
    b_qr = dri("b_qr", [8, 128])
    w_out = dri("w_out", [D, D])
    b_out = dri("b_out", [FC, 128])
    w_g0 = dri("w_g0", [D, J])
    w_g1 = dri("w_g1", [D, J])
    w_l = dri("w_l", [J, D])
    identf = dri("identf", [128, 128])
    identb = dri("identb", [128, 128], BF16)
    rt = dri("rt", [8, 128, 128], BF16)

    out = nc.dram_tensor("out", [T, D], F32, kind="ExternalOutput").ap()

    ag_in = nc.dram_tensor("ag_in", [KV, T], BF16)
    ag_out = nc.dram_tensor("ag_out", [NCORES * KV, T], BF16)
    kc_scr = nc.dram_tensor("kc_scr", [NH, 128, SK], BF16)
    kr_scr = nc.dram_tensor("kr_scr", [NH, RD, SK], BF16)
    v_scr = nc.dram_tensor("v_scr", [NH, KC, 128, HD], BF16)
    qcs_scr = nc.dram_tensor("qcs_scr", [NH, 128, T], BF16)
    qrs_scr = nc.dram_tensor("qrs_scr", [NH, RD, T], BF16)
    wt_k = nc.dram_tensor("wt_k", [KV, D], BF16)
    wt_v = nc.dram_tensor("wt_v", [KV, D], BF16)
    wt_kr = nc.dram_tensor("wt_kr", [KV, QC], BF16)
    wt_q = nc.dram_tensor("wt_q", [QC, D], BF16)
    wt_qr = nc.dram_tensor("wt_qr", [QC, QC], BF16)
    wt_o = nc.dram_tensor("wt_o", [D, D], BF16)

    def sbp(n_, sh, dt):
        return ctx.enter_context(nc.sbuf_tensor(n_, sh, dt))

    XT = sbp("XT", [128, FC, T], F32)
    IDF = sbp("IDF", [128, 128], F32)
    IDB = sbp("IDB", [128, 128], BF16)
    RTS = sbp("RTS", [128, 8, 128], BF16)
    ONES = sbp("ONES", [128, 1], BF16)
    ONESM = sbp("ONESM", [1, 128], BF16)
    XLN8 = sbp("XLN8", [128, FC, T], FP8)
    QDB = sbp("QDB", [128, QCC], F32)
    QUB = sbp("QUB", [128, FC], F32)
    BQR = sbp("BQR", [128, 8], F32)
    CKC = sbp("CKC", [128, FC], F32)
    CKR = sbp("CKR", [128, 8], F32)
    COUT = sbp("COUT", [128, FC], F32)
    BHS = sbp("BHS", [128, NH], F32)

    blk_cm = nc.Block()
    block = blk_cm.__enter__()

    p0 = ExitStack()
    STG = [p0.enter_context(nc.sbuf_tensor(f"STG{i}", [128, D], F32)) for i in range(2)]
    PT = [p0.enter_context(nc.psum_tensor(f"PT{i}", [128, 128], F32)) for i in range(2)]

    @block.sync
    def _(s):
        P.inc(s.dma_start(out=IDF[:], in_=identf), "c_ld", 16)
        P.inc(s.dma_start(out=IDB[:], in_=identb), "c_ld", 16)
        P.inc(s.dma_start(out=RTS[:], in_=rt.rearrange("c p n -> p c n")), "c_ld", 16)
        for tc in range(8):
            if tc >= 2:
                P.wait(s, "x_tp", (tc - 1) * FC)
            P.inc(s.dma_start(out=STG[tc % 2][:],
                              in_=x_in[tc * 128:(tc + 1) * 128, :]),
                  f"x_ld{tc % 2}", 16)

    @block.vector
    def _(v):
        v.memset(ONES[:], 1.0)
        P.inc(v.memset(ONESM[:], 1.0), "c_init", 1)

    @block.tensor
    def _(t):
        t.wait_ge(P.sem("c_ld"), 48)
        for tc in range(8):
            t.wait_ge(P.sem(f"x_ld{tc % 2}"), 16 * (tc // 2 + 1))
            for fc in range(FC):
                n = tc * FC + fc
                if n >= 2:
                    t.wait_ge(P.sem("x_ev"), n - 1)
                P.inc(t.transpose(PT[n % 2][:],
                                  STG[tc % 2][:, fc * 128:(fc + 1) * 128],
                                  IDF[:]), "x_tp", 1)

    @block.vector
    def _(v):
        for tc in range(8):
            for fc in range(FC):
                n = tc * FC + fc
                v.wait_ge(P.sem("x_tp"), n + 1)
                P.inc(v.tensor_copy(XT[:, fc, tc * 128:(tc + 1) * 128],
                                    PT[n % 2][:]), "x_ev", 1)

    p0_done = ("x_ev", P.val("x_ev"))
    p0.close()

    def emit_ln(tag, OUTT, pred):
        g = ExitStack()
        nsb = lambda n_, sh, dt: g.enter_context(nc.sbuf_tensor(tag + n_, sh, dt))
        nps = lambda n_, sh, dt: g.enter_context(nc.psum_tensor(tag + n_, sh, dt))
        NB = 2
        SQ = nsb("SQ", [128, NB, T], F32)
        RA = nsb("RA", [128, NB, T], F32)
        HB = nsb("HB", [128, NB, T], BF16)
        LB = nsb("LB", [128, NB, T], BF16)
        L2B = nsb("L2B", [128, NB, T], BF16)
        ROWS = nsb("ROWS", [1, 6, T], F32)
        RWB = nsb("RWB", [1, 6, T], BF16)
        MRS = nsb("MRS", [128, T], F32)
        RRS = nsb("RRS", [128, T], F32)
        XMS = nsb("XMS", [128, T], F32)
        PSX = nps("PSX", [1, T], F32)
        PSXX = nps("PSXX", [1, T], F32)
        PM = nps("PM", [128, T], F32)
        PR = nps("PR", [128, T], F32)
        NBAT = FC // NB

        def row(i):
            return ROWS[:, i, :]

        sS, sM, sQ_, sF, sR, sO = ("ln_sp", "ln_mm", "ln_sq", "ln_fin",
                                   "ln_rep", "ln_out")
        bS, bM, bQ, bF, bR, bO = (P.val(sS), P.val(sM), P.val(sQ_),
                                  P.val(sF), P.val(sR), P.val(sO))

        @block.scalar
        def _(sc):
            P.wait(sc, pred[0], pred[1])
            for b in range(NBAT):
                if b > 0:
                    P.wait(sc, sS, bS + NBAT + b)
                P.inc(sc.activation(SQ[:], XT[:, NB * b:NB * (b + 1), :],
                                    AF.Square), sQ_, 1)

        @block.vector
        def _(v):
            P.wait(v, pred[0], pred[1])
            for b in range(2 * NBAT):
                if b < NBAT:
                    srcx = XT[:, NB * b:NB * (b + 1), :]
                else:
                    P.wait(v, sQ_, bQ + b - NBAT + 1)
                    srcx = SQ[:]
                if b >= 1:
                    P.wait(v, sM, bM + b)
                v.tensor_copy(HB[:], srcx)
                v.tensor_tensor(RA[:], srcx, HB[:], OP.subtract)
                v.tensor_copy(LB[:], RA[:])
                v.tensor_tensor(RA[:], RA[:], LB[:], OP.subtract)
                P.inc(v.tensor_copy(L2B[:], RA[:]), sS, 1)

        @block.tensor
        def _(t):
            P.wait(t, "c_init", 1)
            for b in range(2 * NBAT):
                ps = PSX if b < NBAT else PSXX
                t.wait_ge(P.sem(sS), bS + b + 1)
                li = None
                for buf in (HB, LB, L2B):
                    for fi in range(NB):
                        first = (b % NBAT == 0 and buf is HB and fi == 0)
                        last = (b % NBAT == NBAT - 1 and buf is L2B
                                and fi == NB - 1)
                        for hh in range(2):
                            li = t.matmul(ps[:, hh * 512:(hh + 1) * 512],
                                          ONES[:],
                                          buf[:, fi, hh * 512:(hh + 1) * 512],
                                          start=first, stop=last)
                P.inc(li, sM, 1)

        @block.scalar
        def _(sc):
            P.wait(sc, sM, bM + NBAT)
            P.inc(sc.activation(row(0), PSX[:], AF.Copy, scale=1.0 / D), sF, 1)
            P.wait(sc, sM, bM + 2 * NBAT)
            P.inc(sc.activation(row(1), PSXX[:], AF.Copy, scale=1.0 / D), sF, 1)

        @block.vector
        def _(v):
            P.wait(v, sF, bF + 2)
            v.tensor_tensor(row(2), row(0), row(0), OP.mult)
            v.tensor_tensor(row(1), row(1), row(2), OP.subtract)
            P.inc(v.tensor_scalar_add(row(1), row(1), EPS), sF, 1)

        @block.scalar
        def _(sc):
            P.wait(sc, sF, bF + 3)
            P.inc(sc.activation(row(3), row(1), AF.Sqrt), sF, 1)

        @block.vector
        def _(v):
            P.wait(v, sF, bF + 4)
            v.reciprocal(row(4), row(3))
            v.tensor_tensor(row(2), row(1), row(4), OP.mult)
            v.tensor_tensor(row(2), row(3), row(2), OP.add)
            v.tensor_scalar_mul(row(2), row(2), 0.5)
            v.reciprocal(row(4), row(2))
            v.tensor_tensor(row(3), row(2), row(4), OP.mult)
            v.tensor_scalar(row(3), row(3), 2.0, -1.0, OP.subtract, OP.mult)
            v.tensor_tensor(row(5), row(4), row(3), OP.mult)
            li = None
            for i, ridx in ((0, 0), (3, 5)):
                srcr = row(ridx)
                v.tensor_copy(RWB[:, i, :], srcr)
                v.tensor_tensor(row(3), srcr, RWB[:, i, :], OP.subtract)
                v.tensor_copy(RWB[:, i + 1, :], row(3))
                v.tensor_tensor(row(3), row(3), RWB[:, i + 1, :], OP.subtract)
                li = v.tensor_copy(RWB[:, i + 2, :], row(3))
            P.inc(li, sF, 1)

        @block.tensor
        def _(t):
            P.wait(t, sF, bF + 5)
            for ps, base in ((PM, 0), (PR, 3)):
                li = None
                for i in range(3):
                    for hh in range(2):
                        li = t.matmul(ps[:, hh * 512:(hh + 1) * 512], ONESM[:],
                                      RWB[:, base + i, hh * 512:(hh + 1) * 512],
                                      start=(i == 0), stop=(i == 2))
                P.inc(li, sR, 1)

        @block.vector
        def _(v):
            P.wait(v, sR, bR + 2)
            v.tensor_copy(MRS[:], PM[:])
            v.tensor_copy(RRS[:], PR[:])
            for fc in range(FC):
                v.tensor_tensor(XMS[:], XT[:, fc, :], MRS[:], OP.subtract)
                P.inc(v.tensor_tensor(OUTT[:, fc, :], XMS[:], RRS[:], OP.mult),
                      sO, 1)

        done = (sO, P.val(sO))
        g.close()
        return done

    def emit_ffn(tag, pred):
        g = ExitStack()
        nsb = lambda n_, sh, dt: g.enter_context(nc.sbuf_tensor(tag + n_, sh, dt))
        nps = lambda n_, sh, dt: g.enter_context(nc.psum_tensor(tag + n_, sh, dt))
        W0B = [nsb(f"W0B{i}", [128, FC, 128], FP8) for i in range(2)]
        W1B = [nsb(f"W1B{i}", [128, FC, 128], FP8) for i in range(2)]
        GATE = [nsb(f"GATE{i}", [128, 4, 512], BF16) for i in range(2)]
        FF1B = [nsb(f"FF1B{i}", [128, 4, 512], BF16) for i in range(2)]
        TM1 = nsb("TM1", [128, 4, 512], BF16)
        TM2 = nsb("TM2", [128, 4, 512], BF16)
        ACT8 = nsb("ACT8", [128, JC, T], FP8)
        WLE = [nsb(f"WLE{i}", [128, JC, 128], FP8) for i in range(2)]
        OBF = [nsb(f"OBF{i}", [128, 512], BF16) for i in range(2)]
        PG = [nps(f"PG{i}", [128, 512], F32) for i in range(2)]
        PF = [nps(f"PF{i}", [128, 512], F32) for i in range(2)]
        PO = [nps(f"PO{i}", [128, 512], F32) for i in range(2)]

        sWL, sMM, sEV, sGD, sTH, sTI, sWE, sPM, sOB, sRS = (
            "ff_wl", "ff_mm", "ff_ev", "ff_gd", "ff_th", "ff_ti", "ff_we",
            "ff_pm", "ff_ob", "ff_rs")
        bWL0, bWL1 = P.val(sWL + "0"), P.val(sWL + "1")
        bMM, bEV, bGD, bTH, bTI = (P.val(sMM), P.val(sEV), P.val(sGD),
                                   P.val(sTH), P.val(sTI))
        bWE0, bWE1 = P.val(sWE + "0"), P.val(sWE + "1")
        bPM, bOB, bRS = P.val(sPM), P.val(sOB), P.val(sRS)
        NJT = 2 * JC

        @block.gpsimd
        def _(gp):
            P.wait(gp, pred[0], pred[1])
            for n in range(NJT):
                tt, jc = divmod(n, JC)
                if n >= 2:
                    P.wait(gp, sMM, bMM + n - 1)
                src0 = w_g0.rearrange("(fc p) j -> p fc j", p=128)[:, :, jc * 128:(jc + 1) * 128]
                src1 = w_g1.rearrange("(fc p) j -> p fc j", p=128)[:, :, jc * 128:(jc + 1) * 128]
                P.inc(gp.dma_start(out=W0B[n % 2][:], in_=src0), sWL + str(n % 2), 16)
                P.inc(gp.dma_start(out=W1B[n % 2][:], in_=src1), sWL + str(n % 2), 16)

        @block.tensor
        def _(t):
            P.wait(t, pred[0], pred[1])
            for n in range(NJT):
                tt, jc = divmod(n, JC)
                t.wait_ge(P.sem(sWL + str(n % 2)), (bWL0 if n % 2 == 0 else bWL1) + 32 * (n // 2 + 1))
                if n >= 2:
                    t.wait_ge(P.sem(sEV), bEV + 2 * (n - 1))
                li = None
                for fc in range(FC):
                    li = t.matmul(PG[n % 2][:], W0B[n % 2][:, fc, :],
                                  XLN8[:, fc, tt * 512:(tt + 1) * 512],
                                  start=(fc == 0), stop=(fc == FC - 1))
                for fc in range(FC):
                    li = t.matmul(PF[n % 2][:], W1B[n % 2][:, fc, :],
                                  XLN8[:, fc, tt * 512:(tt + 1) * 512],
                                  start=(fc == 0), stop=(fc == FC - 1))
                P.inc(li, sMM, 1)

        @block.scalar
        def _(sc):
            for n in range(NJT):
                tt, jc = divmod(n, JC)
                gslot = jc % 4
                gbuf = (n // 4) % 2
                P.wait(sc, sMM, bMM + n + 1)
                if n >= 8 and gslot == 0:
                    P.wait(sc, sGD, bGD + n // 4 - 1)
                P.inc(sc.activation(GATE[gbuf][:, gslot, :], PG[n % 2][:],
                                    AF.Copy), sEV, 1)
                P.inc(sc.activation(FF1B[gbuf][:, gslot, :], PF[n % 2][:],
                                    AF.Copy), sEV, 1)
                if gslot == 3:
                    gi = n // 4
                    P.wait(sc, sTI, bTI + gi + 1)
                    P.inc(sc.activation(TM2[:], TM1[:], AF.Tanh), sTH, 1)

        @block.vector
        def _(v):
            for gi in range(32):
                gbuf = gi % 2
                tt = gi // 16
                P.wait(v, sEV, bEV + (gi + 1) * 8)
                xg = GATE[gbuf][:]
                v.tensor_tensor(TM1[:], xg, xg, OP.mult)
                v.tensor_tensor(TM2[:], TM1[:], xg, OP.mult)
                v.tensor_scalar_mul(TM1[:], TM2[:], C44715)
                v.tensor_tensor(TM2[:], xg, TM1[:], OP.add)
                P.inc(v.tensor_scalar_mul(TM1[:], TM2[:], SQ2PI), sTI, 1)
                P.wait(v, sTH, bTH + gi + 1)
                v.tensor_scalar(TM1[:], TM2[:], 1.0, 0.5, OP.add, OP.mult)
                v.tensor_tensor(TM2[:], xg, TM1[:], OP.mult)
                v.tensor_tensor(TM1[:], TM2[:], FF1B[gbuf][:], OP.mult)
                jbase = (gi % 16) * 4
                P.inc(v.tensor_copy(
                    ACT8[:, jbase:jbase + 4, tt * 512:(tt + 1) * 512],
                    TM1[:]), sGD, 1)

        @block.gpsimd
        def _(gp):
            for ec in range(FC):
                if ec >= 2:
                    P.wait(gp, sPM, bPM + 2 * (ec - 1))
                src = w_l.rearrange("(kc p) e -> p kc e", p=128)[:, :, ec * 128:(ec + 1) * 128]
                P.inc(gp.dma_start(out=WLE[ec % 2][:], in_=src),
                      sWE + str(ec % 2), 16)

        @block.tensor
        def _(t):
            P.wait(t, sGD, bGD + 32)
            for ec in range(FC):
                t.wait_ge(P.sem(sWE + str(ec % 2)), (bWE0 if ec % 2 == 0 else bWE1) + 16 * (ec // 2 + 1))
                for tt in range(2):
                    n = ec * 2 + tt
                    if n >= 2:
                        t.wait_ge(P.sem(sOB), bOB + n - 1)
                    li = None
                    for jc in range(JC):
                        li = t.matmul(PO[n % 2][:], WLE[ec % 2][:, jc, :],
                                      ACT8[:, jc, tt * 512:(tt + 1) * 512],
                                      start=(jc == 0), stop=(jc == JC - 1))
                    P.inc(li, sPM, 1)

        @block.scalar
        def _(sc):
            for ec in range(FC):
                for tt in range(2):
                    n = ec * 2 + tt
                    P.wait(sc, sPM, bPM + n + 1)
                    if n >= 2:
                        P.wait(sc, sRS, bRS + n - 1)
                    P.inc(sc.activation(OBF[n % 2][:], PO[n % 2][:], AF.Copy),
                          sOB, 1)

        @block.vector
        def _(v):
            for ec in range(FC):
                for tt in range(2):
                    n = ec * 2 + tt
                    P.wait(v, sOB, bOB + n + 1)
                    P.inc(v.scalar_tensor_tensor(
                        XT[:, ec, tt * 512:(tt + 1) * 512], OBF[n % 2][:], 0.5,
                        XT[:, ec, tt * 512:(tt + 1) * 512], OP.mult, OP.add),
                        sRS, 1)

        done = (sRS, P.val(sRS))
        g.close()
        return done

    def emit_attention(pred, sub=None):
        gat = ExitStack()
        QCL = gat.enter_context(nc.sbuf_tensor("QCL", [128, QCC, T], BF16))
        ga = ExitStack()
        XLN2 = ga.enter_context(nc.sbuf_tensor("XLN2", [128, FC, T], BF16))
        ln2_done = emit_ln("l2", XLN2, pred)
        if sub == "ln2":
            ga.close(); gat.close()
            return ln2_done

        gb = ExitStack()
        nsb = lambda n_, sh, dt: gb.enter_context(nc.sbuf_tensor(n_, sh, dt))
        nps = lambda n_, sh, dt: gb.enter_context(nc.psum_tensor(n_, sh, dt))
        KV0L = nsb("KV0L", [128, CC, T], BF16)
        BQDN = nsb("BQDN", [QCC, 128], F32)
        TPB = [nps(f"TPB{i}", [128, 128], BF16) for i in range(2)]
        TPFb = nps("TPFb", [128, 128], F32)
        PKV = [nps(f"PKV{i}", [128, 512], F32) for i in range(2)]

        @block.sync
        def _(s):
            P.wait(s, ln2_done[0], ln2_done[1])
            P.inc(s.dma_start(out=BQDN[:], in_=b_qd), "at_bl", 16)

        gb1 = ExitStack()
        WKVN = gb1.enter_context(nc.sbuf_tensor("WKVN", [128, CC, D], BF16))
        WKVT = gb1.enter_context(nc.sbuf_tensor("WKVT", [128, FC, KV], BF16))

        @block.gpsimd
        def _(gp):
            P.wait(gp, ln2_done[0], ln2_done[1])
            P.inc(gp.dma_start(out=WKVN[:], in_=w_kvd.rearrange(
                "(c p) f -> p c f", p=128)), "at_wl", 16)

        @block.tensor
        def _(t):
            P.wait(t, "at_wl", 16)
            n = 0
            for cc in range(CC):
                for fc in range(FC):
                    if n >= 2:
                        P.wait(t, "at_we", n - 1)
                    P.inc(t.transpose(TPB[n % 2][:],
                                      WKVN[:, cc, fc * 128:(fc + 1) * 128],
                                      IDB[:]), "at_wt", 1)
                    n += 1

        @block.vector
        def _(v):
            n = 0
            for cc in range(CC):
                for fc in range(FC):
                    v.wait_ge(P.sem("at_wt"), n + 1)
                    P.inc(v.tensor_copy(WKVT[:, fc, cc * 128:(cc + 1) * 128],
                                        TPB[n % 2][:]), "at_we", 1)
                    n += 1

        @block.tensor
        def _(t):
            P.wait(t, ln2_done[0], ln2_done[1])
            P.wait(t, "at_we", 64)
            n = 0
            for cc in range(CC):
                for tt in range(2):
                    if n >= 2:
                        P.wait(t, "kv_ev", n - 1)
                    li = None
                    for fc in range(FC):
                        li = t.matmul(PKV[n % 2][:],
                                      WKVT[:, fc, cc * 128:(cc + 1) * 128],
                                      XLN2[:, fc, tt * 512:(tt + 1) * 512],
                                      start=(fc == 0), stop=(fc == FC - 1))
                    P.inc(li, "kv_mm", 1)
                    n += 1

        @block.scalar
        def _(sc):
            n = 0
            for cc in range(CC):
                for tt in range(2):
                    P.wait(sc, "kv_mm", n + 1)
                    P.inc(sc.activation(KV0L[:, cc, tt * 512:(tt + 1) * 512],
                                        PKV[n % 2][:], AF.Copy), "kv_ev", 1)
                    n += 1

        @block.gpsimd
        def _(gp):
            P.wait(gp, "kv_ev", 8)
            P.inc(gp.dma_start(out=ag_in.ap().rearrange("(c p) t -> p c t", p=128),
                               in_=KV0L[:]), "ag_w", 16)
            gp.wait_ge(P.sem("ag_w"), 16)
            P.inc(gp.collective_compute(
                "AllGather", OP.bypass,
                replica_groups=[list(range(NCORES))],
                ins=[ag_in.ap().opt()], outs=[ag_out.ap().opt()]), "ag_c", 1)

        gb1.close()

        gb2 = ExitStack()
        WQDN = gb2.enter_context(nc.sbuf_tensor("WQDN", [128, QCC, D], BF16))
        WQDT = gb2.enter_context(nc.sbuf_tensor("WQDT", [128, FC, QC], BF16))

        @block.gpsimd
        def _(gp):
            P.wait(gp, "kv_mm", 8)
            P.inc(gp.dma_start(out=WQDN[:], in_=w_qd.rearrange(
                "(c p) f -> p c f", p=128)), "at_wl", 32)

        @block.tensor
        def _(t):
            t.wait_ge(P.sem("at_wl"), 48)
            n = 64
            for qcc in range(QCC):
                for fc in range(FC):
                    if n >= 2:
                        P.wait(t, "at_we", n - 1)
                    P.inc(t.transpose(TPB[n % 2][:],
                                      WQDN[:, qcc, fc * 128:(fc + 1) * 128],
                                      IDB[:]), "at_wt", 1)
                    n += 1
            P.wait(t, "at_bl", 16)
            P.inc(t.transpose(TPFb[:, 0:QCC], BQDN[:], IDF[0:QCC, 0:QCC]), "at_bt", 1)

        @block.vector
        def _(v):
            n = 64
            for qcc in range(QCC):
                for fc in range(FC):
                    v.wait_ge(P.sem("at_wt"), n + 1)
                    P.inc(v.tensor_copy(WQDT[:, fc, qcc * 128:(qcc + 1) * 128],
                                        TPB[n % 2][:]), "at_we", 1)
                    n += 1
            P.wait(v, "at_bt", 1)
            P.inc(v.tensor_copy(QDB[:], TPFb[:, 0:QCC]), "at_be", 1)

        @block.tensor
        def _(t):
            P.wait(t, "at_we", 192)
            P.wait(t, "at_be", 1)
            n = 8
            for qcc in range(QCC):
                for tt in range(2):
                    if n >= 2:
                        P.wait(t, "kv_ev", n - 1)
                    li = None
                    for fc in range(FC):
                        li = t.matmul(PKV[n % 2][:],
                                      WQDT[:, fc, qcc * 128:(qcc + 1) * 128],
                                      XLN2[:, fc, tt * 512:(tt + 1) * 512],
                                      start=(fc == 0), stop=(fc == FC - 1))
                    P.inc(li, "kv_mm", 1)
                    n += 1

        @block.scalar
        def _(sc):
            n = 8
            for qcc in range(QCC):
                for tt in range(2):
                    P.wait(sc, "kv_mm", n + 1)
                    P.inc(sc.activation(QCL[:, qcc, tt * 512:(tt + 1) * 512],
                                        PKV[n % 2][:], AF.Identity,
                                        bias=QDB[:, qcc:qcc + 1]), "kv_ev", 1)
                    n += 1

        gb2.close()

        b_done = ("kv_ev", P.val("kv_ev"))
        gb.close()
        ga.close()
        if sub == "bphase":
            gat.close()
            return b_done

        gco = ExitStack()
        osb = lambda n_, sh, dt: gco.enter_context(nc.sbuf_tensor(n_, sh, dt))
        ops_ = lambda n_, sh, dt: gco.enter_context(nc.psum_tensor(n_, sh, dt))
        CVREP = osb("CVREP", [128, D], F32)
        gcr = ExitStack()
        rsb = lambda n_, sh, dt: gcr.enter_context(nc.sbuf_tensor(n_, sh, dt))
        CROW = rsb("CROW", [1, 1, D], F32)
        CRB = rsb("CRB", [1, 3, D], BF16)
        SMLF = rsb("SMLF", [16, 128], F32)
        SMLB = rsb("SMLB", [CC, 3, 128], BF16)
        BKVT = rsb("BKVT", [128, 3, CC], BF16)
        BVU = rsb("BVU", [1, D], F32)
        BRB = rsb("BRB", [128, 8], BF16)
        gcp = ExitStack()
        cps_ = lambda n_, sh, dt: gcp.enter_context(nc.psum_tensor(n_, sh, dt))
        CPT = [cps_(f"CPT{i}", [128, 128], BF16) for i in range(2)]
        CPQ = cps_("CPQ", [128, D], F32)
        TPX = cps_("TPX", [128, 128], F32)

        P.ser_begin()
        P.ser("sync", lambda e: e.dma_start(out=SMLF[0:CC, :], in_=b_kvd_c), 16)
        P.ser("sync", lambda e: e.dma_start(out=BVU[:], in_=b_vu), 16)
        P.ser("vector", lambda e: e.tensor_copy(SMLB[:, 0, :], SMLF[0:CC, :]))
        P.ser("vector", lambda e: e.tensor_tensor(SMLF[0:CC, :], SMLF[0:CC, :],
                                                  SMLB[:, 0, :], OP.subtract))
        P.ser("vector", lambda e: e.tensor_copy(SMLB[:, 1, :], SMLF[0:CC, :]))
        P.ser("vector", lambda e: e.tensor_tensor(SMLF[0:CC, :], SMLF[0:CC, :],
                                                  SMLB[:, 1, :], OP.subtract))
        P.ser("vector", lambda e: e.tensor_copy(SMLB[:, 2, :], SMLF[0:CC, :]))
        for sp in range(3):
            P.ser("tensor", lambda e, sp=sp: e.transpose(
                CPT[0][:, 0:CC], SMLB[:, sp, :], IDB[0:CC, 0:CC]))
            P.ser("vector", lambda e, sp=sp: e.tensor_copy(
                BKVT[:, sp, :], CPT[0][:, 0:CC]))
        for (bsrc, bdst) in ((b_qu, QUB), (b_ku, CKC), (b_out, COUT)):
            P.ser("sync", lambda e, bsrc=bsrc: e.dma_start(out=SMLF[:], in_=bsrc), 16)
            P.ser("tensor", lambda e: e.transpose(TPX[:, 0:FC], SMLF[:], IDF[0:FC, 0:FC]))
            P.ser("vector", lambda e, bdst=bdst: e.tensor_copy(bdst[:], TPX[:, 0:FC]))
        for (bsrc, bdst) in ((b_qr, BQR), (b_kr, CKR)):
            P.ser("sync", lambda e, bsrc=bsrc: e.dma_start(out=SMLF[0:8, :], in_=bsrc), 16)
            P.ser("tensor", lambda e: e.transpose(TPX[:, 0:8], SMLF[0:8, :], IDF[0:8, 0:8]))
            P.ser("vector", lambda e: e.tensor_copy(BRB[:], TPX[:, 0:8]))
            for i in range(8):
                P.ser("tensor", lambda e, i=i: e.matmul(
                    TPX[:, i:i + 1], RTS[:, i, :], BRB[:, i:i + 1],
                    start=True, stop=True))
            P.ser("vector", lambda e, bdst=bdst: e.tensor_copy(bdst[:], TPX[:, 0:8]))
        cst_chain1 = P.ser_emit(block, "cst", pred=b_done)[1]

        gci = ExitStack()
        WNAT = gci.enter_context(nc.sbuf_tensor("WNAT", [128, FC, QC], BF16))
        WTS = gci.enter_context(nc.sbuf_tensor("WTS", [128, QCC, D], BF16))
        WEIGHTS = [
            ("ku", w_ku, FC, CC, CC * 128, wt_k, False, "ckc"),
            ("vu", w_vu, FC, CC, CC * 128, wt_v, False, "cv"),
            ("kr", w_kr, 8, CC, CC * 128, wt_kr, True, "ckr"),
            ("qu", w_qu, FC, QCC, QCC * 128, wt_q, False, None),
            ("qr", w_qr, 8, QCC, QCC * 128, wt_qr, True, None),
        ]
        prev = [None]
        for (nm, wsrc, nrc, ncc, wid, wdst, rot, cst) in WEIGHTS:
            pv = prev[0]
            bL = P.val("c_l"); bR_ = P.val("c_r"); bRC = P.val("c_rc")
            bT = P.val("c_t"); bE = P.val("c_e"); bSt = P.val("c_s")

            @block.gpsimd
            def _(gp, nm=nm, wsrc=wsrc, nrc=nrc, wid=wid, pv=pv):
                if pv is None:
                    P.wait(gp, b_done[0], b_done[1])
                    gp.wait_ge(P.sem("ag_w"), 16)
                else:
                    P.wait(gp, "c_t", pv[1])
                P.inc(gp.dma_start(out=WNAT[:, 0:nrc, 0:wid],
                                   in_=wsrc.rearrange("(rc p) c -> p rc c", p=128)),
                      "c_l", 16)

            if rot:
                @block.tensor
                def _(t, nm=nm, nrc=nrc, wid=wid, pv=pv):
                    t.wait_ge(P.sem("c_l"), bL + 16)
                    t.wait_ge(P.sem("cst"), pv[2] if pv is not None else cst_chain1)
                    for i in range(nrc):
                        if i >= 1:
                            P.wait(t, "c_rc", bRC + i)
                        li = None
                        for hq in range(wid // 512):
                            li = t.matmul(CPQ[:, hq * 512:(hq + 1) * 512],
                                          RTS[:, i, :],
                                          WNAT[:, i, hq * 512:(hq + 1) * 512],
                                          start=True, stop=True)
                        P.inc(li, "c_r", 1)

                @block.vector
                def _(v, nm=nm, nrc=nrc, wid=wid):
                    for i in range(nrc):
                        v.wait_ge(P.sem("c_r"), bR_ + i + 1)
                        P.inc(v.tensor_copy(WNAT[:, i, 0:wid], CPQ[:, 0:wid]),
                              "c_rc", 1)

            @block.tensor
            def _(t, nm=nm, nrc=nrc, ncc=ncc, rot=rot, pv=pv):
                if rot:
                    P.wait(t, "c_rc", bRC + nrc)
                else:
                    t.wait_ge(P.sem("c_l"), bL + 16)
                    t.wait_ge(P.sem("cst"),
                              pv[2] if pv is not None else cst_chain1)
                if pv is not None:
                    t.wait_ge(P.sem("c_s"), bSt)
                n = 0
                for rc in range(nrc):
                    for cc2 in range(ncc):
                        if n >= 2:
                            P.wait(t, "c_e", bE + n - 1)
                        P.inc(t.transpose(CPT[n % 2][:],
                                          WNAT[:, rc, cc2 * 128:(cc2 + 1) * 128],
                                          IDB[:]), "c_t", 1)
                        n += 1

            @block.vector
            def _(v, nm=nm, nrc=nrc, ncc=ncc):
                n = 0
                for rc in range(nrc):
                    for cc2 in range(ncc):
                        v.wait_ge(P.sem("c_t"), bT + n + 1)
                        P.inc(v.tensor_copy(WTS[:, cc2, rc * 128:(rc + 1) * 128],
                                            CPT[n % 2][:]), "c_e", 1)
                        n += 1

            @block.sync
            def _(s, nm=nm, nrc=nrc, ncc=ncc, wdst=wdst):
                P.wait(s, "c_e", P.val("c_e"))
                P.inc(s.dma_start(
                    out=wdst.rearrange("(cb p) d -> p cb d", p=128),
                    in_=WTS[:, 0:ncc, 0:nrc * 128]), "c_s", 16)

            if cst is not None:
                P.ser_begin()
                nq = (nrc * 128) // 512
                for cc2 in range(ncc):
                    for sp in range(3):
                        for hh in range(nq):
                            def mmf(e, cc2=cc2, sp=sp, hh=hh,
                                    fi=(cc2 == 0 and sp == 0),
                                    last=(cc2 == ncc - 1 and sp == 2)):
                                return e.matmul(
                                    CPQ[0:1, hh * 512:(hh + 1) * 512],
                                    BKVT[:, sp, cc2:cc2 + 1],
                                    WTS[:, cc2, hh * 512:(hh + 1) * 512],
                                    start=fi, stop=last)
                            P.ser("tensor", mmf)
                P.ser("vector", lambda e, nrc=nrc: e.tensor_copy(
                    CROW[:, 0, 0:nrc * 128], CPQ[0:1, 0:nrc * 128]))
                if cst == "cv":
                    P.ser("vector", lambda e: e.tensor_tensor(
                        CROW[:, 0, :], CROW[:, 0, :], BVU[:], OP.add))
                    P.ser("vector", lambda e: e.tensor_copy(CRB[:, 0, :], CROW[:, 0, :]))
                    P.ser("vector", lambda e: e.tensor_tensor(
                        CVREP[0:1, :], CROW[:, 0, :], CRB[:, 0, :], OP.subtract))
                    P.ser("vector", lambda e: e.tensor_copy(CRB[:, 1, :], CVREP[0:1, :]))
                    P.ser("vector", lambda e: e.tensor_tensor(
                        CVREP[0:1, :], CVREP[0:1, :], CRB[:, 1, :], OP.subtract))
                    P.ser("vector", lambda e: e.tensor_copy(CRB[:, 2, :], CVREP[0:1, :]))
                    for hh in range(4):
                        for sp in range(3):
                            P.ser("tensor", lambda e, sp=sp, hh=hh: e.matmul(
                                CPQ[:, hh * 512:(hh + 1) * 512], ONESM[:],
                                CRB[:, sp, hh * 512:(hh + 1) * 512],
                                start=(sp == 0), stop=(sp == 2)))
                    P.ser("vector", lambda e: e.tensor_copy(CVREP[:], CPQ[:]))
                else:
                    dstt = CKC if cst == "ckc" else CKR
                    nch = (nrc * 128) // 128
                    for i in range(nch):
                        P.ser("tensor", lambda e, i=i: e.transpose(
                            TPX[:, i:i + 1],
                            CROW[:, 0, i * 128:(i + 1) * 128], IDF[0:1, 0:1]))
                    P.ser("vector", lambda e, dstt=dstt, nch=nch: e.tensor_tensor(
                        dstt[:, 0:nch], dstt[:, 0:nch], TPX[:, 0:nch], OP.add))
                P.ser_emit(block, "cst", pred=("c_e", P.val("c_e")))
            prev[0] = (nm, P.val("c_t"), P.val("cst"))

        cs_total = P.val("c_s")
        wout_pred_ce = ("c_s", cs_total)
        cst_after_weights = P.val("cst")
        gci.close()

        gcr.close()
        gwo = ExitStack()
        WOTS = gwo.enter_context(nc.sbuf_tensor("WOTS", [128, FC, D], BF16))
        WONF = [gwo.enter_context(nc.sbuf_tensor(f"WONF{i}", [128, D], F32))
                for i in range(2)]
        WONB = gwo.enter_context(nc.sbuf_tensor("WONB", [128, D], BF16))
        SCRC = gwo.enter_context(nc.sbuf_tensor("SCRC", [128, D], BF16))
        CSCR = gwo.enter_context(nc.sbuf_tensor("CSCR", [128, FC], F32))

        @block.sync
        def _(s):
            P.wait(s, wout_pred_ce[0], wout_pred_ce[1])
            P.wait(s, "cst", cst_after_weights)
            for ec in range(FC):
                if ec >= 2:
                    P.wait(s, "wo_n", ec - 1)
                P.inc(s.dma_start(out=WONF[ec % 2][:],
                                  in_=w_out[ec * 128:(ec + 1) * 128, :]),
                      "wo_l" + str(ec % 2), 16)

        @block.vector
        def _(v):
            for ec in range(FC):
                v.wait_ge(P.sem("wo_l" + str(ec % 2)), 16 * (ec // 2 + 1))
                P.inc(v.scalar_tensor_tensor(SCRC[:], WONF[ec % 2][:], 1.0,
                                             CVREP[:], OP.mult, OP.mult,
                                             accum_out=CSCR[:, ec:ec + 1]),
                      "wo_c", 1)
                if ec >= 1:
                    P.wait(v, "wo_t", ec * FC)
                P.inc(v.tensor_copy(WONB[:], WONF[ec % 2][:]), "wo_n", 1)
                for rc in range(FC):
                    nn = ec * FC + rc
                    v.wait_ge(P.sem("wo_t"), nn + 1)
                    P.inc(v.tensor_copy(WOTS[:, rc, ec * 128:(ec + 1) * 128],
                                        CPT[nn % 2][:]), "wo_e", 1)
            P.wait(v, "wo_c", FC)
            P.inc(v.tensor_tensor(COUT[:], COUT[:], CSCR[:], OP.add), "wo_f", 1)

        @block.tensor
        def _(t):
            for ec in range(FC):
                P.wait(t, "wo_n", ec + 1)
                for rc in range(FC):
                    nn = ec * FC + rc
                    if nn >= 2:
                        P.wait(t, "wo_e", nn - 1)
                    P.inc(t.transpose(CPT[nn % 2][:],
                                      WONB[:, rc * 128:(rc + 1) * 128], IDB[:]),
                          "wo_t", 1)

        @block.sync
        def _(s):
            P.wait(s, "wo_e", FC * FC)
            P.inc(s.dma_start(out=wt_o.rearrange("(cb p) d -> p cb d", p=128),
                              in_=WOTS[:]), "wo_s", 16)

        if sub == "ctx":
            tap_cv2 = nc.dram_tensor("tap_cv2", [128, D], F32,
                                     kind="ExternalOutput").ap()
            tap_cs2 = nc.dram_tensor("tap_cs2", [128, FC], F32,
                                     kind="ExternalOutput").ap()
            tap_wts = nc.dram_tensor("tap_wts", [KV, D], BF16,
                                     kind="ExternalOutput").ap()

            @block.gpsimd
            def _(gp):
                P.wait(gp, "wo_f", 1)
                P.wait(gp, "wo_s", 16)
                P.inc(gp.dma_start(out=tap_cv2, in_=CVREP[:]), "dbg5", 16)
                P.inc(gp.dma_start(out=tap_cs2, in_=CSCR[:]), "dbg5", 16)
                P.inc(gp.dma_start(out=tap_wts, in_=wt_v.ap()), "dbg5", 16)
                P.wait(gp, "dbg5")
        gwo.close()
        cst_total = P.val("cst")
        gcp.close()
        gco.close()
        if sub == "prep":
            gat.close()
            return ("cst", cst_total)

        gd = ExitStack()
        dsb = lambda n_, sh, dt: gd.enter_context(nc.sbuf_tensor(n_, sh, dt))
        dps = lambda n_, sh, dt: gd.enter_context(nc.psum_tensor(n_, sh, dt))
        KV0SB = dsb("KV0SB", [128, CC, SK], BF16)
        WKH = [dsb(f"WKH{i}", [128, CC, 128], BF16) for i in range(2)]
        WVH = [dsb(f"WVH{i}", [128, CC, 128], BF16) for i in range(2)]
        WKRH = [dsb(f"WKRH{i}", [128, CC, RD], BF16) for i in range(2)]
        WQH = [dsb(f"WQH{i}", [128, QCC, 128], BF16) for i in range(2)]
        WQRH = [dsb(f"WQRH{i}", [128, QCC, RD], BF16) for i in range(2)]
        KSTG = [dsb(f"KSTG{i}", [128, 512], BF16) for i in range(2)]
        KRSTG = [dsb(f"KRSTG{i}", [RD, 512], BF16) for i in range(2)]
        VSTG = [dsb(f"VSTG{i}", [128, 512], BF16) for i in range(2)]
        QSTG = dsb("QSTG", [128, T], BF16)
        QRSTG = dsb("QRSTG", [RD, T], BF16)
        MXC = dsb("MXC", [128, 1], F32)
        MXS = dsb("MXS", [1, 2], F32)
        MXB = dsb("MXB", [1, 1], BF16)
        PKC = [dps(f"PKC{i}", [128, 512], F32) for i in range(2)]
        PKR = [dps(f"PKR{i}", [RD, 512], F32) for i in range(2)]
        PV = [dps(f"PV{i}", [128, 512], F32) for i in range(2)]
        PSS = dps("PSS", [128, 512], F32)
        TPD = dps("TPD", [128, 128], F32)

        @block.gpsimd
        def _(gp):
            gp.wait_ge(P.sem("ag_c"), 1)
            P.wait(gp, "wo_s", 16)
            P.wait(gp, "wo_f", 1)
            for cc in range(CC):
                for r in range(NCORES):
                    P.inc(gp.dma_start(
                        out=KV0SB[:, cc, r * T:(r + 1) * T],
                        in_=ag_out.ap()[r * KV + cc * 128:r * KV + (cc + 1) * 128, :]),
                        "kv0_l", 16)

        NT2 = 16

        for h in range(NH):
            hb = (h % 2) * RD
            hc2 = h // 2

            @block.sync
            def _(s, h=h):
                if h >= 2:
                    P.wait(s, "g_hd", h - 1)
                if h == 0:
                    P.wait(s, "c_s", cs_total)
                    P.wait(s, "wo_s", 16)
                    P.wait(s, "wo_f", 1)
                for (buf, wsrc, wid2) in ((WKH, wt_k, 128), (WVH, wt_v, 128),
                                          (WKRH, wt_kr, RD), (WQH, wt_q, 128),
                                          (WQRH, wt_qr, RD)):
                    P.inc(s.dma_start(
                        out=buf[h % 2][:],
                        in_=wsrc[:, h * wid2:(h + 1) * wid2].rearrange(
                            "(cc p) d -> p cc d", p=128)),
                        "dw" + str(h % 2), 16)

            @block.tensor
            def _(t, h=h):
                t.wait_ge(P.sem("dw" + str(h % 2)), 80 * (h // 2 + 1))
                if h == 0:
                    P.wait(t, "kv0_l", 16 * CC * NCORES)
                    P.wait(t, "wo_s", 16)
                    P.wait(t, "wo_f", 1)
                for ti in range(2):
                    if h >= 1 or ti >= 1:
                        pass
                    if h >= 1:
                        t.wait_ge(P.sem("g_ke"), NT2 * (h - 1) + 15 + ti)
                    li = None
                    for qcc in range(QCC):
                        li = t.matmul(PKC[ti][:], WQH[h % 2][:, qcc, :],
                                      QCL[:, qcc, ti * 512:(ti + 1) * 512],
                                      start=(qcc == 0), stop=(qcc == QCC - 1))
                    P.inc(li, "g_qm", 1)
                for ti in range(2):
                    if h >= 1:
                        t.wait_ge(P.sem("g_re"), NT2 * (h - 1) + 15 + ti)
                    li = None
                    for qcc in range(QCC):
                        li = t.matmul(PKR[ti][:], WQRH[h % 2][:, qcc, :],
                                      QCL[:, qcc, ti * 512:(ti + 1) * 512],
                                      start=(qcc == 0), stop=(qcc == QCC - 1))
                    P.inc(li, "g_qm", 1)

            @block.scalar
            def _(sc, h=h, hb=hb, hc2=hc2):
                if h == 0:
                    P.wait(sc, "cst", cst_total)
                    P.wait(sc, "wo_s", 16)
                    P.wait(sc, "wo_f", 1)
                sc.wait_ge(P.sem("sq"), 16 * h)
                sc.wait_ge(P.sem("bh"), 9 * h)
                for ti in range(2):
                    sc.wait_ge(P.sem("g_qm"), 4 * h + ti + 1)
                    P.inc(sc.activation(QSTG[:, ti * 512:(ti + 1) * 512],
                                        PKC[ti][:], AF.Identity,
                                        bias=QUB[:, h:h + 1]), "g_qe", 1)
                sc.wait_ge(P.sem("sqr"), 16 * h)
                for ti in range(2):
                    sc.wait_ge(P.sem("g_qm"), 4 * h + ti + 3)
                    P.inc(sc.activation(QRSTG[0:RD, ti * 512:(ti + 1) * 512],
                                        PKR[ti][0:RD, :], AF.Identity,
                                        bias=BQR[hb:hb + RD, hc2:hc2 + 1]),
                          "g_qe", 1)

            @block.sync
            def _(s, h=h):
                s.wait_ge(P.sem("g_qe"), 4 * h + 2)
                P.inc(s.dma_start(out=qcs_scr.ap()[h], in_=QSTG[:]), "sq", 16)
                s.wait_ge(P.sem("g_qe"), 4 * h + 4)
                P.inc(s.dma_start(out=qrs_scr.ap()[h], in_=QRSTG[:]), "sqr", 16)

            @block.tensor
            def _(t, h=h):
                t.wait_ge(P.sem("g_qe"), 4 * h + 1)
                li = None
                for cc in range(CC):
                    li = t.matmul(PKC[0][:], WKH[h % 2][:, cc, :],
                                  KV0SB[:, cc, 0:512], start=(cc == 0),
                                  stop=(cc == CC - 1))
                P.inc(li, "g_km", 1)
                t.wait_ge(P.sem("g_qe"), 4 * h + 3)
                li = None
                for cc in range(CC):
                    li = t.matmul(PKR[0][:], WKRH[h % 2][:, cc, :],
                                  KV0SB[:, cc, 0:512], start=(cc == 0),
                                  stop=(cc == CC - 1))
                P.inc(li, "g_rm", 1)

            @block.scalar
            def _(sc, h=h, hb=hb, hc2=hc2):
                sc.wait_ge(P.sem("g_km"), NT2 * h + 1)
                sc.wait_ge(P.sem("skc0"), 16 * 8 * h)
                P.inc(sc.activation(KSTG[0][:], PKC[0][:], AF.Identity,
                                    bias=CKC[:, h:h + 1]), "g_ke", 1)
                sc.wait_ge(P.sem("g_rm"), NT2 * h + 1)
                sc.wait_ge(P.sem("skr0"), 16 * 8 * h)
                P.inc(sc.activation(KRSTG[0][0:RD, :], PKR[0][0:RD, :],
                                    AF.Identity,
                                    bias=CKR[hb:hb + RD, hc2:hc2 + 1]),
                      "g_re", 1)

            P.ser_begin()
            P.ser("tensor", lambda e: e.matmul(PSS[:], KSTG[0][:, 0:128],
                                               QSTG[:, 0:512], start=True,
                                               stop=False))
            P.ser("tensor", lambda e: e.matmul(PSS[:], KRSTG[0][:, 0:128],
                                               QRSTG[:, 0:512],
                                               start=False, stop=True))
            P.ser("vector", lambda e: e.reduce_max(MXC[:], PSS[:],
                                                   mybir.AxisListType.X))
            P.ser("tensor", lambda e: e.transpose(TPD[0:1, :], MXC[:], IDF[:]))
            P.ser("vector", lambda e: e.reduce_max(MXS[:, 0:1], TPD[0:1, :],
                                                   mybir.AxisListType.X))
            P.ser("vector", lambda e: e.tensor_scalar_mul(MXS[:, 1:2],
                                                          MXS[:, 0:1], -SCALE))
            P.ser("vector", lambda e: e.tensor_copy(MXB[:], MXS[:, 1:2]))
            P.ser("tensor", lambda e: e.matmul(TPD[:, 0:1], ONESM[:], MXB[:],
                                               start=True, stop=True))
            P.ser("vector", lambda e, h=h: e.tensor_copy(BHS[:, h:h + 1],
                                                         TPD[:, 0:1]))
            P.ser_emit(block, "bh", pred=("g_re", NT2 * h + 1))

            @block.tensor
            def _(t, h=h):
                def vmm(kt):
                    t.wait_ge(P.sem("g_ve"),
                              NT2 * h + kt - 1 if kt >= 2 else 16 * h)
                    li = None
                    for ci in range(4):
                        for cc in range(CC):
                            li = t.matmul(
                                PV[kt % 2][:, ci * 128:(ci + 1) * 128],
                                KV0SB[:, cc, (kt * 4 + ci) * 128:(kt * 4 + ci + 1) * 128],
                                WVH[h % 2][:, cc, :],
                                start=(cc == 0), stop=(cc == CC - 1))
                    P.inc(li, "g_vm", 1)

                vmm(0)
                for kt in range(1, NT2):
                    if kt == 1:
                        t.wait_ge(P.sem("g_qe"), 4 * h + 2)
                    else:
                        t.wait_ge(P.sem("g_ke"), NT2 * h + kt - 1)
                    li = None
                    for cc in range(CC):
                        li = t.matmul(PKC[kt % 2][:], WKH[h % 2][:, cc, :],
                                      KV0SB[:, cc, kt * 512:(kt + 1) * 512],
                                      start=(cc == 0), stop=(cc == CC - 1))
                    P.inc(li, "g_km", 1)
                    if kt == 1:
                        t.wait_ge(P.sem("g_qe"), 4 * h + 4)
                    else:
                        t.wait_ge(P.sem("g_re"), NT2 * h + kt - 1)
                    li = None
                    for cc in range(CC):
                        li = t.matmul(PKR[kt % 2][:], WKRH[h % 2][:, cc, :],
                                      KV0SB[:, cc, kt * 512:(kt + 1) * 512],
                                      start=(cc == 0), stop=(cc == CC - 1))
                    P.inc(li, "g_rm", 1)
                    vmm(kt)
                P.inc(t.nop(), "g_hd", 1)

            @block.scalar
            def _(sc, h=h, hb=hb, hc2=hc2):
                def vev(kt):
                    sc.wait_ge(P.sem("g_vm"), NT2 * h + kt + 1)
                    sc.wait_ge(P.sem("sv" + str(kt % 2)),
                               16 * (8 * h + kt // 2))
                    P.inc(sc.activation(VSTG[kt % 2][:], PV[kt % 2][:],
                                        AF.Copy), "g_ve", 1)

                vev(0)
                for kt in range(1, NT2):
                    sc.wait_ge(P.sem("g_km"), NT2 * h + kt + 1)
                    sc.wait_ge(P.sem("skc" + str(kt % 2)),
                               16 * (8 * h + kt // 2))
                    if kt == 2:
                        sc.wait_ge(P.sem("bh"), 9 * (h + 1))
                    P.inc(sc.activation(KSTG[kt % 2][:], PKC[kt % 2][:],
                                        AF.Identity, bias=CKC[:, h:h + 1]),
                          "g_ke", 1)
                    sc.wait_ge(P.sem("g_rm"), NT2 * h + kt + 1)
                    sc.wait_ge(P.sem("skr" + str(kt % 2)),
                               16 * (8 * h + kt // 2))
                    if kt == 2:
                        sc.wait_ge(P.sem("bh"), 9 * (h + 1))
                    P.inc(sc.activation(KRSTG[kt % 2][0:RD, :],
                                        PKR[kt % 2][0:RD, :], AF.Identity,
                                        bias=CKR[hb:hb + RD, hc2:hc2 + 1]),
                          "g_re", 1)
                    vev(kt)

            @block.sync
            def _(s, h=h):
                def vst(kt):
                    s.wait_ge(P.sem("g_ve"), NT2 * h + kt + 1)
                    P.inc(s.dma_start(
                        out=v_scr.ap()[h, kt * 4:(kt + 1) * 4].rearrange(
                            "kc p d -> p kc d"),
                        in_=VSTG[kt % 2][:].rearrange("p (kc d) -> p kc d", kc=4)),
                        "sv" + str(kt % 2), 16)

                vst(0)
                s.wait_ge(P.sem("g_ke"), NT2 * h + 1)
                P.inc(s.dma_start(
                    out=kc_scr.ap()[h, :, 0:512],
                    in_=KSTG[0][:]), "skc0", 16)
                s.wait_ge(P.sem("g_re"), NT2 * h + 1)
                P.inc(s.dma_start(
                    out=kr_scr.ap()[h, :, 0:512],
                    in_=KRSTG[0][:]), "skr0", 16)
                for kt in range(1, NT2):
                    s.wait_ge(P.sem("g_ke"), NT2 * h + kt + 1)
                    P.inc(s.dma_start(
                        out=kc_scr.ap()[h, :, kt * 512:(kt + 1) * 512],
                        in_=KSTG[kt % 2][:]), "skc" + str(kt % 2), 16)
                    s.wait_ge(P.sem("g_re"), NT2 * h + kt + 1)
                    P.inc(s.dma_start(
                        out=kr_scr.ap()[h, :, kt * 512:(kt + 1) * 512],
                        in_=KRSTG[kt % 2][:]), "skr" + str(kt % 2), 16)
                    vst(kt)

        gen_waits = [(nm, P.val(nm)) for nm in
                     ("skc0", "skc1", "skr0", "skr1", "sv0", "sv1", "sq", "sqr")]
        bh_total = P.val("bh")
        gd.close()
        gat.close()
        if sub == "gen":
            tap_kc = nc.dram_tensor("tap_kc", [128, SK], BF16, kind="ExternalOutput").ap()
            tap_kr = nc.dram_tensor("tap_kr", [RD, SK], BF16, kind="ExternalOutput").ap()
            tap_v = nc.dram_tensor("tap_v", [KC, 128, HD], BF16, kind="ExternalOutput").ap()
            tap_qc = nc.dram_tensor("tap_qc", [128, T], BF16, kind="ExternalOutput").ap()
            tap_qr = nc.dram_tensor("tap_qr", [RD, T], BF16, kind="ExternalOutput").ap()
            tap_ag = nc.dram_tensor("tap_ag", [NCORES * KV, T], BF16, kind="ExternalOutput").ap()
            tap_bh = nc.dram_tensor("tap_bh", [128, NH], F32, kind="ExternalOutput").ap()
            tap_ck = nc.dram_tensor("tap_ck", [128, FC], F32, kind="ExternalOutput").ap()
            tap_co = nc.dram_tensor("tap_co", [128, FC], F32, kind="ExternalOutput").ap()
            HDBG = 3

            @block.sync
            def _(s):
                for nm_, vv_ in [("skc0", None), ("skc1", None), ("skr0", None),
                                 ("skr1", None), ("sv0", None), ("sv1", None),
                                 ("sq", None), ("sqr", None)]:
                    P.wait(s, nm_)
                P.wait(s, "bh", P.val("bh"))
                P.inc(s.dma_start(out=tap_kc, in_=kc_scr.ap()[HDBG]), "dbg", 16)
                P.inc(s.dma_start(out=tap_kr, in_=kr_scr.ap()[HDBG]), "dbg", 16)
                P.inc(s.dma_start(out=tap_v, in_=v_scr.ap()[HDBG]), "dbg", 16)
                P.inc(s.dma_start(out=tap_qc, in_=qcs_scr.ap()[HDBG]), "dbg", 16)
                P.inc(s.dma_start(out=tap_qr, in_=qrs_scr.ap()[HDBG]), "dbg", 16)
                P.inc(s.dma_start(out=tap_ag, in_=ag_out.ap()), "dbg", 16)
                P.wait(s, "dbg")

            @block.gpsimd
            def _(gp):
                P.wait(gp, "bh", P.val("bh"))
                P.inc(gp.dma_start(out=tap_bh, in_=BHS[:]), "dbg2", 16)
                P.inc(gp.dma_start(out=tap_ck, in_=CKC[:]), "dbg2", 16)
                P.inc(gp.dma_start(out=tap_co, in_=COUT[:]), "dbg2", 16)
                P.wait(gp, "dbg2")
            return ("g_hd", P.val("g_hd"))

        ge = ExitStack()
        esb = lambda n_, sh, dt: ge.enter_context(nc.sbuf_tensor(n_, sh, dt))
        eps_ = lambda n_, sh, dt: ge.enter_context(nc.psum_tensor(n_, sh, dt))
        CTX = esb("CTX", [128, NH, T], BF16)
        KCB = [esb(f"KCB{i}", [128, SK // 2], BF16) for i in range(2)]
        KRB = [esb(f"KRB{i}", [RD, SK // 2], BF16) for i in range(2)]
        VB = [esb(f"VB{i}", [128, 32, 128], BF16) for i in range(2)]
        QCB = [esb(f"QCB{i}", [128, T], BF16) for i in range(2)]
        QRB = [esb(f"QRB{i}", [RD, T], BF16) for i in range(2)]
        EG = [esb(f"EG{i}", [128, 1024], BF16) for i in range(2)]
        RZROW = esb("RZROW", [1, 512], F32)
        RZB = esb("RZB", [1, 512], BF16)
        RZS = esb("RZS", [128, 512], BF16)
        gep = ExitStack()
        eps2 = lambda n_, sh, dt: gep.enter_context(nc.psum_tensor(n_, sh, dt))
        SG = [eps2(f"SG{i}", [128, 1024], F32) for i in range(2)]
        CP = [eps2(f"CP{i}", [128, 512], F32) for i in range(2)]
        ZP = [eps2(f"ZP{i}", [1, 512], F32) for i in range(2)]

        @block.sync
        def _(s):
            for nm, vv in gen_waits:
                s.wait_ge(P.sem(nm), vv)
            for m in range(2 * NH):
                h, half = divmod(m, 2)
                if m >= 2:
                    P.wait(s, "e_cd", m - 1)
                P.inc(s.dma_start(out=KCB[m % 2][:],
                                  in_=kc_scr.ap()[h, :, half * 4096:(half + 1) * 4096]),
                      "e_ld" + str(m % 2), 16)
                P.inc(s.dma_start(out=KRB[m % 2][:],
                                  in_=kr_scr.ap()[h, :, half * 4096:(half + 1) * 4096]),
                      "e_ld" + str(m % 2), 16)
                P.inc(s.dma_start(out=VB[m % 2][:],
                                  in_=v_scr.ap()[h, half * 32:(half + 1) * 32].rearrange(
                                      "kc p d -> p kc d")),
                      "e_ld" + str(m % 2), 16)
                if half == 0:
                    P.inc(s.dma_start(out=QCB[h % 2][:], in_=qcs_scr.ap()[h]),
                          "e_ld" + str(m % 2), 16)
                    P.inc(s.dma_start(out=QRB[h % 2][:], in_=qrs_scr.ap()[h]),
                          "e_ld" + str(m % 2), 16)

        for h in range(NH):
            @block.tensor
            def _(t, h=h):
                for half in range(2):
                    m = 2 * h + half
                    t.wait_ge(P.sem("e_ld" + str(m % 2)),
                              (80 if m % 2 == 0 else 48) * (h + 1))
                    for qt in range(2):
                        if half == 0:
                            t.wait_ge(P.sem("en"), 10 * h)
                        pend = []
                        for g in range(16):
                            egi = (m * 2 + qt) * 16 + g
                            if egi >= 2:
                                t.wait_ge(P.sem("e_ex"), egi - 1)
                            li = None
                            for i in range(2):
                                kc_ = 2 * g + i
                                t.matmul(SG[egi % 2][:, i * 512:(i + 1) * 512],
                                         KCB[m % 2][:, kc_ * 128:(kc_ + 1) * 128],
                                         QCB[h % 2][:, qt * 512:(qt + 1) * 512],
                                         start=True, stop=False)
                                li = t.matmul(SG[egi % 2][:, i * 512:(i + 1) * 512],
                                              KRB[m % 2][:, kc_ * 128:(kc_ + 1) * 128],
                                              QRB[h % 2][:, qt * 512:(qt + 1) * 512],
                                              start=False, stop=True)
                            P.inc(li, "e_sm", 1)
                            pend.append((egi, g))
                            if len(pend) >= 2:
                                pegi, pg = pend.pop(0)
                                t.wait_ge(P.sem("e_ex"), pegi + 1)
                                li2 = None
                                for i in range(2):
                                    kc_ = 2 * pg + i
                                    st = (half == 0 and pg == 0 and i == 0)
                                    sp = (half == 1 and pg == 15 and i == 1)
                                    li2 = t.matmul(CP[qt][:],
                                                   VB[m % 2][:, kc_, :],
                                                   EG[pegi % 2][:, i * 512:(i + 1) * 512],
                                                   start=st, stop=sp)
                                    li2 = t.matmul(ZP[qt][:], ONES[:],
                                                   EG[pegi % 2][:, i * 512:(i + 1) * 512],
                                                   start=st, stop=sp)
                                P.inc(li2, "e_cz", 1)
                        pegi, pg = pend.pop(0)
                        t.wait_ge(P.sem("e_ex"), pegi + 1)
                        li2 = None
                        for i in range(2):
                            kc_ = 2 * pg + i
                            st = (half == 0 and pg == 0 and i == 0)
                            sp = (half == 1 and pg == 15 and i == 1)
                            li2 = t.matmul(CP[qt][:], VB[m % 2][:, kc_, :],
                                           EG[pegi % 2][:, i * 512:(i + 1) * 512],
                                           start=st, stop=sp)
                            li2 = t.matmul(ZP[qt][:], ONES[:],
                                           EG[pegi % 2][:, i * 512:(i + 1) * 512],
                                           start=st, stop=sp)
                        P.inc(li2, "e_cz", 1)
                    P.inc(t.nop(), "e_cd", 1)

            @block.scalar
            def _(sc, h=h):
                sc.wait_ge(P.sem("bh"), 9 * (h + 1))
                for half in range(2):
                    m = 2 * h + half
                    for qt in range(2):
                        for g in range(16):
                            egi = (m * 2 + qt) * 16 + g
                            sc.wait_ge(P.sem("e_sm"), egi + 1)
                            if egi >= 2:
                                sc.wait_ge(P.sem("e_cz"), egi - 1)
                            P.inc(sc.activation(EG[egi % 2][:], SG[egi % 2][:],
                                                AF.Exp, bias=BHS[:, h:h + 1],
                                                scale=SCALE), "e_ex", 1)

            for qt in range(2):
                m1 = 2 * h + 1
                ecz_after = m1 * 32 + 32
                P.ser_begin()
                P.ser("vector", lambda e, qt=qt: e.reciprocal(RZROW[:], ZP[qt][:]))
                P.ser("vector", lambda e: e.tensor_copy(RZB[:], RZROW[:]))
                P.ser("tensor", lambda e: e.matmul(SG[0][:, 0:512], ONESM[:],
                                                   RZB[:], start=True, stop=True))
                P.ser("scalar", lambda e: e.activation(RZS[:], SG[0][:, 0:512],
                                                       AF.Copy))
                P.ser("vector", lambda e, h=h, qt=qt: e.tensor_tensor(
                    CTX[:, h, qt * 512:(qt + 1) * 512], CP[qt][:], RZS[:],
                    OP.mult))
                P.ser_emit(block, "en", pred=("e_cz", ecz_after))

                en_total = P.val("en")
        gep.close()

        WOE = [ge.enter_context(nc.sbuf_tensor(f"WOE{i}", [128, FC, 128], BF16))
               for i in range(2)]
        POF = [ge.enter_context(nc.psum_tensor(f"POF{i}", [128, 512], F32))
               for i in range(2)]

        @block.gpsimd
        def _(gp):
            P.wait(gp, "wo_s", 16)
            P.wait(gp, "en", en_total)
            for ec in range(FC):
                if ec >= 2:
                    P.wait(gp, "f_pm", 2 * (ec - 1))
                P.inc(gp.dma_start(
                    out=WOE[ec % 2][:],
                    in_=wt_o[:, ec * 128:(ec + 1) * 128].rearrange(
                        "(hc p) e -> p hc e", p=128)), "f_wl" + str(ec % 2), 16)

        @block.tensor
        def _(t):
            P.wait(t, "en", en_total)
            for ec in range(FC):
                t.wait_ge(P.sem("f_wl" + str(ec % 2)), 16 * (ec // 2 + 1))
                for tt in range(2):
                    n = ec * 2 + tt
                    if n >= 2:
                        t.wait_ge(P.sem("f_rs"), n - 1)
                    li = None
                    for hc in range(FC):
                        li = t.matmul(POF[n % 2][:], WOE[ec % 2][:, hc, :],
                                      CTX[:, hc, tt * 512:(tt + 1) * 512],
                                      start=(hc == 0), stop=(hc == FC - 1))
                    P.inc(li, "f_pm", 1)

        @block.vector
        def _(v):
            P.wait(v, "wo_f", 1)
            for ec in range(FC):
                for tt in range(2):
                    n = ec * 2 + tt
                    v.wait_ge(P.sem("f_pm"), n + 1)
                    P.inc(v.scalar_tensor_tensor(
                        XT[:, ec, tt * 512:(tt + 1) * 512], POF[n % 2][:],
                        COUT[:, ec:ec + 1],
                        XT[:, ec, tt * 512:(tt + 1) * 512], OP.add, OP.add),
                        "f_rs", 1)

        if sub == "ctx":
            tap_ctx = nc.dram_tensor("tap_ctx", [128, NH, T], BF16,
                                     kind="ExternalOutput").ap()
            HDBG = 0
            tap_kc = nc.dram_tensor("tap_kc", [128, SK], BF16, kind="ExternalOutput").ap()
            tap_kr = nc.dram_tensor("tap_kr", [RD, SK], BF16, kind="ExternalOutput").ap()
            tap_v = nc.dram_tensor("tap_v", [KC, 128, HD], BF16, kind="ExternalOutput").ap()
            tap_qc = nc.dram_tensor("tap_qc", [128, T], BF16, kind="ExternalOutput").ap()
            tap_qr = nc.dram_tensor("tap_qr", [RD, T], BF16, kind="ExternalOutput").ap()
            tap_bh = nc.dram_tensor("tap_bh", [128, NH], F32, kind="ExternalOutput").ap()
            tap_ag = nc.dram_tensor("tap_ag", [NCORES * KV, T], BF16, kind="ExternalOutput").ap()
            tap_agi = nc.dram_tensor("tap_agi", [KV, T], BF16, kind="ExternalOutput").ap()
            tap_co = nc.dram_tensor("tap_co", [128, FC], F32, kind="ExternalOutput").ap()
            tap_cv = nc.dram_tensor("tap_cv", [128, D], F32, kind="ExternalOutput").ap()

            @block.sync
            def _(s):
                P.wait(s, "en", P.val("en"))
                P.inc(s.dma_start(out=tap_ctx, in_=CTX[:]), "dbg3", 16)
                P.inc(s.dma_start(out=tap_kc, in_=kc_scr.ap()[HDBG]), "dbg3", 16)
                P.inc(s.dma_start(out=tap_kr, in_=kr_scr.ap()[HDBG]), "dbg3", 16)
                P.inc(s.dma_start(out=tap_v, in_=v_scr.ap()[HDBG]), "dbg3", 16)
                P.inc(s.dma_start(out=tap_qc, in_=qcs_scr.ap()[HDBG]), "dbg3", 16)
                P.inc(s.dma_start(out=tap_qr, in_=qrs_scr.ap()[HDBG]), "dbg3", 16)
                P.inc(s.dma_start(out=tap_ag, in_=ag_out.ap()), "dbg3", 16)
                P.inc(s.dma_start(out=tap_agi, in_=ag_in.ap()), "dbg3", 16)
                P.wait(s, "dbg3")

            @block.gpsimd
            def _(gp):
                P.wait(gp, "en", P.val("en"))
                P.inc(gp.dma_start(out=tap_bh, in_=BHS[:]), "dbg4", 16)
                P.inc(gp.dma_start(out=tap_co, in_=COUT[:]), "dbg4", 16)
                P.wait(gp, "dbg4")
            ge.close()
            return ("en", P.val("en"))
        done = ("f_rs", P.val("f_rs"))
        ge.close()
        return done

    ln1_done = emit_ln("l1", XLN8, p0_done)
    ffn1_done = emit_ffn("f1", ln1_done)

    if stop_after == "ffn1":
        final_pred = ffn1_done
    else:
        sub = stop_after if stop_after in ("ln2", "bphase", "prep", "gen", "ctx") else None
        attn_done = emit_attention(ffn1_done, sub=sub)
        if stop_after in ("attn", "ln2", "bphase", "prep", "gen", "ctx"):
            final_pred = attn_done
        else:
            ln3_done = emit_ln("l3", XLN8, attn_done)
            final_pred = emit_ffn("f3", ln3_done)

    fin = ExitStack()
    OSTG = [fin.enter_context(nc.sbuf_tensor(f"OSTG{i}", [128, D], F32))
            for i in range(2)]
    PTO = [fin.enter_context(nc.psum_tensor(f"PTO{i}", [128, 128], F32))
           for i in range(2)]

    @block.tensor
    def _(t):
        P.wait(t, final_pred[0], final_pred[1])
        for tc in range(8):
            for fc in range(FC):
                n = tc * FC + fc
                if n >= 2:
                    t.wait_ge(P.sem("o_ev"), n - 1)
                P.inc(t.transpose(PTO[n % 2][:],
                                  XT[:, fc, tc * 128:(tc + 1) * 128], IDF[:]),
                      "o_tp", 1)

    @block.vector
    def _(v):
        for tc in range(8):
            for fc in range(FC):
                n = tc * FC + fc
                v.wait_ge(P.sem("o_tp"), n + 1)
                if fc == 0 and tc >= 2:
                    P.wait(v, f"o_st{tc % 2}", 16 * (tc // 2))
                P.inc(v.tensor_copy(OSTG[tc % 2][:, fc * 128:(fc + 1) * 128],
                                    PTO[n % 2][:]), "o_ev", 1)

    @block.sync
    def _(s):
        for tc in range(8):
            s.wait_ge(P.sem("o_ev"), (tc + 1) * FC)
            P.inc(s.dma_start(out=out[tc * 128:(tc + 1) * 128, :],
                              in_=OSTG[tc % 2][:]), f"o_st{tc % 2}", 16)
        s.wait_ge(P.sem("o_st0"), P.val("o_st0"))
        s.wait_ge(P.sem("o_st1"), P.val("o_st1"))

    fin.close()
    blk_cm.__exit__(None, None, None)
    ctx.close()
    return nc, {}


def make_inmaps(inp):
    NHRD = NH * RD
    inv_freq = 1.0 / 10000.0 ** (np.arange(0, RD, 2, dtype=np.float64) / RD)
    Rfull = np.zeros((NHRD, NHRD), dtype=np.float32)
    for h in range(NH):
        ang = h * inv_freq
        c, s = np.cos(ang), np.sin(ang)
        for m in range(RD // 2):
            b0 = h * RD + 2 * m
            Rfull[b0, b0] = c[m]; Rfull[b0, b0 + 1] = -s[m]
            Rfull[b0 + 1, b0] = s[m]; Rfull[b0 + 1, b0 + 1] = c[m]
    RT_ = Rfull.T
    RfT = np.zeros((8, 128, 128), dtype=ml_dtypes.bfloat16)
    for i in range(8):
        RfT[i] = RT_[i * 128:(i + 1) * 128, i * 128:(i + 1) * 128].astype(
            ml_dtypes.bfloat16)

    com = dict(
        w_kvd=inp["L0_attn_kv_down_w"],
        b_kvd_c=inp["L0_attn_kv_down_b"].reshape(CC, 128),
        w_ku=inp["L0_attn_key_up_w"],
        b_ku=inp["L0_attn_key_up_b"].reshape(FC, 128),
        w_vu=inp["L0_attn_value_up_w"],
        b_vu=inp["L0_attn_value_up_b"].reshape(1, D),
        w_kr=inp["L0_attn_key_rope_w"],
        b_kr=inp["L0_attn_key_rope_b"].reshape(8, 128),
        w_qd=inp["L0_attn_query_down_w"],
        b_qd=inp["L0_attn_query_down_b"].reshape(QCC, 128),
        w_qu=inp["L0_attn_query_up_w"],
        b_qu=inp["L0_attn_query_up_b"].reshape(FC, 128),
        w_qr=inp["L0_attn_query_rope_w"],
        b_qr=inp["L0_attn_query_rope_b"].reshape(8, 128),
        w_out=inp["L0_attn_out_w"],
        b_out=inp["L0_attn_out_b"].reshape(FC, 128),
        w_g0=np.ascontiguousarray(inp["L0_ffn_w_gating"][0]),
        w_g1=np.ascontiguousarray(inp["L0_ffn_w_gating"][1]),
        w_l=inp["L0_ffn_w_linear"],
        identf=np.eye(128, dtype=np.float32),
        identb=np.eye(128, dtype=ml_dtypes.bfloat16),
        rt=RfT,
    )
    x = np.asarray(inp["L0_attn_x"][0])
    maps = []
    for c in range(NCORES):
        m = dict(com)
        m["x"] = np.ascontiguousarray(x[c * T:(c + 1) * T, :])
        maps.append(m)
    return maps


def kernel(**inputs):
    from concourse.bass_utils import run_bass_kernel_spmd
    nc, _ = build()
    maps = make_inmaps(inputs)
    res = run_bass_kernel_spmd(nc, maps, core_ids=list(range(NCORES)))
    out = np.concatenate([res.results[c]["out"] for c in range(NCORES)], axis=0)
    return out.reshape(1, NCORES * T, D)


# version 02
# speedup vs baseline: 1.0555x; 1.0555x over previous
import numpy as np
import ml_dtypes
import concourse.bass as bass
import concourse.mybir as mybir
from contextlib import ExitStack

F32 = mybir.dt.float32
BF16 = mybir.dt.bfloat16
FP8 = mybir.dt.float8e4
AF = mybir.ActivationFunctionType
OP = mybir.AluOpType

NCORES = 8
T = 1024
D = 2048
FC = 16
J = 8192
JC = 64
NH, HD, RD = 16, 128, 64
KV, CC = 512, 4
QC, QCC = 1024, 8
SK, KC = 8192, 64
SCALE = float(1.0 / np.sqrt(HD + RD))
C44715 = float(np.float32(ml_dtypes.bfloat16(0.044715)))
SQ2PI = float(np.float32(np.sqrt(2 / np.pi).astype(ml_dtypes.bfloat16)))
EPS = 1e-5


class Prog:
    def __init__(self, nc, ctx):
        self.nc, self.ctx = nc, ctx
        self.sems, self.cnt = {}, {}

    def sem(self, name):
        if name not in self.sems:
            self.sems[name] = self.ctx.enter_context(self.nc.semaphore(name))
            self.cnt[name] = 0
        return self.sems[name]

    def inc(self, instr, name, n=1):
        instr.then_inc(self.sem(name), n)
        self.cnt[name] += n
        return instr

    def val(self, name):
        self.sem(name)
        return self.cnt[name]

    def wait(self, eng, name, val=None):
        eng.wait_ge(self.sem(name), self.cnt[name] if val is None else val)

    def ser_begin(self):
        self._ser = []

    def ser(self, eng_name, fn, inc=1):
        self._ser.append((eng_name, fn, inc))

    def ser_emit(self, block, sem_name, pred=None):
        bodies = {"sync": block.sync, "tensor": block.tensor,
                  "vector": block.vector, "scalar": block.scalar,
                  "gpsimd": block.gpsimd}
        ops = self._ser
        sm = self.sem(sem_name)
        starts = []
        v = self.cnt[sem_name]
        for (_, _, inc) in ops:
            starts.append(v)
            v += inc
        self.cnt[sem_name] = v
        for ename in ("sync", "tensor", "vector", "scalar", "gpsimd"):
            todo = [(i, f, inc) for i, (en, f, inc) in enumerate(ops)
                    if en == ename]
            if not todo:
                continue

            def mk(todo_list):
                def _body(eng):
                    if pred is not None:
                        eng.wait_ge(self.sem(pred[0]), pred[1])
                    for i, f, inc in todo_list:
                        eng.wait_ge(sm, starts[i])
                        f(eng).then_inc(sm, inc)
                return _body
            bodies[ename](mk(todo))
        return (sem_name, v)


def build(stop_after=None):
    nc = bass.Bass("TRN2", target_bir_lowering=False)
    ctx = ExitStack()
    P = Prog(nc, ctx)

    def dri(name, shape, dtype=F32):
        return nc.dram_tensor(name, shape, dtype, kind="ExternalInput").ap()

    x_in = dri("x", [T, D])
    w_kvd = dri("w_kvd", [KV, D])
    b_kvd_c = dri("b_kvd_c", [CC, 128])
    w_ku = dri("w_ku", [D, KV])
    b_ku = dri("b_ku", [FC, 128])
    w_vu = dri("w_vu", [D, KV])
    b_vu = dri("b_vu", [1, D])
    w_kr = dri("w_kr", [QC, KV])
    b_kr = dri("b_kr", [8, 128])
    w_qd = dri("w_qd", [QC, D])
    b_qd = dri("b_qd", [QCC, 128])
    w_qu = dri("w_qu", [D, QC])
    b_qu = dri("b_qu", [FC, 128])
    w_qr = dri("w_qr", [QC, QC])
    b_qr = dri("b_qr", [8, 128])
    w_out = dri("w_out", [D, D])
    b_out = dri("b_out", [FC, 128])
    w_g0 = dri("w_g0", [D, J])
    w_g1 = dri("w_g1", [D, J])
    w_l = dri("w_l", [J, D])
    identf = dri("identf", [128, 128])
    identb = dri("identb", [128, 128], BF16)
    rt = dri("rt", [8, 128, 128], BF16)

    out = nc.dram_tensor("out", [T, D], F32, kind="ExternalOutput").ap()

    ag_in = nc.dram_tensor("ag_in", [KV, T], BF16)
    ag_out = nc.dram_tensor("ag_out", [NCORES * KV, T], BF16)
    kc_scr = nc.dram_tensor("kc_scr", [NH, 128, SK], BF16)
    kr_scr = nc.dram_tensor("kr_scr", [NH, RD, SK], BF16)
    v_scr = nc.dram_tensor("v_scr", [NH, KC, 128, HD], BF16)
    qcs_scr = nc.dram_tensor("qcs_scr", [NH, 128, T], BF16)
    qrs_scr = nc.dram_tensor("qrs_scr", [NH, RD, T], BF16)
    wt_k = nc.dram_tensor("wt_k", [KV, D], BF16)
    wt_v = nc.dram_tensor("wt_v", [KV, D], BF16)
    wt_kr = nc.dram_tensor("wt_kr", [KV, QC], BF16)
    wt_q = nc.dram_tensor("wt_q", [QC, D], BF16)
    wt_qr = nc.dram_tensor("wt_qr", [QC, QC], BF16)
    wt_o = nc.dram_tensor("wt_o", [D, D], BF16)

    def sbp(n_, sh, dt):
        return ctx.enter_context(nc.sbuf_tensor(n_, sh, dt))

    XT = sbp("XT", [128, FC, T], F32)
    IDF = sbp("IDF", [128, 128], F32)
    IDB = sbp("IDB", [128, 128], BF16)
    RTS = sbp("RTS", [128, 8, 128], BF16)
    ONES = sbp("ONES", [128, 1], BF16)
    ONESM = sbp("ONESM", [1, 128], BF16)
    XLN8 = sbp("XLN8", [128, FC, T], FP8)
    QDB = sbp("QDB", [128, QCC], F32)
    QUB = sbp("QUB", [128, FC], F32)
    BQR = sbp("BQR", [128, 8], F32)
    CKC = sbp("CKC", [128, FC], F32)
    CKR = sbp("CKR", [128, 8], F32)
    COUT = sbp("COUT", [128, FC], F32)
    BHS = sbp("BHS", [128, NH], F32)

    blk_cm = nc.Block()
    block = blk_cm.__enter__()

    p0 = ExitStack()
    STG = [p0.enter_context(nc.sbuf_tensor(f"STG{i}", [128, D], F32)) for i in range(2)]
    PT = [p0.enter_context(nc.psum_tensor(f"PT{i}", [128, 128], F32)) for i in range(2)]

    @block.sync
    def _(s):
        P.inc(s.dma_start(out=IDF[:], in_=identf), "c_ld", 16)
        P.inc(s.dma_start(out=IDB[:], in_=identb), "c_ld", 16)
        P.inc(s.dma_start(out=RTS[:], in_=rt.rearrange("c p n -> p c n")), "c_ld", 16)
        for tc in range(8):
            if tc >= 2:
                P.wait(s, "x_tp", (tc - 1) * FC)
            P.inc(s.dma_start(out=STG[tc % 2][:],
                              in_=x_in[tc * 128:(tc + 1) * 128, :]),
                  f"x_ld{tc % 2}", 16)

    @block.vector
    def _(v):
        v.memset(ONES[:], 1.0)
        P.inc(v.memset(ONESM[:], 1.0), "c_init", 1)

    @block.tensor
    def _(t):
        t.wait_ge(P.sem("c_ld"), 48)
        for tc in range(8):
            t.wait_ge(P.sem(f"x_ld{tc % 2}"), 16 * (tc // 2 + 1))
            for fc in range(FC):
                n = tc * FC + fc
                if n >= 2:
                    t.wait_ge(P.sem("x_ev"), n - 1)
                P.inc(t.transpose(PT[n % 2][:],
                                  STG[tc % 2][:, fc * 128:(fc + 1) * 128],
                                  IDF[:]), "x_tp", 1)

    @block.vector
    def _(v):
        for tc in range(8):
            for fc in range(FC):
                n = tc * FC + fc
                v.wait_ge(P.sem("x_tp"), n + 1)
                P.inc(v.tensor_copy(XT[:, fc, tc * 128:(tc + 1) * 128],
                                    PT[n % 2][:]), "x_ev", 1)

    p0_done = ("x_ev", P.val("x_ev"))
    p0.close()

    def emit_ln(tag, OUTT, pred):
        g = ExitStack()
        nsb = lambda n_, sh, dt: g.enter_context(nc.sbuf_tensor(tag + n_, sh, dt))
        nps = lambda n_, sh, dt: g.enter_context(nc.psum_tensor(tag + n_, sh, dt))
        NB = 2
        SQ = nsb("SQ", [128, NB, T], F32)
        RA = nsb("RA", [128, NB, T], F32)
        HB = nsb("HB", [128, NB, T], BF16)
        LB = nsb("LB", [128, NB, T], BF16)
        L2B = nsb("L2B", [128, NB, T], BF16)
        ROWS = nsb("ROWS", [1, 6, T], F32)
        RWB = nsb("RWB", [1, 6, T], BF16)
        MRS = nsb("MRS", [128, T], F32)
        RRS = nsb("RRS", [128, T], F32)
        XMS = nsb("XMS", [128, T], F32)
        PSX = nps("PSX", [1, T], F32)
        PSXX = nps("PSXX", [1, T], F32)
        PM = nps("PM", [128, T], F32)
        PR = nps("PR", [128, T], F32)
        NBAT = FC // NB

        def row(i):
            return ROWS[:, i, :]

        sS, sM, sQ_, sF, sR, sO = ("ln_sp", "ln_mm", "ln_sq", "ln_fin",
                                   "ln_rep", "ln_out")
        bS, bM, bQ, bF, bR, bO = (P.val(sS), P.val(sM), P.val(sQ_),
                                  P.val(sF), P.val(sR), P.val(sO))

        @block.scalar
        def _(sc):
            P.wait(sc, pred[0], pred[1])
            for b in range(NBAT):
                if b > 0:
                    P.wait(sc, sS, bS + NBAT + b)
                P.inc(sc.activation(SQ[:], XT[:, NB * b:NB * (b + 1), :],
                                    AF.Square), sQ_, 1)

        @block.vector
        def _(v):
            P.wait(v, pred[0], pred[1])
            for b in range(2 * NBAT):
                if b < NBAT:
                    srcx = XT[:, NB * b:NB * (b + 1), :]
                else:
                    P.wait(v, sQ_, bQ + b - NBAT + 1)
                    srcx = SQ[:]
                if b >= 1:
                    P.wait(v, sM, bM + b)
                v.tensor_copy(HB[:], srcx)
                v.tensor_tensor(RA[:], srcx, HB[:], OP.subtract)
                v.tensor_copy(LB[:], RA[:])
                v.tensor_tensor(RA[:], RA[:], LB[:], OP.subtract)
                P.inc(v.tensor_copy(L2B[:], RA[:]), sS, 1)

        @block.tensor
        def _(t):
            P.wait(t, "c_init", 1)
            for b in range(2 * NBAT):
                ps = PSX if b < NBAT else PSXX
                t.wait_ge(P.sem(sS), bS + b + 1)
                li = None
                for buf in (HB, LB, L2B):
                    for fi in range(NB):
                        first = (b % NBAT == 0 and buf is HB and fi == 0)
                        last = (b % NBAT == NBAT - 1 and buf is L2B
                                and fi == NB - 1)
                        for hh in range(2):
                            li = t.matmul(ps[:, hh * 512:(hh + 1) * 512],
                                          ONES[:],
                                          buf[:, fi, hh * 512:(hh + 1) * 512],
                                          start=first, stop=last)
                P.inc(li, sM, 1)

        @block.scalar
        def _(sc):
            P.wait(sc, sM, bM + NBAT)
            P.inc(sc.activation(row(0), PSX[:], AF.Copy, scale=1.0 / D), sF, 1)
            P.wait(sc, sM, bM + 2 * NBAT)
            P.inc(sc.activation(row(1), PSXX[:], AF.Copy, scale=1.0 / D), sF, 1)

        @block.vector
        def _(v):
            P.wait(v, sF, bF + 2)
            v.tensor_tensor(row(2), row(0), row(0), OP.mult)
            v.tensor_tensor(row(1), row(1), row(2), OP.subtract)
            P.inc(v.tensor_scalar_add(row(1), row(1), EPS), sF, 1)

        @block.scalar
        def _(sc):
            P.wait(sc, sF, bF + 3)
            P.inc(sc.activation(row(3), row(1), AF.Sqrt), sF, 1)

        @block.vector
        def _(v):
            P.wait(v, sF, bF + 4)
            v.reciprocal(row(4), row(3))
            v.tensor_tensor(row(2), row(1), row(4), OP.mult)
            v.tensor_tensor(row(2), row(3), row(2), OP.add)
            v.tensor_scalar_mul(row(2), row(2), 0.5)
            v.reciprocal(row(4), row(2))
            v.tensor_tensor(row(3), row(2), row(4), OP.mult)
            v.tensor_scalar(row(3), row(3), 2.0, -1.0, OP.subtract, OP.mult)
            v.tensor_tensor(row(5), row(4), row(3), OP.mult)
            li = None
            for i, ridx in ((0, 0), (3, 5)):
                srcr = row(ridx)
                v.tensor_copy(RWB[:, i, :], srcr)
                v.tensor_tensor(row(3), srcr, RWB[:, i, :], OP.subtract)
                v.tensor_copy(RWB[:, i + 1, :], row(3))
                v.tensor_tensor(row(3), row(3), RWB[:, i + 1, :], OP.subtract)
                li = v.tensor_copy(RWB[:, i + 2, :], row(3))
            P.inc(li, sF, 1)

        @block.tensor
        def _(t):
            P.wait(t, sF, bF + 5)
            for ps, base in ((PM, 0), (PR, 3)):
                li = None
                for i in range(3):
                    for hh in range(2):
                        li = t.matmul(ps[:, hh * 512:(hh + 1) * 512], ONESM[:],
                                      RWB[:, base + i, hh * 512:(hh + 1) * 512],
                                      start=(i == 0), stop=(i == 2))
                P.inc(li, sR, 1)

        @block.vector
        def _(v):
            P.wait(v, sR, bR + 2)
            v.tensor_copy(MRS[:], PM[:])
            v.tensor_copy(RRS[:], PR[:])
            for fc in range(FC):
                v.tensor_tensor(XMS[:], XT[:, fc, :], MRS[:], OP.subtract)
                P.inc(v.tensor_tensor(OUTT[:, fc, :], XMS[:], RRS[:], OP.mult),
                      sO, 1)

        done = (sO, P.val(sO))
        g.close()
        return done

    def emit_ffn(tag, pred):
        g = ExitStack()
        nsb = lambda n_, sh, dt: g.enter_context(nc.sbuf_tensor(tag + n_, sh, dt))
        nps = lambda n_, sh, dt: g.enter_context(nc.psum_tensor(tag + n_, sh, dt))
        W0B = [nsb(f"W0B{i}", [128, FC, 128], FP8) for i in range(2)]
        W1B = [nsb(f"W1B{i}", [128, FC, 128], FP8) for i in range(2)]
        GATE = [nsb(f"GATE{i}", [128, 4, 512], BF16) for i in range(2)]
        FF1B = [nsb(f"FF1B{i}", [128, 4, 512], BF16) for i in range(2)]
        TM1 = nsb("TM1", [128, 4, 512], BF16)
        TM2 = nsb("TM2", [128, 4, 512], BF16)
        ACT8 = nsb("ACT8", [128, JC, T], FP8)
        WLE = [nsb(f"WLE{i}", [128, JC, 128], FP8) for i in range(2)]
        OBF = [nsb(f"OBF{i}", [128, 512], BF16) for i in range(2)]
        PG = [nps(f"PG{i}", [128, 512], F32) for i in range(2)]
        PF = [nps(f"PF{i}", [128, 512], F32) for i in range(2)]
        PO = [nps(f"PO{i}", [128, 512], F32) for i in range(2)]

        sWL, sMM, sEV, sGD, sTH, sTI, sWE, sPM, sOB, sRS = (
            "ff_wl", "ff_mm", "ff_ev", "ff_gd", "ff_th", "ff_ti", "ff_we",
            "ff_pm", "ff_ob", "ff_rs")
        bWL0, bWL1 = P.val(sWL + "0"), P.val(sWL + "1")
        bMM, bEV, bGD, bTH, bTI = (P.val(sMM), P.val(sEV), P.val(sGD),
                                   P.val(sTH), P.val(sTI))
        bWE0, bWE1 = P.val(sWE + "0"), P.val(sWE + "1")
        bPM, bOB, bRS = P.val(sPM), P.val(sOB), P.val(sRS)
        NJT = 2 * JC

        @block.gpsimd
        def _(gp):
            P.wait(gp, pred[0], pred[1])
            for n in range(NJT):
                tt, jc = divmod(n, JC)
                if n >= 2:
                    P.wait(gp, sMM, bMM + n - 1)
                src0 = w_g0.rearrange("(fc p) j -> p fc j", p=128)[:, :, jc * 128:(jc + 1) * 128]
                src1 = w_g1.rearrange("(fc p) j -> p fc j", p=128)[:, :, jc * 128:(jc + 1) * 128]
                P.inc(gp.dma_start(out=W0B[n % 2][:], in_=src0), sWL + str(n % 2), 16)
                P.inc(gp.dma_start(out=W1B[n % 2][:], in_=src1), sWL + str(n % 2), 16)

        @block.tensor
        def _(t):
            P.wait(t, pred[0], pred[1])
            for n in range(NJT):
                tt, jc = divmod(n, JC)
                t.wait_ge(P.sem(sWL + str(n % 2)), (bWL0 if n % 2 == 0 else bWL1) + 32 * (n // 2 + 1))
                if n >= 2:
                    t.wait_ge(P.sem(sEV), bEV + 2 * (n - 1))
                li = None
                for fc in range(FC):
                    li = t.matmul(PG[n % 2][:], W0B[n % 2][:, fc, :],
                                  XLN8[:, fc, tt * 512:(tt + 1) * 512],
                                  start=(fc == 0), stop=(fc == FC - 1))
                for fc in range(FC):
                    li = t.matmul(PF[n % 2][:], W1B[n % 2][:, fc, :],
                                  XLN8[:, fc, tt * 512:(tt + 1) * 512],
                                  start=(fc == 0), stop=(fc == FC - 1))
                P.inc(li, sMM, 1)

        @block.scalar
        def _(sc):
            for n in range(NJT):
                tt, jc = divmod(n, JC)
                gslot = jc % 4
                gbuf = (n // 4) % 2
                P.wait(sc, sMM, bMM + n + 1)
                if n >= 8 and gslot == 0:
                    P.wait(sc, sGD, bGD + n // 4 - 1)
                P.inc(sc.activation(GATE[gbuf][:, gslot, :], PG[n % 2][:],
                                    AF.Copy), sEV, 1)
                P.inc(sc.activation(FF1B[gbuf][:, gslot, :], PF[n % 2][:],
                                    AF.Copy), sEV, 1)
                if gslot == 3:
                    gi = n // 4
                    P.wait(sc, sTI, bTI + gi + 1)
                    P.inc(sc.activation(TM2[:], TM1[:], AF.Tanh), sTH, 1)

        @block.vector
        def _(v):
            for gi in range(32):
                gbuf = gi % 2
                tt = gi // 16
                P.wait(v, sEV, bEV + (gi + 1) * 8)
                xg = GATE[gbuf][:]
                v.tensor_tensor(TM1[:], xg, xg, OP.mult)
                v.tensor_tensor(TM2[:], TM1[:], xg, OP.mult)
                v.tensor_scalar_mul(TM1[:], TM2[:], C44715)
                v.tensor_tensor(TM2[:], xg, TM1[:], OP.add)
                P.inc(v.tensor_scalar_mul(TM1[:], TM2[:], SQ2PI), sTI, 1)
                P.wait(v, sTH, bTH + gi + 1)
                v.tensor_scalar(TM1[:], TM2[:], 1.0, 0.5, OP.add, OP.mult)
                v.tensor_tensor(TM2[:], xg, TM1[:], OP.mult)
                v.tensor_tensor(TM1[:], TM2[:], FF1B[gbuf][:], OP.mult)
                jbase = (gi % 16) * 4
                P.inc(v.tensor_copy(
                    ACT8[:, jbase:jbase + 4, tt * 512:(tt + 1) * 512],
                    TM1[:]), sGD, 1)

        @block.gpsimd
        def _(gp):
            for ec in range(FC):
                if ec >= 2:
                    P.wait(gp, sPM, bPM + 2 * (ec - 1))
                src = w_l.rearrange("(kc p) e -> p kc e", p=128)[:, :, ec * 128:(ec + 1) * 128]
                P.inc(gp.dma_start(out=WLE[ec % 2][:], in_=src),
                      sWE + str(ec % 2), 16)

        @block.tensor
        def _(t):
            P.wait(t, sGD, bGD + 32)
            for ec in range(FC):
                t.wait_ge(P.sem(sWE + str(ec % 2)), (bWE0 if ec % 2 == 0 else bWE1) + 16 * (ec // 2 + 1))
                for tt in range(2):
                    n = ec * 2 + tt
                    if n >= 2:
                        t.wait_ge(P.sem(sOB), bOB + n - 1)
                    li = None
                    for jc in range(JC):
                        li = t.matmul(PO[n % 2][:], WLE[ec % 2][:, jc, :],
                                      ACT8[:, jc, tt * 512:(tt + 1) * 512],
                                      start=(jc == 0), stop=(jc == JC - 1))
                    P.inc(li, sPM, 1)

        @block.scalar
        def _(sc):
            for ec in range(FC):
                for tt in range(2):
                    n = ec * 2 + tt
                    P.wait(sc, sPM, bPM + n + 1)
                    if n >= 2:
                        P.wait(sc, sRS, bRS + n - 1)
                    P.inc(sc.activation(OBF[n % 2][:], PO[n % 2][:], AF.Copy),
                          sOB, 1)

        @block.vector
        def _(v):
            for ec in range(FC):
                for tt in range(2):
                    n = ec * 2 + tt
                    P.wait(v, sOB, bOB + n + 1)
                    P.inc(v.scalar_tensor_tensor(
                        XT[:, ec, tt * 512:(tt + 1) * 512], OBF[n % 2][:], 0.5,
                        XT[:, ec, tt * 512:(tt + 1) * 512], OP.mult, OP.add),
                        sRS, 1)

        done = (sRS, P.val(sRS))
        g.close()
        return done

    def emit_attention(pred, sub=None):
        gat = ExitStack()
        QCL = gat.enter_context(nc.sbuf_tensor("QCL", [128, QCC, T], BF16))
        ga = ExitStack()
        XLN2 = ga.enter_context(nc.sbuf_tensor("XLN2", [128, FC, T], BF16))
        ln2_done = emit_ln("l2", XLN2, pred)
        if sub == "ln2":
            ga.close(); gat.close()
            return ln2_done

        gb = ExitStack()
        nsb = lambda n_, sh, dt: gb.enter_context(nc.sbuf_tensor(n_, sh, dt))
        nps = lambda n_, sh, dt: gb.enter_context(nc.psum_tensor(n_, sh, dt))
        KV0L = nsb("KV0L", [128, CC, T], BF16)
        BQDN = nsb("BQDN", [QCC, 128], F32)
        TPB = [nps(f"TPB{i}", [128, 128], BF16) for i in range(2)]
        TPFb = nps("TPFb", [128, 128], F32)
        PKV = [nps(f"PKV{i}", [128, 512], F32) for i in range(2)]

        @block.sync
        def _(s):
            P.wait(s, ln2_done[0], ln2_done[1])
            P.inc(s.dma_start(out=BQDN[:], in_=b_qd), "at_bl", 16)

        gb1 = ExitStack()
        WKVN = gb1.enter_context(nc.sbuf_tensor("WKVN", [128, CC, D], BF16))
        WKVT = gb1.enter_context(nc.sbuf_tensor("WKVT", [128, FC, KV], BF16))

        @block.gpsimd
        def _(gp):
            P.wait(gp, ln2_done[0], ln2_done[1])
            P.inc(gp.dma_start(out=WKVN[:], in_=w_kvd.rearrange(
                "(c p) f -> p c f", p=128)), "at_wl", 16)

        @block.tensor
        def _(t):
            P.wait(t, "at_wl", 16)
            n = 0
            for cc in range(CC):
                for fc in range(FC):
                    if n >= 2:
                        P.wait(t, "at_we", n - 1)
                    P.inc(t.transpose(TPB[n % 2][:],
                                      WKVN[:, cc, fc * 128:(fc + 1) * 128],
                                      IDB[:]), "at_wt", 1)
                    n += 1

        @block.vector
        def _(v):
            n = 0
            for cc in range(CC):
                for fc in range(FC):
                    v.wait_ge(P.sem("at_wt"), n + 1)
                    P.inc(v.tensor_copy(WKVT[:, fc, cc * 128:(cc + 1) * 128],
                                        TPB[n % 2][:]), "at_we", 1)
                    n += 1

        @block.tensor
        def _(t):
            P.wait(t, ln2_done[0], ln2_done[1])
            P.wait(t, "at_we", 64)
            n = 0
            for cc in range(CC):
                for tt in range(2):
                    if n >= 2:
                        P.wait(t, "kv_ev", n - 1)
                    li = None
                    for fc in range(FC):
                        li = t.matmul(PKV[n % 2][:],
                                      WKVT[:, fc, cc * 128:(cc + 1) * 128],
                                      XLN2[:, fc, tt * 512:(tt + 1) * 512],
                                      start=(fc == 0), stop=(fc == FC - 1))
                    P.inc(li, "kv_mm", 1)
                    n += 1

        @block.scalar
        def _(sc):
            n = 0
            for cc in range(CC):
                for tt in range(2):
                    P.wait(sc, "kv_mm", n + 1)
                    P.inc(sc.activation(KV0L[:, cc, tt * 512:(tt + 1) * 512],
                                        PKV[n % 2][:], AF.Copy), "kv_ev", 1)
                    n += 1

        @block.gpsimd
        def _(gp):
            P.wait(gp, "kv_ev", 8)
            P.inc(gp.dma_start(out=ag_in.ap().rearrange("(c p) t -> p c t", p=128),
                               in_=KV0L[:]), "ag_w", 16)
            gp.wait_ge(P.sem("ag_w"), 16)
            P.inc(gp.collective_compute(
                "AllGather", OP.bypass,
                replica_groups=[list(range(NCORES))],
                ins=[ag_in.ap().opt()], outs=[ag_out.ap().opt()]), "ag_c", 1)

        gb1.close()

        gb2 = ExitStack()
        WQDN = gb2.enter_context(nc.sbuf_tensor("WQDN", [128, QCC, D], BF16))
        WQDT = gb2.enter_context(nc.sbuf_tensor("WQDT", [128, FC, QC], BF16))

        @block.gpsimd
        def _(gp):
            P.wait(gp, "kv_mm", 8)
            P.inc(gp.dma_start(out=WQDN[:], in_=w_qd.rearrange(
                "(c p) f -> p c f", p=128)), "at_wl", 32)

        @block.tensor
        def _(t):
            t.wait_ge(P.sem("at_wl"), 48)
            n = 64
            for qcc in range(QCC):
                for fc in range(FC):
                    if n >= 2:
                        P.wait(t, "at_we", n - 1)
                    P.inc(t.transpose(TPB[n % 2][:],
                                      WQDN[:, qcc, fc * 128:(fc + 1) * 128],
                                      IDB[:]), "at_wt", 1)
                    n += 1
            P.wait(t, "at_bl", 16)
            P.inc(t.transpose(TPFb[:, 0:QCC], BQDN[:], IDF[0:QCC, 0:QCC]), "at_bt", 1)

        @block.vector
        def _(v):
            n = 64
            for qcc in range(QCC):
                for fc in range(FC):
                    v.wait_ge(P.sem("at_wt"), n + 1)
                    P.inc(v.tensor_copy(WQDT[:, fc, qcc * 128:(qcc + 1) * 128],
                                        TPB[n % 2][:]), "at_we", 1)
                    n += 1
            P.wait(v, "at_bt", 1)
            P.inc(v.tensor_copy(QDB[:], TPFb[:, 0:QCC]), "at_be", 1)

        @block.tensor
        def _(t):
            P.wait(t, "at_we", 192)
            P.wait(t, "at_be", 1)
            n = 8
            for qcc in range(QCC):
                for tt in range(2):
                    if n >= 2:
                        P.wait(t, "kv_ev", n - 1)
                    li = None
                    for fc in range(FC):
                        li = t.matmul(PKV[n % 2][:],
                                      WQDT[:, fc, qcc * 128:(qcc + 1) * 128],
                                      XLN2[:, fc, tt * 512:(tt + 1) * 512],
                                      start=(fc == 0), stop=(fc == FC - 1))
                    P.inc(li, "kv_mm", 1)
                    n += 1

        @block.scalar
        def _(sc):
            n = 8
            for qcc in range(QCC):
                for tt in range(2):
                    P.wait(sc, "kv_mm", n + 1)
                    P.inc(sc.activation(QCL[:, qcc, tt * 512:(tt + 1) * 512],
                                        PKV[n % 2][:], AF.Identity,
                                        bias=QDB[:, qcc:qcc + 1]), "kv_ev", 1)
                    n += 1

        gb2.close()

        b_done = ("kv_ev", P.val("kv_ev"))
        gb.close()
        ga.close()
        if sub == "bphase":
            gat.close()
            return b_done

        gco = ExitStack()
        osb = lambda n_, sh, dt: gco.enter_context(nc.sbuf_tensor(n_, sh, dt))
        ops_ = lambda n_, sh, dt: gco.enter_context(nc.psum_tensor(n_, sh, dt))
        CVREP = osb("CVREP", [128, D], F32)
        gcr = ExitStack()
        rsb = lambda n_, sh, dt: gcr.enter_context(nc.sbuf_tensor(n_, sh, dt))
        CROW = rsb("CROW", [1, 1, D], F32)
        CRB = rsb("CRB", [1, 3, D], BF16)
        SMLF = rsb("SMLF", [16, 128], F32)
        SMLB = rsb("SMLB", [CC, 3, 128], BF16)
        BKVT = rsb("BKVT", [128, 3, CC], BF16)
        BVU = rsb("BVU", [1, D], F32)
        BRB = rsb("BRB", [128, 8], BF16)
        gcp = ExitStack()
        cps_ = lambda n_, sh, dt: gcp.enter_context(nc.psum_tensor(n_, sh, dt))
        CPT = [cps_(f"CPT{i}", [128, 128], BF16) for i in range(2)]
        CPQ = cps_("CPQ", [128, D], F32)
        TPX = cps_("TPX", [128, 128], F32)

        P.ser_begin()
        P.ser("sync", lambda e: e.dma_start(out=SMLF[0:CC, :], in_=b_kvd_c), 16)
        P.ser("sync", lambda e: e.dma_start(out=BVU[:], in_=b_vu), 16)
        P.ser("vector", lambda e: e.tensor_copy(SMLB[:, 0, :], SMLF[0:CC, :]))
        P.ser("vector", lambda e: e.tensor_tensor(SMLF[0:CC, :], SMLF[0:CC, :],
                                                  SMLB[:, 0, :], OP.subtract))
        P.ser("vector", lambda e: e.tensor_copy(SMLB[:, 1, :], SMLF[0:CC, :]))
        P.ser("vector", lambda e: e.tensor_tensor(SMLF[0:CC, :], SMLF[0:CC, :],
                                                  SMLB[:, 1, :], OP.subtract))
        P.ser("vector", lambda e: e.tensor_copy(SMLB[:, 2, :], SMLF[0:CC, :]))
        for sp in range(3):
            P.ser("tensor", lambda e, sp=sp: e.transpose(
                CPT[0][:, 0:CC], SMLB[:, sp, :], IDB[0:CC, 0:CC]))
            P.ser("vector", lambda e, sp=sp: e.tensor_copy(
                BKVT[:, sp, :], CPT[0][:, 0:CC]))
        for (bsrc, bdst) in ((b_qu, QUB), (b_ku, CKC), (b_out, COUT)):
            P.ser("sync", lambda e, bsrc=bsrc: e.dma_start(out=SMLF[:], in_=bsrc), 16)
            P.ser("tensor", lambda e: e.transpose(TPX[:, 0:FC], SMLF[:], IDF[0:FC, 0:FC]))
            P.ser("vector", lambda e, bdst=bdst: e.tensor_copy(bdst[:], TPX[:, 0:FC]))
        for (bsrc, bdst) in ((b_qr, BQR), (b_kr, CKR)):
            P.ser("sync", lambda e, bsrc=bsrc: e.dma_start(out=SMLF[0:8, :], in_=bsrc), 16)
            P.ser("tensor", lambda e: e.transpose(TPX[:, 0:8], SMLF[0:8, :], IDF[0:8, 0:8]))
            P.ser("vector", lambda e: e.tensor_copy(BRB[:], TPX[:, 0:8]))
            for i in range(8):
                P.ser("tensor", lambda e, i=i: e.matmul(
                    TPX[:, i:i + 1], RTS[:, i, :], BRB[:, i:i + 1],
                    start=True, stop=True))
            P.ser("vector", lambda e, bdst=bdst: e.tensor_copy(bdst[:], TPX[:, 0:8]))
        cst_chain1 = P.ser_emit(block, "cst", pred=b_done)[1]

        gci = ExitStack()
        WNAT = gci.enter_context(nc.sbuf_tensor("WNAT", [128, FC, QC], BF16))
        WTS = gci.enter_context(nc.sbuf_tensor("WTS", [128, QCC, D], BF16))
        WEIGHTS = [
            ("ku", w_ku, FC, CC, CC * 128, wt_k, False, "ckc"),
            ("vu", w_vu, FC, CC, CC * 128, wt_v, False, "cv"),
            ("kr", w_kr, 8, CC, CC * 128, wt_kr, True, "ckr"),
            ("qu", w_qu, FC, QCC, QCC * 128, wt_q, False, None),
            ("qr", w_qr, 8, QCC, QCC * 128, wt_qr, True, None),
        ]
        prev = [None]
        for (nm, wsrc, nrc, ncc, wid, wdst, rot, cst) in WEIGHTS:
            pv = prev[0]
            bL = P.val("c_l"); bR_ = P.val("c_r"); bRC = P.val("c_rc")
            bT = P.val("c_t"); bE = P.val("c_e"); bSt = P.val("c_s")

            @block.gpsimd
            def _(gp, nm=nm, wsrc=wsrc, nrc=nrc, wid=wid, pv=pv):
                if pv is None:
                    P.wait(gp, b_done[0], b_done[1])
                    gp.wait_ge(P.sem("ag_w"), 16)
                else:
                    P.wait(gp, "c_t", pv[1])
                P.inc(gp.dma_start(out=WNAT[:, 0:nrc, 0:wid],
                                   in_=wsrc.rearrange("(rc p) c -> p rc c", p=128)),
                      "c_l", 16)

            if rot:
                @block.tensor
                def _(t, nm=nm, nrc=nrc, wid=wid, pv=pv):
                    t.wait_ge(P.sem("c_l"), bL + 16)
                    t.wait_ge(P.sem("cst"), pv[2] if pv is not None else cst_chain1)
                    for i in range(nrc):
                        if i >= 1:
                            P.wait(t, "c_rc", bRC + i)
                        li = None
                        for hq in range(wid // 512):
                            li = t.matmul(CPQ[:, hq * 512:(hq + 1) * 512],
                                          RTS[:, i, :],
                                          WNAT[:, i, hq * 512:(hq + 1) * 512],
                                          start=True, stop=True)
                        P.inc(li, "c_r", 1)

                @block.vector
                def _(v, nm=nm, nrc=nrc, wid=wid):
                    for i in range(nrc):
                        v.wait_ge(P.sem("c_r"), bR_ + i + 1)
                        P.inc(v.tensor_copy(WNAT[:, i, 0:wid], CPQ[:, 0:wid]),
                              "c_rc", 1)

            @block.tensor
            def _(t, nm=nm, nrc=nrc, ncc=ncc, rot=rot, pv=pv):
                if rot:
                    P.wait(t, "c_rc", bRC + nrc)
                else:
                    t.wait_ge(P.sem("c_l"), bL + 16)
                    t.wait_ge(P.sem("cst"),
                              pv[2] if pv is not None else cst_chain1)
                if pv is not None:
                    t.wait_ge(P.sem("c_s"), bSt)
                n = 0
                for rc in range(nrc):
                    for cc2 in range(ncc):
                        if n >= 2:
                            P.wait(t, "c_e", bE + n - 1)
                        P.inc(t.transpose(CPT[n % 2][:],
                                          WNAT[:, rc, cc2 * 128:(cc2 + 1) * 128],
                                          IDB[:]), "c_t", 1)
                        n += 1

            @block.vector
            def _(v, nm=nm, nrc=nrc, ncc=ncc):
                n = 0
                for rc in range(nrc):
                    for cc2 in range(ncc):
                        v.wait_ge(P.sem("c_t"), bT + n + 1)
                        P.inc(v.tensor_copy(WTS[:, cc2, rc * 128:(rc + 1) * 128],
                                            CPT[n % 2][:]), "c_e", 1)
                        n += 1

            @block.sync
            def _(s, nm=nm, nrc=nrc, ncc=ncc, wdst=wdst):
                P.wait(s, "c_e", P.val("c_e"))
                P.inc(s.dma_start(
                    out=wdst.rearrange("(cb p) d -> p cb d", p=128),
                    in_=WTS[:, 0:ncc, 0:nrc * 128]), "c_s", 16)

            if cst is not None:
                P.ser_begin()
                nq = (nrc * 128) // 512
                for cc2 in range(ncc):
                    for sp in range(3):
                        for hh in range(nq):
                            def mmf(e, cc2=cc2, sp=sp, hh=hh,
                                    fi=(cc2 == 0 and sp == 0),
                                    last=(cc2 == ncc - 1 and sp == 2)):
                                return e.matmul(
                                    CPQ[0:1, hh * 512:(hh + 1) * 512],
                                    BKVT[:, sp, cc2:cc2 + 1],
                                    WTS[:, cc2, hh * 512:(hh + 1) * 512],
                                    start=fi, stop=last)
                            P.ser("tensor", mmf)
                P.ser("vector", lambda e, nrc=nrc: e.tensor_copy(
                    CROW[:, 0, 0:nrc * 128], CPQ[0:1, 0:nrc * 128]))
                if cst == "cv":
                    P.ser("vector", lambda e: e.tensor_tensor(
                        CROW[:, 0, :], CROW[:, 0, :], BVU[:], OP.add))
                    P.ser("vector", lambda e: e.tensor_copy(CRB[:, 0, :], CROW[:, 0, :]))
                    P.ser("vector", lambda e: e.tensor_tensor(
                        CVREP[0:1, :], CROW[:, 0, :], CRB[:, 0, :], OP.subtract))
                    P.ser("vector", lambda e: e.tensor_copy(CRB[:, 1, :], CVREP[0:1, :]))
                    P.ser("vector", lambda e: e.tensor_tensor(
                        CVREP[0:1, :], CVREP[0:1, :], CRB[:, 1, :], OP.subtract))
                    P.ser("vector", lambda e: e.tensor_copy(CRB[:, 2, :], CVREP[0:1, :]))
                    for hh in range(4):
                        for sp in range(3):
                            P.ser("tensor", lambda e, sp=sp, hh=hh: e.matmul(
                                CPQ[:, hh * 512:(hh + 1) * 512], ONESM[:],
                                CRB[:, sp, hh * 512:(hh + 1) * 512],
                                start=(sp == 0), stop=(sp == 2)))
                    P.ser("vector", lambda e: e.tensor_copy(CVREP[:], CPQ[:]))
                else:
                    dstt = CKC if cst == "ckc" else CKR
                    nch = (nrc * 128) // 128
                    for i in range(nch):
                        P.ser("tensor", lambda e, i=i: e.transpose(
                            TPX[:, i:i + 1],
                            CROW[:, 0, i * 128:(i + 1) * 128], IDF[0:1, 0:1]))
                    P.ser("vector", lambda e, dstt=dstt, nch=nch: e.tensor_tensor(
                        dstt[:, 0:nch], dstt[:, 0:nch], TPX[:, 0:nch], OP.add))
                P.ser_emit(block, "cst", pred=("c_e", P.val("c_e")))
            prev[0] = (nm, P.val("c_t"), P.val("cst"))

        cs_total = P.val("c_s")
        wout_pred_ce = ("c_s", cs_total)
        cst_after_weights = P.val("cst")
        gci.close()

        gcr.close()
        gwo = ExitStack()
        WOTS = gwo.enter_context(nc.sbuf_tensor("WOTS", [128, FC, D], BF16))
        WONF = [gwo.enter_context(nc.sbuf_tensor(f"WONF{i}", [128, D], F32))
                for i in range(2)]
        WONB = gwo.enter_context(nc.sbuf_tensor("WONB", [128, D], BF16))
        SCRC = gwo.enter_context(nc.sbuf_tensor("SCRC", [128, D], BF16))
        CSCR = gwo.enter_context(nc.sbuf_tensor("CSCR", [128, FC], F32))

        @block.sync
        def _(s):
            P.wait(s, wout_pred_ce[0], wout_pred_ce[1])
            P.wait(s, "cst", cst_after_weights)
            for ec in range(FC):
                if ec >= 2:
                    P.wait(s, "wo_n", ec - 1)
                P.inc(s.dma_start(out=WONF[ec % 2][:],
                                  in_=w_out[ec * 128:(ec + 1) * 128, :]),
                      "wo_l" + str(ec % 2), 16)

        @block.vector
        def _(v):
            for ec in range(FC):
                v.wait_ge(P.sem("wo_l" + str(ec % 2)), 16 * (ec // 2 + 1))
                P.inc(v.scalar_tensor_tensor(SCRC[:], WONF[ec % 2][:], 1.0,
                                             CVREP[:], OP.mult, OP.mult,
                                             accum_out=CSCR[:, ec:ec + 1]),
                      "wo_c", 1)
                if ec >= 1:
                    P.wait(v, "wo_t", ec * FC)
                P.inc(v.tensor_copy(WONB[:], WONF[ec % 2][:]), "wo_n", 1)
                for rc in range(FC):
                    nn = ec * FC + rc
                    v.wait_ge(P.sem("wo_t"), nn + 1)
                    P.inc(v.tensor_copy(WOTS[:, rc, ec * 128:(ec + 1) * 128],
                                        CPT[nn % 2][:]), "wo_e", 1)
            P.wait(v, "wo_c", FC)
            P.inc(v.tensor_tensor(COUT[:], COUT[:], CSCR[:], OP.add), "wo_f", 1)

        @block.tensor
        def _(t):
            for ec in range(FC):
                P.wait(t, "wo_n", ec + 1)
                for rc in range(FC):
                    nn = ec * FC + rc
                    if nn >= 2:
                        P.wait(t, "wo_e", nn - 1)
                    P.inc(t.transpose(CPT[nn % 2][:],
                                      WONB[:, rc * 128:(rc + 1) * 128], IDB[:]),
                          "wo_t", 1)

        @block.sync
        def _(s):
            P.wait(s, "wo_e", FC * FC)
            P.inc(s.dma_start(out=wt_o.rearrange("(cb p) d -> p cb d", p=128),
                              in_=WOTS[:]), "wo_s", 16)

        if sub == "ctx":
            tap_cv2 = nc.dram_tensor("tap_cv2", [128, D], F32,
                                     kind="ExternalOutput").ap()
            tap_cs2 = nc.dram_tensor("tap_cs2", [128, FC], F32,
                                     kind="ExternalOutput").ap()
            tap_wts = nc.dram_tensor("tap_wts", [KV, D], BF16,
                                     kind="ExternalOutput").ap()

            @block.gpsimd
            def _(gp):
                P.wait(gp, "wo_f", 1)
                P.wait(gp, "wo_s", 16)
                P.inc(gp.dma_start(out=tap_cv2, in_=CVREP[:]), "dbg5", 16)
                P.inc(gp.dma_start(out=tap_cs2, in_=CSCR[:]), "dbg5", 16)
                P.inc(gp.dma_start(out=tap_wts, in_=wt_v.ap()), "dbg5", 16)
                P.wait(gp, "dbg5")
        gwo.close()
        cst_total = P.val("cst")
        gcp.close()
        gco.close()
        if sub == "prep":
            gat.close()
            return ("cst", cst_total)

        gd = ExitStack()
        dsb = lambda n_, sh, dt: gd.enter_context(nc.sbuf_tensor(n_, sh, dt))
        dps = lambda n_, sh, dt: gd.enter_context(nc.psum_tensor(n_, sh, dt))
        KV0SB = dsb("KV0SB", [128, CC, SK], BF16)
        WKH = [dsb(f"WKH{i}", [128, CC, 128], BF16) for i in range(2)]
        WVH = [dsb(f"WVH{i}", [128, CC, 128], BF16) for i in range(2)]
        WKRH = [dsb(f"WKRH{i}", [128, CC, RD], BF16) for i in range(2)]
        WQH = [dsb(f"WQH{i}", [128, QCC, 128], BF16) for i in range(2)]
        WQRH = [dsb(f"WQRH{i}", [128, QCC, RD], BF16) for i in range(2)]
        KSTG = [dsb(f"KSTG{i}", [128, 512], BF16) for i in range(2)]
        KRSTG = [dsb(f"KRSTG{i}", [RD, 512], BF16) for i in range(2)]
        VSTG = [dsb(f"VSTG{i}", [128, 512], BF16) for i in range(2)]
        QSTG = dsb("QSTG", [128, T], BF16)
        QRSTG = dsb("QRSTG", [RD, T], BF16)
        MXALL = dsb("MXALL", [128, NH], F32)
        MXH = dsb("MXH", [16, 2], F32)
        MXB = dsb("MXB", [1, 16], BF16)
        PKC = [dps(f"PKC{i}", [128, 512], F32) for i in range(2)]
        PKR = [dps(f"PKR{i}", [RD, 512], F32) for i in range(2)]
        PV = [dps(f"PV{i}", [128, 512], F32) for i in range(2)]
        PSS = [dps(f"PSS{i}", [128, 512], F32) for i in range(2)]

        @block.gpsimd
        def _(gp):
            gp.wait_ge(P.sem("ag_c"), 1)
            P.wait(gp, "wo_s", 16)
            P.wait(gp, "wo_f", 1)
            for cc in range(CC):
                for r in range(NCORES):
                    P.inc(gp.dma_start(
                        out=KV0SB[:, cc, r * T:(r + 1) * T],
                        in_=ag_out.ap()[r * KV + cc * 128:r * KV + (cc + 1) * 128, :]),
                        "kv0_l", 16)

        NT2 = 16

        for h in range(NH):
            hb = (h % 2) * RD
            hc2 = h // 2

            @block.sync
            def _(s, h=h):
                if h >= 2:
                    P.wait(s, "g_hd", h - 1)
                if h == 0:
                    P.wait(s, "c_s", cs_total)
                    P.wait(s, "wo_s", 16)
                    P.wait(s, "wo_f", 1)
                for (buf, wsrc, wid2) in ((WKH, wt_k, 128), (WVH, wt_v, 128),
                                          (WKRH, wt_kr, RD), (WQH, wt_q, 128),
                                          (WQRH, wt_qr, RD)):
                    P.inc(s.dma_start(
                        out=buf[h % 2][:],
                        in_=wsrc[:, h * wid2:(h + 1) * wid2].rearrange(
                            "(cc p) d -> p cc d", p=128)),
                        "dw" + str(h % 2), 16)

            @block.tensor
            def _(t, h=h):
                t.wait_ge(P.sem("dw" + str(h % 2)), 80 * (h // 2 + 1))
                if h == 0:
                    P.wait(t, "kv0_l", 16 * CC * NCORES)
                    P.wait(t, "wo_s", 16)
                    P.wait(t, "wo_f", 1)
                for ti in range(2):
                    if h >= 1 or ti >= 1:
                        pass
                    if h >= 1:
                        t.wait_ge(P.sem("g_ke"), NT2 * (h - 1) + 15 + ti)
                    li = None
                    for qcc in range(QCC):
                        li = t.matmul(PKC[ti][:], WQH[h % 2][:, qcc, :],
                                      QCL[:, qcc, ti * 512:(ti + 1) * 512],
                                      start=(qcc == 0), stop=(qcc == QCC - 1))
                    P.inc(li, "g_qm", 1)
                for ti in range(2):
                    if h >= 1:
                        t.wait_ge(P.sem("g_re"), NT2 * (h - 1) + 15 + ti)
                    li = None
                    for qcc in range(QCC):
                        li = t.matmul(PKR[ti][:], WQRH[h % 2][:, qcc, :],
                                      QCL[:, qcc, ti * 512:(ti + 1) * 512],
                                      start=(qcc == 0), stop=(qcc == QCC - 1))
                    P.inc(li, "g_qm", 1)

            @block.scalar
            def _(sc, h=h, hb=hb, hc2=hc2):
                if h == 0:
                    P.wait(sc, "cst", cst_total)
                    P.wait(sc, "wo_s", 16)
                    P.wait(sc, "wo_f", 1)
                sc.wait_ge(P.sem("sq"), 16 * h)
                sc.wait_ge(P.sem("g_sm"), h)
                for ti in range(2):
                    sc.wait_ge(P.sem("g_qm"), 4 * h + ti + 1)
                    P.inc(sc.activation(QSTG[:, ti * 512:(ti + 1) * 512],
                                        PKC[ti][:], AF.Identity,
                                        bias=QUB[:, h:h + 1]), "g_qe", 1)
                sc.wait_ge(P.sem("sqr"), 16 * h)
                for ti in range(2):
                    sc.wait_ge(P.sem("g_qm"), 4 * h + ti + 3)
                    P.inc(sc.activation(QRSTG[0:RD, ti * 512:(ti + 1) * 512],
                                        PKR[ti][0:RD, :], AF.Identity,
                                        bias=BQR[hb:hb + RD, hc2:hc2 + 1]),
                          "g_qe", 1)

            @block.sync
            def _(s, h=h):
                s.wait_ge(P.sem("g_qe"), 4 * h + 2)
                P.inc(s.dma_start(out=qcs_scr.ap()[h], in_=QSTG[:]), "sq", 16)
                s.wait_ge(P.sem("g_qe"), 4 * h + 4)
                P.inc(s.dma_start(out=qrs_scr.ap()[h], in_=QRSTG[:]), "sqr", 16)

            @block.tensor
            def _(t, h=h):
                t.wait_ge(P.sem("g_qe"), 4 * h + 1)
                li = None
                for cc in range(CC):
                    li = t.matmul(PKC[0][:], WKH[h % 2][:, cc, :],
                                  KV0SB[:, cc, 0:512], start=(cc == 0),
                                  stop=(cc == CC - 1))
                P.inc(li, "g_km", 1)
                t.wait_ge(P.sem("g_qe"), 4 * h + 3)
                li = None
                for cc in range(CC):
                    li = t.matmul(PKR[0][:], WKRH[h % 2][:, cc, :],
                                  KV0SB[:, cc, 0:512], start=(cc == 0),
                                  stop=(cc == CC - 1))
                P.inc(li, "g_rm", 1)

            @block.scalar
            def _(sc, h=h, hb=hb, hc2=hc2):
                sc.wait_ge(P.sem("g_km"), NT2 * h + 1)
                sc.wait_ge(P.sem("skc0"), 16 * 8 * h)
                P.inc(sc.activation(KSTG[0][:], PKC[0][:], AF.Identity,
                                    bias=CKC[:, h:h + 1]), "g_ke", 1)
                sc.wait_ge(P.sem("g_rm"), NT2 * h + 1)
                sc.wait_ge(P.sem("skr0"), 16 * 8 * h)
                P.inc(sc.activation(KRSTG[0][0:RD, :], PKR[0][0:RD, :],
                                    AF.Identity,
                                    bias=CKR[hb:hb + RD, hc2:hc2 + 1]),
                      "g_re", 1)

            @block.tensor
            def _(t, h=h):
                t.wait_ge(P.sem("g_re"), NT2 * h + 1)
                if h >= 2:
                    t.wait_ge(P.sem("g_mx"), h - 1)
                t.matmul(PSS[h % 2][:], KSTG[0][:, 0:128], QSTG[:, 0:512],
                         start=True, stop=False)
                P.inc(t.matmul(PSS[h % 2][:], KRSTG[0][:, 0:128],
                               QRSTG[:, 0:512], start=False, stop=True),
                      "g_sm", 1)

            @block.vector
            def _(v, h=h):
                P.wait(v, "g_sm", h + 1)
                P.inc(v.reduce_max(MXALL[:, h:h + 1], PSS[h % 2][:],
                                   mybir.AxisListType.X), "g_mx", 1)

            @block.tensor
            def _(t, h=h):
                def vmm(kt):
                    t.wait_ge(P.sem("g_ve"),
                              NT2 * h + kt - 1 if kt >= 2 else 16 * h)
                    li = None
                    for ci in range(4):
                        for cc in range(CC):
                            li = t.matmul(
                                PV[kt % 2][:, ci * 128:(ci + 1) * 128],
                                KV0SB[:, cc, (kt * 4 + ci) * 128:(kt * 4 + ci + 1) * 128],
                                WVH[h % 2][:, cc, :],
                                start=(cc == 0), stop=(cc == CC - 1))
                    P.inc(li, "g_vm", 1)

                vmm(0)
                for kt in range(1, NT2):
                    if kt == 1:
                        t.wait_ge(P.sem("g_qe"), 4 * h + 2)
                    else:
                        t.wait_ge(P.sem("g_ke"), NT2 * h + kt - 1)
                    li = None
                    for cc in range(CC):
                        li = t.matmul(PKC[kt % 2][:], WKH[h % 2][:, cc, :],
                                      KV0SB[:, cc, kt * 512:(kt + 1) * 512],
                                      start=(cc == 0), stop=(cc == CC - 1))
                    P.inc(li, "g_km", 1)
                    if kt == 1:
                        t.wait_ge(P.sem("g_qe"), 4 * h + 4)
                    else:
                        t.wait_ge(P.sem("g_re"), NT2 * h + kt - 1)
                    li = None
                    for cc in range(CC):
                        li = t.matmul(PKR[kt % 2][:], WKRH[h % 2][:, cc, :],
                                      KV0SB[:, cc, kt * 512:(kt + 1) * 512],
                                      start=(cc == 0), stop=(cc == CC - 1))
                    P.inc(li, "g_rm", 1)
                    vmm(kt)
                P.inc(t.nop(), "g_hd", 1)

            @block.scalar
            def _(sc, h=h, hb=hb, hc2=hc2):
                def vev(kt):
                    sc.wait_ge(P.sem("g_vm"), NT2 * h + kt + 1)
                    sc.wait_ge(P.sem("sv" + str(kt % 2)),
                               16 * (8 * h + kt // 2))
                    P.inc(sc.activation(VSTG[kt % 2][:], PV[kt % 2][:],
                                        AF.Copy), "g_ve", 1)

                vev(0)
                for kt in range(1, NT2):
                    sc.wait_ge(P.sem("g_km"), NT2 * h + kt + 1)
                    sc.wait_ge(P.sem("skc" + str(kt % 2)),
                               16 * (8 * h + kt // 2))
                    if kt == 2:
                        sc.wait_ge(P.sem("g_sm"), h + 1)
                    P.inc(sc.activation(KSTG[kt % 2][:], PKC[kt % 2][:],
                                        AF.Identity, bias=CKC[:, h:h + 1]),
                          "g_ke", 1)
                    sc.wait_ge(P.sem("g_rm"), NT2 * h + kt + 1)
                    sc.wait_ge(P.sem("skr" + str(kt % 2)),
                               16 * (8 * h + kt // 2))
                    if kt == 2:
                        sc.wait_ge(P.sem("g_sm"), h + 1)
                    P.inc(sc.activation(KRSTG[kt % 2][0:RD, :],
                                        PKR[kt % 2][0:RD, :], AF.Identity,
                                        bias=CKR[hb:hb + RD, hc2:hc2 + 1]),
                          "g_re", 1)
                    vev(kt)

            @block.sync
            def _(s, h=h):
                def vst(kt):
                    s.wait_ge(P.sem("g_ve"), NT2 * h + kt + 1)
                    P.inc(s.dma_start(
                        out=v_scr.ap()[h, kt * 4:(kt + 1) * 4].rearrange(
                            "kc p d -> p kc d"),
                        in_=VSTG[kt % 2][:].rearrange("p (kc d) -> p kc d", kc=4)),
                        "sv" + str(kt % 2), 16)

                vst(0)
                s.wait_ge(P.sem("g_ke"), NT2 * h + 1)
                P.inc(s.dma_start(
                    out=kc_scr.ap()[h, :, 0:512],
                    in_=KSTG[0][:]), "skc0", 16)
                s.wait_ge(P.sem("g_re"), NT2 * h + 1)
                P.inc(s.dma_start(
                    out=kr_scr.ap()[h, :, 0:512],
                    in_=KRSTG[0][:]), "skr0", 16)
                for kt in range(1, NT2):
                    s.wait_ge(P.sem("g_ke"), NT2 * h + kt + 1)
                    P.inc(s.dma_start(
                        out=kc_scr.ap()[h, :, kt * 512:(kt + 1) * 512],
                        in_=KSTG[kt % 2][:]), "skc" + str(kt % 2), 16)
                    s.wait_ge(P.sem("g_re"), NT2 * h + kt + 1)
                    P.inc(s.dma_start(
                        out=kr_scr.ap()[h, :, kt * 512:(kt + 1) * 512],
                        in_=KRSTG[kt % 2][:]), "skr" + str(kt % 2), 16)
                    vst(kt)

        P.ser_begin()
        _gke_t = P.val("g_ke"); _gre_t = P.val("g_re"); _gve_t = P.val("g_ve")

        def _bh_first(e):
            e.wait_ge(P.sem("g_ke"), _gke_t)
            e.wait_ge(P.sem("g_re"), _gre_t)
            e.wait_ge(P.sem("g_ve"), _gve_t)
            return e.transpose(PKR[0][0:16, 0:128], MXALL[:], IDF[:])
        P.ser("tensor", _bh_first)
        P.ser("vector", lambda e: e.reduce_max(MXH[:, 0:1], PKR[0][0:16, 0:128],
                                               mybir.AxisListType.X))
        P.ser("vector", lambda e: e.tensor_scalar_mul(MXH[:, 1:2], MXH[:, 0:1],
                                                      -SCALE))
        P.ser("tensor", lambda e: e.transpose(PKC[0][0:1, 0:16], MXH[:, 1:2],
                                              IDF[0:16, 0:16]))
        P.ser("vector", lambda e: e.tensor_copy(MXB[:], PKC[0][0:1, 0:16]))
        P.ser("tensor", lambda e: e.matmul(PKC[1][:, 0:16], ONESM[:], MXB[:],
                                           start=True, stop=True))
        P.ser("vector", lambda e: e.tensor_copy(BHS[:], PKC[1][:, 0:16]))
        P.ser_emit(block, "bh", pred=("g_mx", NH))

        gen_waits = [(nm, P.val(nm)) for nm in
                     ("skc0", "skc1", "skr0", "skr1", "sv0", "sv1", "sq", "sqr")]
        bh_total = P.val("bh")
        gd.close()
        gat.close()
        if sub == "gen":
            tap_kc = nc.dram_tensor("tap_kc", [128, SK], BF16, kind="ExternalOutput").ap()
            tap_kr = nc.dram_tensor("tap_kr", [RD, SK], BF16, kind="ExternalOutput").ap()
            tap_v = nc.dram_tensor("tap_v", [KC, 128, HD], BF16, kind="ExternalOutput").ap()
            tap_qc = nc.dram_tensor("tap_qc", [128, T], BF16, kind="ExternalOutput").ap()
            tap_qr = nc.dram_tensor("tap_qr", [RD, T], BF16, kind="ExternalOutput").ap()
            tap_ag = nc.dram_tensor("tap_ag", [NCORES * KV, T], BF16, kind="ExternalOutput").ap()
            tap_bh = nc.dram_tensor("tap_bh", [128, NH], F32, kind="ExternalOutput").ap()
            tap_ck = nc.dram_tensor("tap_ck", [128, FC], F32, kind="ExternalOutput").ap()
            tap_co = nc.dram_tensor("tap_co", [128, FC], F32, kind="ExternalOutput").ap()
            HDBG = 3

            @block.sync
            def _(s):
                for nm_, vv_ in [("skc0", None), ("skc1", None), ("skr0", None),
                                 ("skr1", None), ("sv0", None), ("sv1", None),
                                 ("sq", None), ("sqr", None)]:
                    P.wait(s, nm_)
                P.wait(s, "bh", P.val("bh"))
                P.inc(s.dma_start(out=tap_kc, in_=kc_scr.ap()[HDBG]), "dbg", 16)
                P.inc(s.dma_start(out=tap_kr, in_=kr_scr.ap()[HDBG]), "dbg", 16)
                P.inc(s.dma_start(out=tap_v, in_=v_scr.ap()[HDBG]), "dbg", 16)
                P.inc(s.dma_start(out=tap_qc, in_=qcs_scr.ap()[HDBG]), "dbg", 16)
                P.inc(s.dma_start(out=tap_qr, in_=qrs_scr.ap()[HDBG]), "dbg", 16)
                P.inc(s.dma_start(out=tap_ag, in_=ag_out.ap()), "dbg", 16)
                P.wait(s, "dbg")

            @block.gpsimd
            def _(gp):
                P.wait(gp, "bh", P.val("bh"))
                P.inc(gp.dma_start(out=tap_bh, in_=BHS[:]), "dbg2", 16)
                P.inc(gp.dma_start(out=tap_ck, in_=CKC[:]), "dbg2", 16)
                P.inc(gp.dma_start(out=tap_co, in_=COUT[:]), "dbg2", 16)
                P.wait(gp, "dbg2")
            return ("g_hd", P.val("g_hd"))

        ge = ExitStack()
        esb = lambda n_, sh, dt: ge.enter_context(nc.sbuf_tensor(n_, sh, dt))
        eps_ = lambda n_, sh, dt: ge.enter_context(nc.psum_tensor(n_, sh, dt))
        CTX = esb("CTX", [128, NH, T], BF16)
        KCB = [esb(f"KCB{i}", [128, SK // 2], BF16) for i in range(2)]
        KRB = [esb(f"KRB{i}", [128, SK // 2], BF16) for i in range(2)]
        VB = [esb(f"VB{i}", [128, 32, 128], BF16) for i in range(2)]
        QCB = [esb(f"QCB{i}", [128, T], BF16) for i in range(2)]
        QRB = [esb(f"QRB{i}", [RD, T], BF16) for i in range(2)]
        QRB2 = [esb(f"QRB2{i}", [128, T], BF16) for i in range(2)]
        EG = [esb(f"EG{i}", [128, 1024], BF16) for i in range(2)]
        RZROW = esb("RZROW", [1, 1024], F32)
        RZB = esb("RZB", [1, 1024], BF16)
        RZS = esb("RZS", [128, 1024], BF16)
        gep = ExitStack()
        eps2 = lambda n_, sh, dt: gep.enter_context(nc.psum_tensor(n_, sh, dt))
        SG = [eps2(f"SG{i}", [128, 1024], F32) for i in range(2)]
        CP = [eps2(f"CP{i}", [128, 512], F32) for i in range(2)]
        ZP = [eps2(f"ZP{i}", [1, 512], F32) for i in range(2)]

        @block.sync
        def _(s):
            for nm, vv in gen_waits:
                s.wait_ge(P.sem(nm), vv)
            for m in range(2 * NH):
                h, half = divmod(m, 2)
                if m >= 2:
                    P.wait(s, "e_cd", m - 1)
                P.inc(s.dma_start(out=KCB[m % 2][:],
                                  in_=kc_scr.ap()[h, :, half * 4096:(half + 1) * 4096]),
                      "e_ld" + str(m % 2), 16)
                P.inc(s.dma_start(out=KRB[m % 2][0:RD, :],
                                  in_=kr_scr.ap()[h, :, half * 4096:(half + 1) * 4096]),
                      "e_ld" + str(m % 2), 16)
                P.inc(s.dma_start(out=KRB[m % 2][RD:128, :],
                                  in_=kr_scr.ap()[h, :, half * 4096:(half + 1) * 4096]),
                      "e_ld" + str(m % 2), 16)
                P.inc(s.dma_start(out=VB[m % 2][:],
                                  in_=v_scr.ap()[h, half * 32:(half + 1) * 32].rearrange(
                                      "kc p d -> p kc d")),
                      "e_ld" + str(m % 2), 16)
                if half == 0:
                    P.inc(s.dma_start(out=QCB[h % 2][:], in_=qcs_scr.ap()[h]),
                          "e_ld" + str(m % 2), 16)
                    P.inc(s.dma_start(out=QRB[h % 2][:], in_=qrs_scr.ap()[h]),
                          "e_ld" + str(m % 2), 16)
                    P.inc(s.dma_start(out=QRB2[h % 2][64:128, :], in_=qrs_scr.ap()[h]),
                          "e_ld" + str(m % 2), 16)

        for h in range(NH):
            @block.tensor
            def _(t, h=h):
                if h == 0:
                    P.wait(t, "bh", bh_total)
                for half in range(2):
                    m = 2 * h + half
                    t.wait_ge(P.sem("e_ld" + str(m % 2)),
                              (112 if m % 2 == 0 else 64) * (h + 1))
                    for qt in range(2):
                        if half == 0:
                            t.wait_ge(P.sem("en"), 8 * h)
                        pend = []
                        for g in range(16):
                            egi = (m * 2 + qt) * 16 + g
                            if egi >= 2:
                                t.wait_ge(P.sem("e_ex"), egi - 1)
                            kc0, kc1 = 2 * g, 2 * g + 1
                            t.matmul(SG[egi % 2][:, 0:512],
                                     KCB[m % 2][:, kc0 * 128:(kc0 + 1) * 128],
                                     QCB[h % 2][:, qt * 512:(qt + 1) * 512],
                                     start=True, stop=False)
                            t.matmul(SG[egi % 2][:, 512:1024],
                                     KCB[m % 2][:, kc1 * 128:(kc1 + 1) * 128],
                                     QCB[h % 2][:, qt * 512:(qt + 1) * 512],
                                     start=True, stop=False)
                            t.matmul(SG[egi % 2][:, 0:512],
                                     KRB[m % 2][0:RD, kc0 * 128:(kc0 + 1) * 128],
                                     QRB[h % 2][:, qt * 512:(qt + 1) * 512],
                                     start=False, stop=True)
                            li = t.matmul(SG[egi % 2][:, 512:1024],
                                          KRB[m % 2][RD:128, kc1 * 128:(kc1 + 1) * 128],
                                          QRB2[h % 2][64:128, qt * 512:(qt + 1) * 512],
                                          start=False, stop=True)
                            P.inc(li, "e_sm", 1)
                            pend.append((egi, g))
                            if len(pend) >= 2:
                                pegi, pg = pend.pop(0)
                                t.wait_ge(P.sem("e_ex"), pegi + 1)
                                li2 = None
                                for i in range(2):
                                    kc_ = 2 * pg + i
                                    st = (half == 0 and pg == 0 and i == 0)
                                    sp = (half == 1 and pg == 15 and i == 1)
                                    li2 = t.matmul(CP[qt][:],
                                                   VB[m % 2][:, kc_, :],
                                                   EG[pegi % 2][:, i * 512:(i + 1) * 512],
                                                   start=st, stop=sp)
                                    li2 = t.matmul(ZP[qt][:], ONES[:],
                                                   EG[pegi % 2][:, i * 512:(i + 1) * 512],
                                                   start=st, stop=sp)
                                P.inc(li2, "e_cz", 1)
                        pegi, pg = pend.pop(0)
                        t.wait_ge(P.sem("e_ex"), pegi + 1)
                        li2 = None
                        for i in range(2):
                            kc_ = 2 * pg + i
                            st = (half == 0 and pg == 0 and i == 0)
                            sp = (half == 1 and pg == 15 and i == 1)
                            li2 = t.matmul(CP[qt][:], VB[m % 2][:, kc_, :],
                                           EG[pegi % 2][:, i * 512:(i + 1) * 512],
                                           start=st, stop=sp)
                            li2 = t.matmul(ZP[qt][:], ONES[:],
                                           EG[pegi % 2][:, i * 512:(i + 1) * 512],
                                           start=st, stop=sp)
                        P.inc(li2, "e_cz", 1)
                    P.inc(t.nop(), "e_cd", 1)

            @block.scalar
            def _(sc, h=h):
                P.wait(sc, "bh", bh_total)
                for half in range(2):
                    m = 2 * h + half
                    for qt in range(2):
                        for g in range(16):
                            egi = (m * 2 + qt) * 16 + g
                            sc.wait_ge(P.sem("e_sm"), egi + 1)
                            if egi >= 2:
                                sc.wait_ge(P.sem("e_cz"), egi - 1)
                            P.inc(sc.activation(EG[egi % 2][:], SG[egi % 2][:],
                                                AF.Exp, bias=BHS[:, h:h + 1],
                                                scale=SCALE), "e_ex", 1)

            m1 = 2 * h + 1
            ecz_after = m1 * 32 + 32
            P.ser_begin()
            P.ser("vector", lambda e: e.reciprocal(RZROW[:, 0:512], ZP[0][:]))
            P.ser("vector", lambda e: e.reciprocal(RZROW[:, 512:1024], ZP[1][:]))
            P.ser("vector", lambda e: e.tensor_copy(RZB[:], RZROW[:]))
            P.ser("tensor", lambda e: e.matmul(SG[0][:, 0:512], ONESM[:],
                                               RZB[:, 0:512], start=True,
                                               stop=True))
            P.ser("tensor", lambda e: e.matmul(SG[0][:, 512:1024], ONESM[:],
                                               RZB[:, 512:1024], start=True,
                                               stop=True))
            P.ser("scalar", lambda e: e.activation(RZS[:], SG[0][:, 0:1024],
                                                   AF.Copy))
            P.ser("vector", lambda e, h=h: e.tensor_tensor(
                CTX[:, h, 0:512], CP[0][:], RZS[:, 0:512], OP.mult))
            P.ser("vector", lambda e, h=h: e.tensor_tensor(
                CTX[:, h, 512:1024], CP[1][:], RZS[:, 512:1024], OP.mult))
            P.ser_emit(block, "en", pred=("e_cz", ecz_after))

        en_total = P.val("en")
        gep.close()

        WOE = [ge.enter_context(nc.sbuf_tensor(f"WOE{i}", [128, FC, 128], BF16))
               for i in range(2)]
        POF = [ge.enter_context(nc.psum_tensor(f"POF{i}", [128, 512], F32))
               for i in range(2)]

        @block.gpsimd
        def _(gp):
            P.wait(gp, "wo_s", 16)
            P.wait(gp, "en", en_total)
            for ec in range(FC):
                if ec >= 2:
                    P.wait(gp, "f_pm", 2 * (ec - 1))
                P.inc(gp.dma_start(
                    out=WOE[ec % 2][:],
                    in_=wt_o[:, ec * 128:(ec + 1) * 128].rearrange(
                        "(hc p) e -> p hc e", p=128)), "f_wl" + str(ec % 2), 16)

        @block.tensor
        def _(t):
            P.wait(t, "en", en_total)
            for ec in range(FC):
                t.wait_ge(P.sem("f_wl" + str(ec % 2)), 16 * (ec // 2 + 1))
                for tt in range(2):
                    n = ec * 2 + tt
                    if n >= 2:
                        t.wait_ge(P.sem("f_rs"), n - 1)
                    li = None
                    for hc in range(FC):
                        li = t.matmul(POF[n % 2][:], WOE[ec % 2][:, hc, :],
                                      CTX[:, hc, tt * 512:(tt + 1) * 512],
                                      start=(hc == 0), stop=(hc == FC - 1))
                    P.inc(li, "f_pm", 1)

        @block.vector
        def _(v):
            P.wait(v, "wo_f", 1)
            for ec in range(FC):
                for tt in range(2):
                    n = ec * 2 + tt
                    v.wait_ge(P.sem("f_pm"), n + 1)
                    P.inc(v.scalar_tensor_tensor(
                        XT[:, ec, tt * 512:(tt + 1) * 512], POF[n % 2][:],
                        COUT[:, ec:ec + 1],
                        XT[:, ec, tt * 512:(tt + 1) * 512], OP.add, OP.add),
                        "f_rs", 1)

        if sub == "ctx":
            tap_ctx = nc.dram_tensor("tap_ctx", [128, NH, T], BF16,
                                     kind="ExternalOutput").ap()
            HDBG = 0
            tap_kc = nc.dram_tensor("tap_kc", [128, SK], BF16, kind="ExternalOutput").ap()
            tap_kr = nc.dram_tensor("tap_kr", [RD, SK], BF16, kind="ExternalOutput").ap()
            tap_v = nc.dram_tensor("tap_v", [KC, 128, HD], BF16, kind="ExternalOutput").ap()
            tap_qc = nc.dram_tensor("tap_qc", [128, T], BF16, kind="ExternalOutput").ap()
            tap_qr = nc.dram_tensor("tap_qr", [RD, T], BF16, kind="ExternalOutput").ap()
            tap_bh = nc.dram_tensor("tap_bh", [128, NH], F32, kind="ExternalOutput").ap()
            tap_ag = nc.dram_tensor("tap_ag", [NCORES * KV, T], BF16, kind="ExternalOutput").ap()
            tap_agi = nc.dram_tensor("tap_agi", [KV, T], BF16, kind="ExternalOutput").ap()
            tap_co = nc.dram_tensor("tap_co", [128, FC], F32, kind="ExternalOutput").ap()
            tap_cv = nc.dram_tensor("tap_cv", [128, D], F32, kind="ExternalOutput").ap()

            @block.sync
            def _(s):
                P.wait(s, "en", P.val("en"))
                P.inc(s.dma_start(out=tap_ctx, in_=CTX[:]), "dbg3", 16)
                P.inc(s.dma_start(out=tap_kc, in_=kc_scr.ap()[HDBG]), "dbg3", 16)
                P.inc(s.dma_start(out=tap_kr, in_=kr_scr.ap()[HDBG]), "dbg3", 16)
                P.inc(s.dma_start(out=tap_v, in_=v_scr.ap()[HDBG]), "dbg3", 16)
                P.inc(s.dma_start(out=tap_qc, in_=qcs_scr.ap()[HDBG]), "dbg3", 16)
                P.inc(s.dma_start(out=tap_qr, in_=qrs_scr.ap()[HDBG]), "dbg3", 16)
                P.inc(s.dma_start(out=tap_ag, in_=ag_out.ap()), "dbg3", 16)
                P.inc(s.dma_start(out=tap_agi, in_=ag_in.ap()), "dbg3", 16)
                P.wait(s, "dbg3")

            @block.gpsimd
            def _(gp):
                P.wait(gp, "en", P.val("en"))
                P.inc(gp.dma_start(out=tap_bh, in_=BHS[:]), "dbg4", 16)
                P.inc(gp.dma_start(out=tap_co, in_=COUT[:]), "dbg4", 16)
                P.wait(gp, "dbg4")
            ge.close()
            return ("en", P.val("en"))
        done = ("f_rs", P.val("f_rs"))
        ge.close()
        return done

    ln1_done = emit_ln("l1", XLN8, p0_done)
    ffn1_done = emit_ffn("f1", ln1_done)

    if stop_after == "ffn1":
        final_pred = ffn1_done
    else:
        sub = stop_after if stop_after in ("ln2", "bphase", "prep", "gen", "ctx") else None
        attn_done = emit_attention(ffn1_done, sub=sub)
        if stop_after in ("attn", "ln2", "bphase", "prep", "gen", "ctx"):
            final_pred = attn_done
        else:
            ln3_done = emit_ln("l3", XLN8, attn_done)
            final_pred = emit_ffn("f3", ln3_done)

    fin = ExitStack()
    OSTG = [fin.enter_context(nc.sbuf_tensor(f"OSTG{i}", [128, D], F32))
            for i in range(2)]
    PTO = [fin.enter_context(nc.psum_tensor(f"PTO{i}", [128, 128], F32))
           for i in range(2)]

    @block.tensor
    def _(t):
        P.wait(t, final_pred[0], final_pred[1])
        for tc in range(8):
            for fc in range(FC):
                n = tc * FC + fc
                if n >= 2:
                    t.wait_ge(P.sem("o_ev"), n - 1)
                P.inc(t.transpose(PTO[n % 2][:],
                                  XT[:, fc, tc * 128:(tc + 1) * 128], IDF[:]),
                      "o_tp", 1)

    @block.vector
    def _(v):
        for tc in range(8):
            for fc in range(FC):
                n = tc * FC + fc
                v.wait_ge(P.sem("o_tp"), n + 1)
                if fc == 0 and tc >= 2:
                    P.wait(v, f"o_st{tc % 2}", 16 * (tc // 2))
                P.inc(v.tensor_copy(OSTG[tc % 2][:, fc * 128:(fc + 1) * 128],
                                    PTO[n % 2][:]), "o_ev", 1)

    @block.sync
    def _(s):
        for tc in range(8):
            s.wait_ge(P.sem("o_ev"), (tc + 1) * FC)
            P.inc(s.dma_start(out=out[tc * 128:(tc + 1) * 128, :],
                              in_=OSTG[tc % 2][:]), f"o_st{tc % 2}", 16)
        s.wait_ge(P.sem("o_st0"), P.val("o_st0"))
        s.wait_ge(P.sem("o_st1"), P.val("o_st1"))

    fin.close()
    blk_cm.__exit__(None, None, None)
    ctx.close()
    return nc, {}


def make_inmaps(inp):
    NHRD = NH * RD
    inv_freq = 1.0 / 10000.0 ** (np.arange(0, RD, 2, dtype=np.float64) / RD)
    Rfull = np.zeros((NHRD, NHRD), dtype=np.float32)
    for h in range(NH):
        ang = h * inv_freq
        c, s = np.cos(ang), np.sin(ang)
        for m in range(RD // 2):
            b0 = h * RD + 2 * m
            Rfull[b0, b0] = c[m]; Rfull[b0, b0 + 1] = -s[m]
            Rfull[b0 + 1, b0] = s[m]; Rfull[b0 + 1, b0 + 1] = c[m]
    RT_ = Rfull.T
    RfT = np.zeros((8, 128, 128), dtype=ml_dtypes.bfloat16)
    for i in range(8):
        RfT[i] = RT_[i * 128:(i + 1) * 128, i * 128:(i + 1) * 128].astype(
            ml_dtypes.bfloat16)

    com = dict(
        w_kvd=inp["L0_attn_kv_down_w"],
        b_kvd_c=inp["L0_attn_kv_down_b"].reshape(CC, 128),
        w_ku=inp["L0_attn_key_up_w"],
        b_ku=inp["L0_attn_key_up_b"].reshape(FC, 128),
        w_vu=inp["L0_attn_value_up_w"],
        b_vu=inp["L0_attn_value_up_b"].reshape(1, D),
        w_kr=inp["L0_attn_key_rope_w"],
        b_kr=inp["L0_attn_key_rope_b"].reshape(8, 128),
        w_qd=inp["L0_attn_query_down_w"],
        b_qd=inp["L0_attn_query_down_b"].reshape(QCC, 128),
        w_qu=inp["L0_attn_query_up_w"],
        b_qu=inp["L0_attn_query_up_b"].reshape(FC, 128),
        w_qr=inp["L0_attn_query_rope_w"],
        b_qr=inp["L0_attn_query_rope_b"].reshape(8, 128),
        w_out=inp["L0_attn_out_w"],
        b_out=inp["L0_attn_out_b"].reshape(FC, 128),
        w_g0=np.ascontiguousarray(inp["L0_ffn_w_gating"][0]),
        w_g1=np.ascontiguousarray(inp["L0_ffn_w_gating"][1]),
        w_l=inp["L0_ffn_w_linear"],
        identf=np.eye(128, dtype=np.float32),
        identb=np.eye(128, dtype=ml_dtypes.bfloat16),
        rt=RfT,
    )
    x = np.asarray(inp["L0_attn_x"][0])
    maps = []
    for c in range(NCORES):
        m = dict(com)
        m["x"] = np.ascontiguousarray(x[c * T:(c + 1) * T, :])
        maps.append(m)
    return maps


def kernel(**inputs):
    from concourse.bass_utils import run_bass_kernel_spmd
    nc, _ = build()
    maps = make_inmaps(inputs)
    res = run_bass_kernel_spmd(nc, maps, core_ids=list(range(NCORES)))
    out = np.concatenate([res.results[c]["out"] for c in range(NCORES)], axis=0)
    return out.reshape(1, NCORES * T, D)


# version 03
# speedup vs baseline: 1.0578x; 1.0022x over previous
import numpy as np
import ml_dtypes
import concourse.bass as bass
import concourse.mybir as mybir
from contextlib import ExitStack

F32 = mybir.dt.float32
BF16 = mybir.dt.bfloat16
FP8 = mybir.dt.float8e4
AF = mybir.ActivationFunctionType
OP = mybir.AluOpType

NCORES = 8
T = 1024
D = 2048
FC = 16
J = 8192
JC = 64
NH, HD, RD = 16, 128, 64
KV, CC = 512, 4
QC, QCC = 1024, 8
SK, KC = 8192, 64
SCALE = float(1.0 / np.sqrt(HD + RD))
C44715 = float(np.float32(ml_dtypes.bfloat16(0.044715)))
SQ2PI = float(np.float32(np.sqrt(2 / np.pi).astype(ml_dtypes.bfloat16)))
EPS = 1e-5


class Prog:
    def __init__(self, nc, ctx):
        self.nc, self.ctx = nc, ctx
        self.sems, self.cnt = {}, {}

    def sem(self, name):
        if name not in self.sems:
            self.sems[name] = self.ctx.enter_context(self.nc.semaphore(name))
            self.cnt[name] = 0
        return self.sems[name]

    def inc(self, instr, name, n=1):
        instr.then_inc(self.sem(name), n)
        self.cnt[name] += n
        return instr

    def val(self, name):
        self.sem(name)
        return self.cnt[name]

    def wait(self, eng, name, val=None):
        eng.wait_ge(self.sem(name), self.cnt[name] if val is None else val)

    def ser_begin(self):
        self._ser = []

    def ser(self, eng_name, fn, inc=1):
        self._ser.append((eng_name, fn, inc))

    def ser_emit(self, block, sem_name, pred=None):
        bodies = {"sync": block.sync, "tensor": block.tensor,
                  "vector": block.vector, "scalar": block.scalar,
                  "gpsimd": block.gpsimd}
        ops = self._ser
        sm = self.sem(sem_name)
        starts = []
        v = self.cnt[sem_name]
        for (_, _, inc) in ops:
            starts.append(v)
            v += inc
        self.cnt[sem_name] = v
        for ename in ("sync", "tensor", "vector", "scalar", "gpsimd"):
            todo = [(i, f, inc) for i, (en, f, inc) in enumerate(ops)
                    if en == ename]
            if not todo:
                continue

            def mk(todo_list):
                def _body(eng):
                    if pred is not None:
                        eng.wait_ge(self.sem(pred[0]), pred[1])
                    for i, f, inc in todo_list:
                        eng.wait_ge(sm, starts[i])
                        f(eng).then_inc(sm, inc)
                return _body
            bodies[ename](mk(todo))
        return (sem_name, v)


def build(stop_after=None):
    nc = bass.Bass("TRN2", target_bir_lowering=False)
    ctx = ExitStack()
    P = Prog(nc, ctx)

    def dri(name, shape, dtype=F32):
        return nc.dram_tensor(name, shape, dtype, kind="ExternalInput").ap()

    x_in = dri("x", [T, D])
    w_kvd = dri("w_kvd", [KV, D])
    b_kvd_c = dri("b_kvd_c", [CC, 128])
    w_ku = dri("w_ku", [D, KV])
    b_ku = dri("b_ku", [FC, 128])
    w_vu = dri("w_vu", [D, KV])
    b_vu = dri("b_vu", [1, D])
    w_kr = dri("w_kr", [QC, KV])
    b_kr = dri("b_kr", [8, 128])
    w_qd = dri("w_qd", [QC, D])
    b_qd = dri("b_qd", [QCC, 128])
    w_qu = dri("w_qu", [D, QC])
    b_qu = dri("b_qu", [FC, 128])
    w_qr = dri("w_qr", [QC, QC])
    b_qr = dri("b_qr", [8, 128])
    w_out = dri("w_out", [D, D])
    b_out = dri("b_out", [FC, 128])
    w_g0 = dri("w_g0", [D, J])
    w_g1 = dri("w_g1", [D, J])
    w_l = dri("w_l", [J, D])
    identf = dri("identf", [128, 128])
    identb = dri("identb", [128, 128], BF16)
    rt = dri("rt", [8, 128, 128], BF16)

    out = nc.dram_tensor("out", [T, D], F32, kind="ExternalOutput").ap()

    ag_in = nc.dram_tensor("ag_in", [KV, T], BF16)
    ag_out = nc.dram_tensor("ag_out", [NCORES * KV, T], BF16)
    kc_scr = nc.dram_tensor("kc_scr", [NH, 128, SK], BF16)
    kr_scr = nc.dram_tensor("kr_scr", [NH, RD, SK], BF16)
    v_scr = nc.dram_tensor("v_scr", [NH, KC, 128, HD], BF16)
    qcs_scr = nc.dram_tensor("qcs_scr", [NH, 128, T], BF16)
    qrs_scr = nc.dram_tensor("qrs_scr", [NH, RD, T], BF16)
    wt_k = nc.dram_tensor("wt_k", [KV, D], BF16)
    wt_v = nc.dram_tensor("wt_v", [KV, D], BF16)
    wt_kr = nc.dram_tensor("wt_kr", [KV, QC], BF16)
    wt_q = nc.dram_tensor("wt_q", [QC, D], BF16)
    wt_qr = nc.dram_tensor("wt_qr", [QC, QC], BF16)
    wt_o = nc.dram_tensor("wt_o", [D, D], BF16)

    def sbp(n_, sh, dt):
        return ctx.enter_context(nc.sbuf_tensor(n_, sh, dt))

    XT = sbp("XT", [128, FC, T], F32)
    IDF = sbp("IDF", [128, 128], F32)
    IDB = sbp("IDB", [128, 128], BF16)
    RTS = sbp("RTS", [128, 8, 128], BF16)
    ONES = sbp("ONES", [128, 1], BF16)
    ONESM = sbp("ONESM", [1, 128], BF16)
    XLN8 = sbp("XLN8", [128, FC, T], FP8)
    QDB = sbp("QDB", [128, QCC], F32)
    QUB = sbp("QUB", [128, FC], F32)
    BQR = sbp("BQR", [128, 8], F32)
    CKC = sbp("CKC", [128, FC], F32)
    CKR = sbp("CKR", [128, 8], F32)
    COUT = sbp("COUT", [128, FC], F32)
    BHS = sbp("BHS", [128, NH], F32)


    blk_cm = nc.Block()
    block = blk_cm.__enter__()

    p0 = ExitStack()
    STG = [p0.enter_context(nc.sbuf_tensor(f"STG{i}", [128, D], F32)) for i in range(2)]
    PT = [p0.enter_context(nc.psum_tensor(f"PT{i}", [128, 128], F32)) for i in range(2)]

    @block.sync
    def _(s):
        P.inc(s.dma_start(out=IDF[:], in_=identf), "c_ld", 16)
        P.inc(s.dma_start(out=IDB[:], in_=identb), "c_ld", 16)
        P.inc(s.dma_start(out=RTS[:], in_=rt.rearrange("c p n -> p c n")), "c_ld", 16)
        for tc in range(8):
            if tc >= 2:
                P.wait(s, "x_tp", (tc - 1) * FC)
            P.inc(s.dma_start(out=STG[tc % 2][:],
                              in_=x_in[tc * 128:(tc + 1) * 128, :]),
                  f"x_ld{tc % 2}", 16)

    @block.vector
    def _(v):
        v.memset(ONES[:], 1.0)
        P.inc(v.memset(ONESM[:], 1.0), "c_init", 1)

    @block.tensor
    def _(t):
        t.wait_ge(P.sem("c_ld"), 48)
        for tc in range(8):
            t.wait_ge(P.sem(f"x_ld{tc % 2}"), 16 * (tc // 2 + 1))
            for fc in range(FC):
                n = tc * FC + fc
                if n >= 2:
                    t.wait_ge(P.sem("x_ev"), n - 1)
                P.inc(t.transpose(PT[n % 2][:],
                                  STG[tc % 2][:, fc * 128:(fc + 1) * 128],
                                  IDF[:]), "x_tp", 1)

    @block.vector
    def _(v):
        for tc in range(8):
            for fc in range(FC):
                n = tc * FC + fc
                v.wait_ge(P.sem("x_tp"), n + 1)
                P.inc(v.tensor_copy(XT[:, fc, tc * 128:(tc + 1) * 128],
                                    PT[n % 2][:]), "x_ev", 1)

    p0_done = ("x_ev", P.val("x_ev"))
    p0.close()

    def emit_ln(tag, OUTT, pred):
        g = ExitStack()
        nsb = lambda n_, sh, dt: g.enter_context(nc.sbuf_tensor(tag + n_, sh, dt))
        nps = lambda n_, sh, dt: g.enter_context(nc.psum_tensor(tag + n_, sh, dt))
        NB = 2
        SQ = nsb("SQ", [128, NB, T], F32)
        RA = nsb("RA", [128, NB, T], F32)
        HB = nsb("HB", [128, NB, T], BF16)
        LB = nsb("LB", [128, NB, T], BF16)
        L2B = nsb("L2B", [128, NB, T], BF16)
        ROWS = nsb("ROWS", [1, 6, T], F32)
        RWB = nsb("RWB", [1, 6, T], BF16)
        MRS = nsb("MRS", [128, T], F32)
        RRS = nsb("RRS", [128, T], F32)
        XMS = nsb("XMS", [128, T], F32)
        PSX = nps("PSX", [1, T], F32)
        PSXX = nps("PSXX", [1, T], F32)
        PM = nps("PM", [128, T], F32)
        PR = nps("PR", [128, T], F32)
        NBAT = FC // NB

        def row(i):
            return ROWS[:, i, :]

        sS, sM, sQ_, sF, sR, sO = ("ln_sp", "ln_mm", "ln_sq", "ln_fin",
                                   "ln_rep", "ln_out")
        bS, bM, bQ, bF, bR, bO = (P.val(sS), P.val(sM), P.val(sQ_),
                                  P.val(sF), P.val(sR), P.val(sO))

        @block.scalar
        def _(sc):
            P.wait(sc, pred[0], pred[1])
            for b in range(NBAT):
                if b > 0:
                    P.wait(sc, sS, bS + NBAT + b)
                P.inc(sc.activation(SQ[:], XT[:, NB * b:NB * (b + 1), :],
                                    AF.Square), sQ_, 1)

        @block.vector
        def _(v):
            P.wait(v, pred[0], pred[1])
            for b in range(2 * NBAT):
                if b < NBAT:
                    srcx = XT[:, NB * b:NB * (b + 1), :]
                else:
                    P.wait(v, sQ_, bQ + b - NBAT + 1)
                    srcx = SQ[:]
                if b >= 1:
                    P.wait(v, sM, bM + b)
                v.tensor_copy(HB[:], srcx)
                v.tensor_tensor(RA[:], srcx, HB[:], OP.subtract)
                v.tensor_copy(LB[:], RA[:])
                v.tensor_tensor(RA[:], RA[:], LB[:], OP.subtract)
                P.inc(v.tensor_copy(L2B[:], RA[:]), sS, 1)

        @block.tensor
        def _(t):
            P.wait(t, "c_init", 1)
            for b in range(2 * NBAT):
                ps = PSX if b < NBAT else PSXX
                t.wait_ge(P.sem(sS), bS + b + 1)
                li = None
                for buf in (HB, LB, L2B):
                    for fi in range(NB):
                        first = (b % NBAT == 0 and buf is HB and fi == 0)
                        last = (b % NBAT == NBAT - 1 and buf is L2B
                                and fi == NB - 1)
                        for hh in range(2):
                            li = t.matmul(ps[:, hh * 512:(hh + 1) * 512],
                                          ONES[:],
                                          buf[:, fi, hh * 512:(hh + 1) * 512],
                                          start=first, stop=last)
                P.inc(li, sM, 1)

        @block.scalar
        def _(sc):
            P.wait(sc, sM, bM + NBAT)
            P.inc(sc.activation(row(0), PSX[:], AF.Copy, scale=1.0 / D), sF, 1)
            P.wait(sc, sM, bM + 2 * NBAT)
            P.inc(sc.activation(row(1), PSXX[:], AF.Copy, scale=1.0 / D), sF, 1)

        @block.vector
        def _(v):
            P.wait(v, sF, bF + 2)
            v.tensor_tensor(row(2), row(0), row(0), OP.mult)
            v.tensor_tensor(row(1), row(1), row(2), OP.subtract)
            P.inc(v.tensor_scalar_add(row(1), row(1), EPS), sF, 1)

        @block.scalar
        def _(sc):
            P.wait(sc, sF, bF + 3)
            P.inc(sc.activation(row(3), row(1), AF.Sqrt), sF, 1)

        @block.vector
        def _(v):
            P.wait(v, sF, bF + 4)
            v.reciprocal(row(4), row(3))
            v.tensor_tensor(row(2), row(1), row(4), OP.mult)
            v.tensor_tensor(row(2), row(3), row(2), OP.add)
            v.tensor_scalar_mul(row(2), row(2), 0.5)
            v.reciprocal(row(4), row(2))
            v.tensor_tensor(row(3), row(2), row(4), OP.mult)
            v.tensor_scalar(row(3), row(3), 2.0, -1.0, OP.subtract, OP.mult)
            v.tensor_tensor(row(5), row(4), row(3), OP.mult)
            li = None
            for i, ridx in ((0, 0), (3, 5)):
                srcr = row(ridx)
                v.tensor_copy(RWB[:, i, :], srcr)
                v.tensor_tensor(row(3), srcr, RWB[:, i, :], OP.subtract)
                v.tensor_copy(RWB[:, i + 1, :], row(3))
                v.tensor_tensor(row(3), row(3), RWB[:, i + 1, :], OP.subtract)
                li = v.tensor_copy(RWB[:, i + 2, :], row(3))
            P.inc(li, sF, 1)

        @block.tensor
        def _(t):
            P.wait(t, sF, bF + 5)
            for ps, base in ((PM, 0), (PR, 3)):
                li = None
                for i in range(3):
                    for hh in range(2):
                        li = t.matmul(ps[:, hh * 512:(hh + 1) * 512], ONESM[:],
                                      RWB[:, base + i, hh * 512:(hh + 1) * 512],
                                      start=(i == 0), stop=(i == 2))
                P.inc(li, sR, 1)

        @block.vector
        def _(v):
            P.wait(v, sR, bR + 2)
            v.tensor_copy(MRS[:], PM[:])
            v.tensor_copy(RRS[:], PR[:])
            for fc in range(FC):
                v.tensor_tensor(XMS[:], XT[:, fc, :], MRS[:], OP.subtract)
                P.inc(v.tensor_tensor(OUTT[:, fc, :], XMS[:], RRS[:], OP.mult),
                      sO, 1)

        done = (sO, P.val(sO))
        g.close()
        return done

    def emit_ffn(tag, pred, pred_ld=None, wbufs=None):
        if pred_ld is None:
            pred_ld = pred
        g = ExitStack()
        nsb = lambda n_, sh, dt: g.enter_context(nc.sbuf_tensor(tag + n_, sh, dt))
        nps = lambda n_, sh, dt: g.enter_context(nc.psum_tensor(tag + n_, sh, dt))
        if wbufs is not None:
            W0B, W1B = wbufs
        else:
            W0B = [nsb(f"W0B{i}", [128, FC, 128], FP8) for i in range(2)]
            W1B = [nsb(f"W1B{i}", [128, FC, 128], FP8) for i in range(2)]
        GATE = [nsb(f"GATE{i}", [128, 4, 512], BF16) for i in range(2)]
        FF1B = [nsb(f"FF1B{i}", [128, 4, 512], BF16) for i in range(2)]
        TM1 = nsb("TM1", [128, 4, 512], BF16)
        TM2 = nsb("TM2", [128, 4, 512], BF16)
        ACT8 = nsb("ACT8", [128, JC, T], FP8)
        WLE = [nsb(f"WLE{i}", [128, JC, 128], FP8) for i in range(2)]
        OBF = [nsb(f"OBF{i}", [128, 512], BF16) for i in range(2)]
        PG = [nps(f"PG{i}", [128, 512], F32) for i in range(2)]
        PF = [nps(f"PF{i}", [128, 512], F32) for i in range(2)]
        PO = [nps(f"PO{i}", [128, 512], F32) for i in range(3)]

        sWL, sMM, sEV, sGD, sTH, sTI, sWE, sPM, sOB, sRS = (
            "ff_wl", "ff_mm", "ff_ev", "ff_gd", "ff_th", "ff_ti", "ff_we",
            "ff_pm", "ff_ob", "ff_rs")
        bWL0, bWL1 = P.val(sWL + "0"), P.val(sWL + "1")
        bMM, bEV, bGD, bTH, bTI = (P.val(sMM), P.val(sEV), P.val(sGD),
                                   P.val(sTH), P.val(sTI))
        bWE0, bWE1 = P.val(sWE + "0"), P.val(sWE + "1")
        bPM, bOB, bRS = P.val(sPM), P.val(sOB), P.val(sRS)
        NJT = 2 * JC

        @block.gpsimd
        def _(gp):
            P.wait(gp, pred_ld[0], pred_ld[1])
            for n in range(NJT):
                tt, jc = divmod(n, JC)
                if n >= 2:
                    P.wait(gp, sMM, bMM + n - 1)
                src0 = w_g0.rearrange("(fc p) j -> p fc j", p=128)[:, :, jc * 128:(jc + 1) * 128]
                src1 = w_g1.rearrange("(fc p) j -> p fc j", p=128)[:, :, jc * 128:(jc + 1) * 128]
                P.inc(gp.dma_start(out=W0B[n % 2][:], in_=src0), sWL + str(n % 2), 16)
                P.inc(gp.dma_start(out=W1B[n % 2][:], in_=src1), sWL + str(n % 2), 16)

        @block.tensor
        def _(t):
            P.wait(t, pred[0], pred[1])
            for n in range(NJT):
                tt, jc = divmod(n, JC)
                t.wait_ge(P.sem(sWL + str(n % 2)), (bWL0 if n % 2 == 0 else bWL1) + 32 * (n // 2 + 1))
                if n >= 2:
                    t.wait_ge(P.sem(sEV), bEV + 2 * (n - 1))
                li = None
                for fc in range(FC):
                    li = t.matmul(PG[n % 2][:], W0B[n % 2][:, fc, :],
                                  XLN8[:, fc, tt * 512:(tt + 1) * 512],
                                  start=(fc == 0), stop=(fc == FC - 1))
                for fc in range(FC):
                    li = t.matmul(PF[n % 2][:], W1B[n % 2][:, fc, :],
                                  XLN8[:, fc, tt * 512:(tt + 1) * 512],
                                  start=(fc == 0), stop=(fc == FC - 1))
                P.inc(li, sMM, 1)

        @block.scalar
        def _(sc):
            for n in range(NJT):
                tt, jc = divmod(n, JC)
                gslot = jc % 4
                gbuf = (n // 4) % 2
                P.wait(sc, sMM, bMM + n + 1)
                if n >= 8 and gslot == 0:
                    P.wait(sc, sGD, bGD + n // 4 - 1)
                P.inc(sc.activation(GATE[gbuf][:, gslot, :], PG[n % 2][:],
                                    AF.Copy), sEV, 1)
                P.inc(sc.activation(FF1B[gbuf][:, gslot, :], PF[n % 2][:],
                                    AF.Copy), sEV, 1)
                if gslot == 3:
                    gi = n // 4
                    P.wait(sc, sTI, bTI + gi + 1)
                    P.inc(sc.activation(TM2[:], TM1[:], AF.Tanh), sTH, 1)

        @block.vector
        def _(v):
            for gi in range(32):
                gbuf = gi % 2
                tt = gi // 16
                P.wait(v, sEV, bEV + (gi + 1) * 8)
                xg = GATE[gbuf][:]
                v.tensor_tensor(TM1[:], xg, xg, OP.mult)
                v.tensor_tensor(TM2[:], TM1[:], xg, OP.mult)
                v.tensor_scalar_mul(TM1[:], TM2[:], C44715)
                v.tensor_tensor(TM2[:], xg, TM1[:], OP.add)
                P.inc(v.tensor_scalar_mul(TM1[:], TM2[:], SQ2PI), sTI, 1)
                P.wait(v, sTH, bTH + gi + 1)
                v.tensor_scalar(TM1[:], TM2[:], 1.0, 0.5, OP.add, OP.mult)
                v.tensor_tensor(TM2[:], xg, TM1[:], OP.mult)
                v.tensor_tensor(TM1[:], TM2[:], FF1B[gbuf][:], OP.mult)
                jbase = (gi % 16) * 4
                P.inc(v.tensor_copy(
                    ACT8[:, jbase:jbase + 4, tt * 512:(tt + 1) * 512],
                    TM1[:]), sGD, 1)

        @block.gpsimd
        def _(gp):
            for ec in range(FC):
                if ec >= 2:
                    P.wait(gp, sPM, bPM + 2 * (ec - 1))
                src = w_l.rearrange("(kc p) e -> p kc e", p=128)[:, :, ec * 128:(ec + 1) * 128]
                P.inc(gp.dma_start(out=WLE[ec % 2][:], in_=src),
                      sWE + str(ec % 2), 16)

        @block.tensor
        def _(t):
            P.wait(t, sGD, bGD + 32)
            for ec in range(FC):
                t.wait_ge(P.sem(sWE + str(ec % 2)), (bWE0 if ec % 2 == 0 else bWE1) + 16 * (ec // 2 + 1))
                for tt in range(2):
                    n = ec * 2 + tt
                    if n >= 3:
                        t.wait_ge(P.sem(sOB), bOB + n - 2)
                    li = None
                    for jc in range(JC):
                        li = t.matmul(PO[n % 3][:], WLE[ec % 2][:, jc, :],
                                      ACT8[:, jc, tt * 512:(tt + 1) * 512],
                                      start=(jc == 0), stop=(jc == JC - 1))
                    P.inc(li, sPM, 1)

        @block.scalar
        def _(sc):
            for ec in range(FC):
                for tt in range(2):
                    n = ec * 2 + tt
                    P.wait(sc, sPM, bPM + n + 1)
                    if n >= 2:
                        P.wait(sc, sRS, bRS + n - 1)
                    P.inc(sc.activation(OBF[n % 2][:], PO[n % 3][:], AF.Copy),
                          sOB, 1)

        @block.vector
        def _(v):
            for ec in range(FC):
                for tt in range(2):
                    n = ec * 2 + tt
                    P.wait(v, sOB, bOB + n + 1)
                    P.inc(v.scalar_tensor_tensor(
                        XT[:, ec, tt * 512:(tt + 1) * 512], OBF[n % 2][:], 0.5,
                        XT[:, ec, tt * 512:(tt + 1) * 512], OP.mult, OP.add),
                        sRS, 1)

        done = (sRS, P.val(sRS))
        g.close()
        return done

    def emit_attention(pred, sub=None):
        gat = ExitStack()
        QCL = gat.enter_context(nc.sbuf_tensor("QCL", [128, QCC, T], BF16))
        ga = ExitStack()
        XLN2 = ga.enter_context(nc.sbuf_tensor("XLN2", [128, FC, T], BF16))
        ln2_done = emit_ln("l2", XLN2, pred)
        if sub == "ln2":
            ga.close(); gat.close()
            return ln2_done

        gb = ExitStack()
        nsb = lambda n_, sh, dt: gb.enter_context(nc.sbuf_tensor(n_, sh, dt))
        nps = lambda n_, sh, dt: gb.enter_context(nc.psum_tensor(n_, sh, dt))
        KV0L = nsb("KV0L", [128, CC, T], BF16)
        BQDN = nsb("BQDN", [QCC, 128], F32)
        TPB = [nps(f"TPB{i}", [128, 128], BF16) for i in range(2)]
        TPFb = nps("TPFb", [128, 128], F32)
        PKV = [nps(f"PKV{i}", [128, 512], F32) for i in range(2)]

        @block.sync
        def _(s):
            P.wait(s, ln2_done[0], ln2_done[1])
            P.inc(s.dma_start(out=BQDN[:], in_=b_qd), "at_bl", 16)

        gb1 = ExitStack()
        WKVN = gb1.enter_context(nc.sbuf_tensor("WKVN", [128, CC, D], BF16))
        WKVT = gb1.enter_context(nc.sbuf_tensor("WKVT", [128, FC, KV], BF16))

        @block.gpsimd
        def _(gp):
            P.wait(gp, ln2_done[0], ln2_done[1])
            P.inc(gp.dma_start(out=WKVN[:], in_=w_kvd.rearrange(
                "(c p) f -> p c f", p=128)), "at_wl", 16)

        @block.tensor
        def _(t):
            P.wait(t, "at_wl", 16)
            n = 0
            for cc in range(CC):
                for fc in range(FC):
                    if n >= 2:
                        P.wait(t, "at_we", n - 1)
                    P.inc(t.transpose(TPB[n % 2][:],
                                      WKVN[:, cc, fc * 128:(fc + 1) * 128],
                                      IDB[:]), "at_wt", 1)
                    n += 1

        @block.vector
        def _(v):
            n = 0
            for cc in range(CC):
                for fc in range(FC):
                    v.wait_ge(P.sem("at_wt"), n + 1)
                    P.inc(v.tensor_copy(WKVT[:, fc, cc * 128:(cc + 1) * 128],
                                        TPB[n % 2][:]), "at_we", 1)
                    n += 1

        @block.tensor
        def _(t):
            P.wait(t, ln2_done[0], ln2_done[1])
            P.wait(t, "at_we", 64)
            n = 0
            for cc in range(CC):
                for tt in range(2):
                    if n >= 2:
                        P.wait(t, "kv_ev", n - 1)
                    li = None
                    for fc in range(FC):
                        li = t.matmul(PKV[n % 2][:],
                                      WKVT[:, fc, cc * 128:(cc + 1) * 128],
                                      XLN2[:, fc, tt * 512:(tt + 1) * 512],
                                      start=(fc == 0), stop=(fc == FC - 1))
                    P.inc(li, "kv_mm", 1)
                    n += 1

        @block.scalar
        def _(sc):
            n = 0
            for cc in range(CC):
                for tt in range(2):
                    P.wait(sc, "kv_mm", n + 1)
                    P.inc(sc.activation(KV0L[:, cc, tt * 512:(tt + 1) * 512],
                                        PKV[n % 2][:], AF.Copy), "kv_ev", 1)
                    n += 1

        @block.gpsimd
        def _(gp):
            P.wait(gp, "kv_ev", 8)
            P.inc(gp.dma_start(out=ag_in.ap().rearrange("(c p) t -> p c t", p=128),
                               in_=KV0L[:]), "ag_w", 16)
            gp.wait_ge(P.sem("ag_w"), 16)
            P.inc(gp.collective_compute(
                "AllGather", OP.bypass,
                replica_groups=[list(range(NCORES))],
                ins=[ag_in.ap().opt()], outs=[ag_out.ap().opt()]), "ag_c", 1)

        gb1.close()

        gb2 = ExitStack()
        WQDN = gb2.enter_context(nc.sbuf_tensor("WQDN", [128, QCC, D], BF16))
        WQDT = gb2.enter_context(nc.sbuf_tensor("WQDT", [128, FC, QC], BF16))

        @block.gpsimd
        def _(gp):
            P.wait(gp, "kv_mm", 8)
            P.inc(gp.dma_start(out=WQDN[:], in_=w_qd.rearrange(
                "(c p) f -> p c f", p=128)), "at_wl", 32)

        @block.tensor
        def _(t):
            t.wait_ge(P.sem("at_wl"), 48)
            n = 64
            for qcc in range(QCC):
                for fc in range(FC):
                    if n >= 2:
                        P.wait(t, "at_we", n - 1)
                    P.inc(t.transpose(TPB[n % 2][:],
                                      WQDN[:, qcc, fc * 128:(fc + 1) * 128],
                                      IDB[:]), "at_wt", 1)
                    n += 1
            P.wait(t, "at_bl", 16)
            P.inc(t.transpose(TPFb[:, 0:QCC], BQDN[:], IDF[0:QCC, 0:QCC]), "at_bt", 1)

        @block.vector
        def _(v):
            n = 64
            for qcc in range(QCC):
                for fc in range(FC):
                    v.wait_ge(P.sem("at_wt"), n + 1)
                    P.inc(v.tensor_copy(WQDT[:, fc, qcc * 128:(qcc + 1) * 128],
                                        TPB[n % 2][:]), "at_we", 1)
                    n += 1
            P.wait(v, "at_bt", 1)
            P.inc(v.tensor_copy(QDB[:], TPFb[:, 0:QCC]), "at_be", 1)

        @block.tensor
        def _(t):
            P.wait(t, "at_we", 192)
            P.wait(t, "at_be", 1)
            n = 8
            for qcc in range(QCC):
                for tt in range(2):
                    if n >= 2:
                        P.wait(t, "kv_ev", n - 1)
                    li = None
                    for fc in range(FC):
                        li = t.matmul(PKV[n % 2][:],
                                      WQDT[:, fc, qcc * 128:(qcc + 1) * 128],
                                      XLN2[:, fc, tt * 512:(tt + 1) * 512],
                                      start=(fc == 0), stop=(fc == FC - 1))
                    P.inc(li, "kv_mm", 1)
                    n += 1

        @block.scalar
        def _(sc):
            n = 8
            for qcc in range(QCC):
                for tt in range(2):
                    P.wait(sc, "kv_mm", n + 1)
                    P.inc(sc.activation(QCL[:, qcc, tt * 512:(tt + 1) * 512],
                                        PKV[n % 2][:], AF.Identity,
                                        bias=QDB[:, qcc:qcc + 1]), "kv_ev", 1)
                    n += 1

        gb2.close()

        b_done = ("kv_ev", P.val("kv_ev"))
        gb.close()
        ga.close()
        if sub == "bphase":
            gat.close()
            return b_done

        gco = ExitStack()
        osb = lambda n_, sh, dt: gco.enter_context(nc.sbuf_tensor(n_, sh, dt))
        ops_ = lambda n_, sh, dt: gco.enter_context(nc.psum_tensor(n_, sh, dt))
        CVREP = osb("CVREP", [128, D], F32)
        gcr = ExitStack()
        rsb = lambda n_, sh, dt: gcr.enter_context(nc.sbuf_tensor(n_, sh, dt))
        CROW = rsb("CROW", [1, 1, D], F32)
        CRB = rsb("CRB", [1, 3, D], BF16)
        SMLF = rsb("SMLF", [16, 128], F32)
        SMLB = rsb("SMLB", [CC, 3, 128], BF16)
        BKVT = rsb("BKVT", [128, 3, CC], BF16)
        BVU = rsb("BVU", [1, D], F32)
        BRB = rsb("BRB", [128, 8], BF16)
        gcp = ExitStack()
        cps_ = lambda n_, sh, dt: gcp.enter_context(nc.psum_tensor(n_, sh, dt))
        CPT = [cps_(f"CPT{i}", [128, 128], BF16) for i in range(3)]
        CPQ = cps_("CPQ", [128, D], F32)
        TPX = cps_("TPX", [128, 128], F32)

        P.ser_begin()
        P.ser("sync", lambda e: e.dma_start(out=SMLF[0:CC, :], in_=b_kvd_c), 16)
        P.ser("sync", lambda e: e.dma_start(out=BVU[:], in_=b_vu), 16)
        P.ser("vector", lambda e: e.tensor_copy(SMLB[:, 0, :], SMLF[0:CC, :]))
        P.ser("vector", lambda e: e.tensor_tensor(SMLF[0:CC, :], SMLF[0:CC, :],
                                                  SMLB[:, 0, :], OP.subtract))
        P.ser("vector", lambda e: e.tensor_copy(SMLB[:, 1, :], SMLF[0:CC, :]))
        P.ser("vector", lambda e: e.tensor_tensor(SMLF[0:CC, :], SMLF[0:CC, :],
                                                  SMLB[:, 1, :], OP.subtract))
        P.ser("vector", lambda e: e.tensor_copy(SMLB[:, 2, :], SMLF[0:CC, :]))
        for sp in range(3):
            P.ser("tensor", lambda e, sp=sp: e.transpose(
                CPT[0][:, 0:CC], SMLB[:, sp, :], IDB[0:CC, 0:CC]))
            P.ser("vector", lambda e, sp=sp: e.tensor_copy(
                BKVT[:, sp, :], CPT[0][:, 0:CC]))
        for (bsrc, bdst) in ((b_qu, QUB), (b_ku, CKC), (b_out, COUT)):
            P.ser("sync", lambda e, bsrc=bsrc: e.dma_start(out=SMLF[:], in_=bsrc), 16)
            P.ser("tensor", lambda e: e.transpose(TPX[:, 0:FC], SMLF[:], IDF[0:FC, 0:FC]))
            P.ser("vector", lambda e, bdst=bdst: e.tensor_copy(bdst[:], TPX[:, 0:FC]))
        for (bsrc, bdst) in ((b_qr, BQR), (b_kr, CKR)):
            P.ser("sync", lambda e, bsrc=bsrc: e.dma_start(out=SMLF[0:8, :], in_=bsrc), 16)
            P.ser("tensor", lambda e: e.transpose(TPX[:, 0:8], SMLF[0:8, :], IDF[0:8, 0:8]))
            P.ser("vector", lambda e: e.tensor_copy(BRB[:], TPX[:, 0:8]))
            for i in range(8):
                P.ser("tensor", lambda e, i=i: e.matmul(
                    TPX[:, i:i + 1], RTS[:, i, :], BRB[:, i:i + 1],
                    start=True, stop=True))
            P.ser("vector", lambda e, bdst=bdst: e.tensor_copy(bdst[:], TPX[:, 0:8]))
        cst_chain1 = P.ser_emit(block, "cst", pred=b_done)[1]

        gci = ExitStack()
        WNAT = gci.enter_context(nc.sbuf_tensor("WNAT", [128, FC, QC], BF16))
        WTS = gci.enter_context(nc.sbuf_tensor("WTS", [128, QCC, D], BF16))
        WEIGHTS = [
            ("ku", w_ku, FC, CC, CC * 128, wt_k, False, "ckc"),
            ("vu", w_vu, FC, CC, CC * 128, wt_v, False, "cv"),
            ("kr", w_kr, 8, CC, CC * 128, wt_kr, True, "ckr"),
            ("qu", w_qu, FC, QCC, QCC * 128, wt_q, False, None),
            ("qr", w_qr, 8, QCC, QCC * 128, wt_qr, True, None),
        ]
        prev = [None]
        for (nm, wsrc, nrc, ncc, wid, wdst, rot, cst) in WEIGHTS:
            pv = prev[0]
            bL = P.val("c_l"); bR_ = P.val("c_r"); bRC = P.val("c_rc")
            bT = P.val("c_t"); bE = P.val("c_e"); bSt = P.val("c_s")

            @block.gpsimd
            def _(gp, nm=nm, wsrc=wsrc, nrc=nrc, wid=wid, pv=pv):
                if pv is None:
                    P.wait(gp, b_done[0], b_done[1])
                    gp.wait_ge(P.sem("ag_w"), 16)
                else:
                    P.wait(gp, "c_t", pv[1])
                P.inc(gp.dma_start(out=WNAT[:, 0:nrc, 0:wid],
                                   in_=wsrc.rearrange("(rc p) c -> p rc c", p=128)),
                      "c_l", 16)

            if rot:
                @block.tensor
                def _(t, nm=nm, nrc=nrc, wid=wid, pv=pv):
                    t.wait_ge(P.sem("c_l"), bL + 16)
                    t.wait_ge(P.sem("cst"), pv[2] if pv is not None else cst_chain1)
                    for i in range(nrc):
                        if i >= 1:
                            P.wait(t, "c_rc", bRC + i)
                        li = None
                        for hq in range(wid // 512):
                            li = t.matmul(CPQ[:, hq * 512:(hq + 1) * 512],
                                          RTS[:, i, :],
                                          WNAT[:, i, hq * 512:(hq + 1) * 512],
                                          start=True, stop=True)
                        P.inc(li, "c_r", 1)

                @block.vector
                def _(v, nm=nm, nrc=nrc, wid=wid):
                    for i in range(nrc):
                        v.wait_ge(P.sem("c_r"), bR_ + i + 1)
                        P.inc(v.tensor_copy(WNAT[:, i, 0:wid], CPQ[:, 0:wid]),
                              "c_rc", 1)

            @block.tensor
            def _(t, nm=nm, nrc=nrc, ncc=ncc, rot=rot, pv=pv):
                if rot:
                    P.wait(t, "c_rc", bRC + nrc)
                else:
                    t.wait_ge(P.sem("c_l"), bL + 16)
                    t.wait_ge(P.sem("cst"),
                              pv[2] if pv is not None else cst_chain1)
                if pv is not None:
                    t.wait_ge(P.sem("c_s"), bSt)
                n = 0
                for rc in range(nrc):
                    for cc2 in range(ncc):
                        if n >= 3:
                            P.wait(t, "c_e", bE + n - 2)
                        P.inc(t.transpose(CPT[n % 3][:],
                                          WNAT[:, rc, cc2 * 128:(cc2 + 1) * 128],
                                          IDB[:]), "c_t", 1)
                        n += 1

            @block.vector
            def _(v, nm=nm, nrc=nrc, ncc=ncc):
                n = 0
                for rc in range(nrc):
                    for cc2 in range(ncc):
                        v.wait_ge(P.sem("c_t"), bT + n + 1)
                        P.inc(v.tensor_copy(WTS[:, cc2, rc * 128:(rc + 1) * 128],
                                            CPT[n % 3][:]), "c_e", 1)
                        n += 1

            @block.sync
            def _(s, nm=nm, nrc=nrc, ncc=ncc, wdst=wdst):
                P.wait(s, "c_e", P.val("c_e"))
                P.inc(s.dma_start(
                    out=wdst.rearrange("(cb p) d -> p cb d", p=128),
                    in_=WTS[:, 0:ncc, 0:nrc * 128]), "c_s", 16)

            if cst is not None:
                P.ser_begin()
                nq = (nrc * 128) // 512
                for cc2 in range(ncc):
                    for sp in range(3):
                        for hh in range(nq):
                            def mmf(e, cc2=cc2, sp=sp, hh=hh,
                                    fi=(cc2 == 0 and sp == 0),
                                    last=(cc2 == ncc - 1 and sp == 2)):
                                return e.matmul(
                                    CPQ[0:1, hh * 512:(hh + 1) * 512],
                                    BKVT[:, sp, cc2:cc2 + 1],
                                    WTS[:, cc2, hh * 512:(hh + 1) * 512],
                                    start=fi, stop=last)
                            P.ser("tensor", mmf)
                P.ser("vector", lambda e, nrc=nrc: e.tensor_copy(
                    CROW[:, 0, 0:nrc * 128], CPQ[0:1, 0:nrc * 128]))
                if cst == "cv":
                    P.ser("vector", lambda e: e.tensor_tensor(
                        CROW[:, 0, :], CROW[:, 0, :], BVU[:], OP.add))
                    P.ser("vector", lambda e: e.tensor_copy(CRB[:, 0, :], CROW[:, 0, :]))
                    P.ser("vector", lambda e: e.tensor_tensor(
                        CVREP[0:1, :], CROW[:, 0, :], CRB[:, 0, :], OP.subtract))
                    P.ser("vector", lambda e: e.tensor_copy(CRB[:, 1, :], CVREP[0:1, :]))
                    P.ser("vector", lambda e: e.tensor_tensor(
                        CVREP[0:1, :], CVREP[0:1, :], CRB[:, 1, :], OP.subtract))
                    P.ser("vector", lambda e: e.tensor_copy(CRB[:, 2, :], CVREP[0:1, :]))
                    for hh in range(4):
                        for sp in range(3):
                            P.ser("tensor", lambda e, sp=sp, hh=hh: e.matmul(
                                CPQ[:, hh * 512:(hh + 1) * 512], ONESM[:],
                                CRB[:, sp, hh * 512:(hh + 1) * 512],
                                start=(sp == 0), stop=(sp == 2)))
                    P.ser("vector", lambda e: e.tensor_copy(CVREP[:], CPQ[:]))
                else:
                    dstt = CKC if cst == "ckc" else CKR
                    nch = (nrc * 128) // 128
                    for i in range(nch):
                        P.ser("tensor", lambda e, i=i: e.transpose(
                            TPX[:, i:i + 1],
                            CROW[:, 0, i * 128:(i + 1) * 128], IDF[0:1, 0:1]))
                    P.ser("vector", lambda e, dstt=dstt, nch=nch: e.tensor_tensor(
                        dstt[:, 0:nch], dstt[:, 0:nch], TPX[:, 0:nch], OP.add))
                P.ser_emit(block, "cst", pred=("c_e", P.val("c_e")))
            prev[0] = (nm, P.val("c_t"), P.val("cst"))

        cs_total = P.val("c_s")
        wout_pred_ce = ("c_s", cs_total)
        cst_after_weights = P.val("cst")
        gci.close()

        gcr.close()
        gwo = ExitStack()
        WOTS = gwo.enter_context(nc.sbuf_tensor("WOTS", [128, FC, D], BF16))
        WONF = [gwo.enter_context(nc.sbuf_tensor(f"WONF{i}", [128, D], F32))
                for i in range(2)]
        WONB = gwo.enter_context(nc.sbuf_tensor("WONB", [128, D], BF16))
        SCRC = gwo.enter_context(nc.sbuf_tensor("SCRC", [128, D], BF16))
        CSCR = gwo.enter_context(nc.sbuf_tensor("CSCR", [128, FC], F32))

        @block.sync
        def _(s):
            P.wait(s, wout_pred_ce[0], wout_pred_ce[1])
            P.wait(s, "cst", cst_after_weights)
            for ec in range(FC):
                if ec >= 2:
                    P.wait(s, "wo_n", ec - 1)
                P.inc(s.dma_start(out=WONF[ec % 2][:],
                                  in_=w_out[ec * 128:(ec + 1) * 128, :]),
                      "wo_l" + str(ec % 2), 16)

        @block.vector
        def _(v):
            for ec in range(FC):
                v.wait_ge(P.sem("wo_l" + str(ec % 2)), 16 * (ec // 2 + 1))
                P.inc(v.scalar_tensor_tensor(SCRC[:], WONF[ec % 2][:], 1.0,
                                             CVREP[:], OP.mult, OP.mult,
                                             accum_out=CSCR[:, ec:ec + 1]),
                      "wo_c", 1)
                if ec >= 1:
                    P.wait(v, "wo_t", ec * FC)
                P.inc(v.tensor_copy(WONB[:], WONF[ec % 2][:]), "wo_n", 1)
                for rc in range(FC):
                    nn = ec * FC + rc
                    v.wait_ge(P.sem("wo_t"), nn + 1)
                    P.inc(v.tensor_copy(WOTS[:, rc, ec * 128:(ec + 1) * 128],
                                        CPT[nn % 2][:]), "wo_e", 1)
            P.wait(v, "wo_c", FC)
            P.inc(v.tensor_tensor(COUT[:], COUT[:], CSCR[:], OP.add), "wo_f", 1)

        @block.tensor
        def _(t):
            for ec in range(FC):
                P.wait(t, "wo_n", ec + 1)
                for rc in range(FC):
                    nn = ec * FC + rc
                    if nn >= 2:
                        P.wait(t, "wo_e", nn - 1)
                    P.inc(t.transpose(CPT[nn % 2][:],
                                      WONB[:, rc * 128:(rc + 1) * 128], IDB[:]),
                          "wo_t", 1)

        @block.sync
        def _(s):
            P.wait(s, "wo_e", FC * FC)
            P.inc(s.dma_start(out=wt_o.rearrange("(cb p) d -> p cb d", p=128),
                              in_=WOTS[:]), "wo_s", 16)

        if sub == "ctx":
            tap_cv2 = nc.dram_tensor("tap_cv2", [128, D], F32,
                                     kind="ExternalOutput").ap()
            tap_cs2 = nc.dram_tensor("tap_cs2", [128, FC], F32,
                                     kind="ExternalOutput").ap()
            tap_wts = nc.dram_tensor("tap_wts", [KV, D], BF16,
                                     kind="ExternalOutput").ap()

            @block.gpsimd
            def _(gp):
                P.wait(gp, "wo_f", 1)
                P.wait(gp, "wo_s", 16)
                P.inc(gp.dma_start(out=tap_cv2, in_=CVREP[:]), "dbg5", 16)
                P.inc(gp.dma_start(out=tap_cs2, in_=CSCR[:]), "dbg5", 16)
                P.inc(gp.dma_start(out=tap_wts, in_=wt_v.ap()), "dbg5", 16)
                P.wait(gp, "dbg5")
        gwo.close()
        cst_total = P.val("cst")
        gcp.close()
        gco.close()
        if sub == "prep":
            gat.close()
            return ("cst", cst_total)

        gd = ExitStack()
        dsb = lambda n_, sh, dt: gd.enter_context(nc.sbuf_tensor(n_, sh, dt))
        dps = lambda n_, sh, dt: gd.enter_context(nc.psum_tensor(n_, sh, dt))
        KV0SB = dsb("KV0SB", [128, CC, SK], BF16)
        WKH = [dsb(f"WKH{i}", [128, CC, 128], BF16) for i in range(2)]
        WVH = [dsb(f"WVH{i}", [128, CC, 128], BF16) for i in range(2)]
        WKRH = [dsb(f"WKRH{i}", [128, CC, RD], BF16) for i in range(2)]
        WQH = [dsb(f"WQH{i}", [128, QCC, 128], BF16) for i in range(2)]
        WQRH = [dsb(f"WQRH{i}", [128, QCC, RD], BF16) for i in range(2)]
        KSTG = [dsb(f"KSTG{i}", [128, 512], BF16) for i in range(2)]
        KRSTG = [dsb(f"KRSTG{i}", [RD, 512], BF16) for i in range(2)]
        VSTG = [dsb(f"VSTG{i}", [128, 512], BF16) for i in range(2)]
        QSTG = dsb("QSTG", [128, T], BF16)
        QRSTG = dsb("QRSTG", [RD, T], BF16)
        MXALL = dsb("MXALL", [128, NH], F32)
        MXH = dsb("MXH", [16, 2], F32)
        MXB = dsb("MXB", [1, 16], BF16)
        PKC = [dps(f"PKC{i}", [128, 512], F32) for i in range(2)]
        PKR = [dps(f"PKR{i}", [RD, 512], F32) for i in range(2)]
        PV = [dps(f"PV{i}", [128, 512], F32) for i in range(2)]
        PSS = [dps(f"PSS{i}", [128, 512], F32) for i in range(2)]

        @block.gpsimd
        def _(gp):
            gp.wait_ge(P.sem("ag_c"), 1)
            P.wait(gp, "wo_s", 16)
            P.wait(gp, "wo_f", 1)
            for cc in range(CC):
                for r in range(NCORES):
                    P.inc(gp.dma_start(
                        out=KV0SB[:, cc, r * T:(r + 1) * T],
                        in_=ag_out.ap()[r * KV + cc * 128:r * KV + (cc + 1) * 128, :]),
                        "kv0_l", 16)

        NT2 = 16

        for h in range(NH):
            hb = (h % 2) * RD
            hc2 = h // 2

            @block.sync
            def _(s, h=h):
                if h >= 2:
                    P.wait(s, "g_hd", h - 1)
                if h == 0:
                    P.wait(s, "c_s", cs_total)
                    P.wait(s, "wo_s", 16)
                    P.wait(s, "wo_f", 1)
                for (buf, wsrc, wid2) in ((WKH, wt_k, 128), (WVH, wt_v, 128),
                                          (WKRH, wt_kr, RD), (WQH, wt_q, 128),
                                          (WQRH, wt_qr, RD)):
                    P.inc(s.dma_start(
                        out=buf[h % 2][:],
                        in_=wsrc[:, h * wid2:(h + 1) * wid2].rearrange(
                            "(cc p) d -> p cc d", p=128)),
                        "dw" + str(h % 2), 16)

            @block.tensor
            def _(t, h=h):
                t.wait_ge(P.sem("dw" + str(h % 2)), 80 * (h // 2 + 1))
                if h == 0:
                    P.wait(t, "kv0_l", 16 * CC * NCORES)
                    P.wait(t, "wo_s", 16)
                    P.wait(t, "wo_f", 1)
                for ti in range(2):
                    if h >= 1 or ti >= 1:
                        pass
                    if h >= 1:
                        t.wait_ge(P.sem("g_ke"), NT2 * (h - 1) + 15 + ti)
                    li = None
                    for qcc in range(QCC):
                        li = t.matmul(PKC[ti][:], WQH[h % 2][:, qcc, :],
                                      QCL[:, qcc, ti * 512:(ti + 1) * 512],
                                      start=(qcc == 0), stop=(qcc == QCC - 1))
                    P.inc(li, "g_qm", 1)
                for ti in range(2):
                    if h >= 1:
                        t.wait_ge(P.sem("g_re"), NT2 * (h - 1) + 15 + ti)
                    li = None
                    for qcc in range(QCC):
                        li = t.matmul(PKR[ti][:], WQRH[h % 2][:, qcc, :],
                                      QCL[:, qcc, ti * 512:(ti + 1) * 512],
                                      start=(qcc == 0), stop=(qcc == QCC - 1))
                    P.inc(li, "g_qm", 1)

            @block.scalar
            def _(sc, h=h, hb=hb, hc2=hc2):
                if h == 0:
                    P.wait(sc, "cst", cst_total)
                    P.wait(sc, "wo_s", 16)
                    P.wait(sc, "wo_f", 1)
                sc.wait_ge(P.sem("sq"), 16 * h)
                sc.wait_ge(P.sem("g_sm"), h)
                for ti in range(2):
                    sc.wait_ge(P.sem("g_qm"), 4 * h + ti + 1)
                    P.inc(sc.activation(QSTG[:, ti * 512:(ti + 1) * 512],
                                        PKC[ti][:], AF.Identity,
                                        bias=QUB[:, h:h + 1]), "g_qe", 1)
                sc.wait_ge(P.sem("sqr"), 16 * h)
                for ti in range(2):
                    sc.wait_ge(P.sem("g_qm"), 4 * h + ti + 3)
                    P.inc(sc.activation(QRSTG[0:RD, ti * 512:(ti + 1) * 512],
                                        PKR[ti][0:RD, :], AF.Identity,
                                        bias=BQR[hb:hb + RD, hc2:hc2 + 1]),
                          "g_qe", 1)

            @block.sync
            def _(s, h=h):
                s.wait_ge(P.sem("g_qe"), 4 * h + 2)
                P.inc(s.dma_start(out=qcs_scr.ap()[h], in_=QSTG[:]), "sq", 16)
                s.wait_ge(P.sem("g_qe"), 4 * h + 4)
                P.inc(s.dma_start(out=qrs_scr.ap()[h], in_=QRSTG[:]), "sqr", 16)

            @block.tensor
            def _(t, h=h):
                t.wait_ge(P.sem("g_qe"), 4 * h + 1)
                li = None
                for cc in range(CC):
                    li = t.matmul(PKC[0][:], WKH[h % 2][:, cc, :],
                                  KV0SB[:, cc, 0:512], start=(cc == 0),
                                  stop=(cc == CC - 1))
                P.inc(li, "g_km", 1)
                t.wait_ge(P.sem("g_qe"), 4 * h + 3)
                li = None
                for cc in range(CC):
                    li = t.matmul(PKR[0][:], WKRH[h % 2][:, cc, :],
                                  KV0SB[:, cc, 0:512], start=(cc == 0),
                                  stop=(cc == CC - 1))
                P.inc(li, "g_rm", 1)

            @block.scalar
            def _(sc, h=h, hb=hb, hc2=hc2):
                sc.wait_ge(P.sem("g_km"), NT2 * h + 1)
                sc.wait_ge(P.sem("skc0"), 16 * 8 * h)
                P.inc(sc.activation(KSTG[0][:], PKC[0][:], AF.Identity,
                                    bias=CKC[:, h:h + 1]), "g_ke", 1)
                sc.wait_ge(P.sem("g_rm"), NT2 * h + 1)
                sc.wait_ge(P.sem("skr0"), 16 * 8 * h)
                P.inc(sc.activation(KRSTG[0][0:RD, :], PKR[0][0:RD, :],
                                    AF.Identity,
                                    bias=CKR[hb:hb + RD, hc2:hc2 + 1]),
                      "g_re", 1)

            @block.tensor
            def _(t, h=h):
                t.wait_ge(P.sem("g_re"), NT2 * h + 1)
                if h >= 2:
                    t.wait_ge(P.sem("g_mx"), h - 1)
                t.matmul(PSS[h % 2][:], KSTG[0][:, 0:128], QSTG[:, 0:512],
                         start=True, stop=False)
                P.inc(t.matmul(PSS[h % 2][:], KRSTG[0][:, 0:128],
                               QRSTG[:, 0:512], start=False, stop=True),
                      "g_sm", 1)

            @block.vector
            def _(v, h=h):
                P.wait(v, "g_sm", h + 1)
                P.inc(v.reduce_max(MXALL[:, h:h + 1], PSS[h % 2][:],
                                   mybir.AxisListType.X), "g_mx", 1)

            @block.tensor
            def _(t, h=h):
                def vmm(kt):
                    t.wait_ge(P.sem("g_ve"),
                              NT2 * h + kt - 1 if kt >= 2 else 16 * h)
                    li = None
                    for ci in range(4):
                        for cc in range(CC):
                            li = t.matmul(
                                PV[kt % 2][:, ci * 128:(ci + 1) * 128],
                                KV0SB[:, cc, (kt * 4 + ci) * 128:(kt * 4 + ci + 1) * 128],
                                WVH[h % 2][:, cc, :],
                                start=(cc == 0), stop=(cc == CC - 1))
                    P.inc(li, "g_vm", 1)

                vmm(0)
                for kt in range(1, NT2):
                    if kt == 1:
                        t.wait_ge(P.sem("g_qe"), 4 * h + 2)
                    else:
                        t.wait_ge(P.sem("g_ke"), NT2 * h + kt - 1)
                    li = None
                    for cc in range(CC):
                        li = t.matmul(PKC[kt % 2][:], WKH[h % 2][:, cc, :],
                                      KV0SB[:, cc, kt * 512:(kt + 1) * 512],
                                      start=(cc == 0), stop=(cc == CC - 1))
                    P.inc(li, "g_km", 1)
                    if kt == 1:
                        t.wait_ge(P.sem("g_qe"), 4 * h + 4)
                    else:
                        t.wait_ge(P.sem("g_re"), NT2 * h + kt - 1)
                    li = None
                    for cc in range(CC):
                        li = t.matmul(PKR[kt % 2][:], WKRH[h % 2][:, cc, :],
                                      KV0SB[:, cc, kt * 512:(kt + 1) * 512],
                                      start=(cc == 0), stop=(cc == CC - 1))
                    P.inc(li, "g_rm", 1)
                    vmm(kt)
                P.inc(t.nop(), "g_hd", 1)

            @block.scalar
            def _(sc, h=h, hb=hb, hc2=hc2):
                def vev(kt):
                    sc.wait_ge(P.sem("g_vm"), NT2 * h + kt + 1)
                    sc.wait_ge(P.sem("sv" + str(kt % 2)),
                               16 * (8 * h + kt // 2))
                    P.inc(sc.activation(VSTG[kt % 2][:], PV[kt % 2][:],
                                        AF.Copy), "g_ve", 1)

                vev(0)
                for kt in range(1, NT2):
                    sc.wait_ge(P.sem("g_km"), NT2 * h + kt + 1)
                    sc.wait_ge(P.sem("skc" + str(kt % 2)),
                               16 * (8 * h + kt // 2))
                    if kt == 2:
                        sc.wait_ge(P.sem("g_sm"), h + 1)
                    P.inc(sc.activation(KSTG[kt % 2][:], PKC[kt % 2][:],
                                        AF.Identity, bias=CKC[:, h:h + 1]),
                          "g_ke", 1)
                    sc.wait_ge(P.sem("g_rm"), NT2 * h + kt + 1)
                    sc.wait_ge(P.sem("skr" + str(kt % 2)),
                               16 * (8 * h + kt // 2))
                    if kt == 2:
                        sc.wait_ge(P.sem("g_sm"), h + 1)
                    P.inc(sc.activation(KRSTG[kt % 2][0:RD, :],
                                        PKR[kt % 2][0:RD, :], AF.Identity,
                                        bias=CKR[hb:hb + RD, hc2:hc2 + 1]),
                          "g_re", 1)
                    vev(kt)

            @block.sync
            def _(s, h=h):
                def vst(kt):
                    s.wait_ge(P.sem("g_ve"), NT2 * h + kt + 1)
                    P.inc(s.dma_start(
                        out=v_scr.ap()[h, kt * 4:(kt + 1) * 4].rearrange(
                            "kc p d -> p kc d"),
                        in_=VSTG[kt % 2][:].rearrange("p (kc d) -> p kc d", kc=4)),
                        "sv" + str(kt % 2), 16)

                vst(0)
                s.wait_ge(P.sem("g_ke"), NT2 * h + 1)
                P.inc(s.dma_start(
                    out=kc_scr.ap()[h, :, 0:512],
                    in_=KSTG[0][:]), "skc0", 16)
                s.wait_ge(P.sem("g_re"), NT2 * h + 1)
                P.inc(s.dma_start(
                    out=kr_scr.ap()[h, :, 0:512],
                    in_=KRSTG[0][:]), "skr0", 16)
                for kt in range(1, NT2):
                    s.wait_ge(P.sem("g_ke"), NT2 * h + kt + 1)
                    P.inc(s.dma_start(
                        out=kc_scr.ap()[h, :, kt * 512:(kt + 1) * 512],
                        in_=KSTG[kt % 2][:]), "skc" + str(kt % 2), 16)
                    s.wait_ge(P.sem("g_re"), NT2 * h + kt + 1)
                    P.inc(s.dma_start(
                        out=kr_scr.ap()[h, :, kt * 512:(kt + 1) * 512],
                        in_=KRSTG[kt % 2][:]), "skr" + str(kt % 2), 16)
                    vst(kt)

        P.ser_begin()
        _gke_t = P.val("g_ke"); _gre_t = P.val("g_re"); _gve_t = P.val("g_ve")

        def _bh_first(e):
            e.wait_ge(P.sem("g_ke"), _gke_t)
            e.wait_ge(P.sem("g_re"), _gre_t)
            e.wait_ge(P.sem("g_ve"), _gve_t)
            return e.transpose(PKR[0][0:16, 0:128], MXALL[:], IDF[:])
        P.ser("tensor", _bh_first)
        P.ser("vector", lambda e: e.reduce_max(MXH[:, 0:1], PKR[0][0:16, 0:128],
                                               mybir.AxisListType.X))
        P.ser("vector", lambda e: e.tensor_scalar_mul(MXH[:, 1:2], MXH[:, 0:1],
                                                      -SCALE))
        P.ser("tensor", lambda e: e.transpose(PKC[0][0:1, 0:16], MXH[:, 1:2],
                                              IDF[0:16, 0:16]))
        P.ser("vector", lambda e: e.tensor_copy(MXB[:], PKC[0][0:1, 0:16]))
        P.ser("tensor", lambda e: e.matmul(PKC[1][:, 0:16], ONESM[:], MXB[:],
                                           start=True, stop=True))
        P.ser("vector", lambda e: e.tensor_copy(BHS[:], PKC[1][:, 0:16]))
        P.ser_emit(block, "bh", pred=("g_mx", NH))

        gen_waits = [(nm, P.val(nm)) for nm in
                     ("skc0", "skc1", "skr0", "skr1", "sv0", "sv1", "sq", "sqr")]
        bh_total = P.val("bh")
        gd.close()
        gat.close()
        if sub == "gen":
            tap_kc = nc.dram_tensor("tap_kc", [128, SK], BF16, kind="ExternalOutput").ap()
            tap_kr = nc.dram_tensor("tap_kr", [RD, SK], BF16, kind="ExternalOutput").ap()
            tap_v = nc.dram_tensor("tap_v", [KC, 128, HD], BF16, kind="ExternalOutput").ap()
            tap_qc = nc.dram_tensor("tap_qc", [128, T], BF16, kind="ExternalOutput").ap()
            tap_qr = nc.dram_tensor("tap_qr", [RD, T], BF16, kind="ExternalOutput").ap()
            tap_ag = nc.dram_tensor("tap_ag", [NCORES * KV, T], BF16, kind="ExternalOutput").ap()
            tap_bh = nc.dram_tensor("tap_bh", [128, NH], F32, kind="ExternalOutput").ap()
            tap_ck = nc.dram_tensor("tap_ck", [128, FC], F32, kind="ExternalOutput").ap()
            tap_co = nc.dram_tensor("tap_co", [128, FC], F32, kind="ExternalOutput").ap()
            HDBG = 3

            @block.sync
            def _(s):
                for nm_, vv_ in [("skc0", None), ("skc1", None), ("skr0", None),
                                 ("skr1", None), ("sv0", None), ("sv1", None),
                                 ("sq", None), ("sqr", None)]:
                    P.wait(s, nm_)
                P.wait(s, "bh", P.val("bh"))
                P.inc(s.dma_start(out=tap_kc, in_=kc_scr.ap()[HDBG]), "dbg", 16)
                P.inc(s.dma_start(out=tap_kr, in_=kr_scr.ap()[HDBG]), "dbg", 16)
                P.inc(s.dma_start(out=tap_v, in_=v_scr.ap()[HDBG]), "dbg", 16)
                P.inc(s.dma_start(out=tap_qc, in_=qcs_scr.ap()[HDBG]), "dbg", 16)
                P.inc(s.dma_start(out=tap_qr, in_=qrs_scr.ap()[HDBG]), "dbg", 16)
                P.inc(s.dma_start(out=tap_ag, in_=ag_out.ap()), "dbg", 16)
                P.wait(s, "dbg")

            @block.gpsimd
            def _(gp):
                P.wait(gp, "bh", P.val("bh"))
                P.inc(gp.dma_start(out=tap_bh, in_=BHS[:]), "dbg2", 16)
                P.inc(gp.dma_start(out=tap_ck, in_=CKC[:]), "dbg2", 16)
                P.inc(gp.dma_start(out=tap_co, in_=COUT[:]), "dbg2", 16)
                P.wait(gp, "dbg2")
            return ("g_hd", P.val("g_hd"))

        ge = ExitStack()
        esb = lambda n_, sh, dt: ge.enter_context(nc.sbuf_tensor(n_, sh, dt))
        eps_ = lambda n_, sh, dt: ge.enter_context(nc.psum_tensor(n_, sh, dt))
        CTX = esb("CTX", [128, NH, T], BF16)
        KCB = [esb(f"KCB{i}", [128, SK // 2], BF16) for i in range(2)]
        KRB = [esb(f"KRB{i}", [128, SK // 2], BF16) for i in range(2)]
        VB = [esb(f"VB{i}", [128, 32, 128], BF16) for i in range(2)]
        QCB = [esb(f"QCB{i}", [128, T], BF16) for i in range(2)]
        QRB = [esb(f"QRB{i}", [RD, T], BF16) for i in range(2)]
        QRB2 = [esb(f"QRB2{i}", [128, T], BF16) for i in range(2)]
        EG = [esb(f"EG{i}", [128, 1024], BF16) for i in range(2)]
        RZROW = esb("RZROW", [1, 1024], F32)
        RZB = esb("RZB", [1, 1024], BF16)
        RZS = esb("RZS", [128, 1024], BF16)
        gep = ExitStack()
        eps2 = lambda n_, sh, dt: gep.enter_context(nc.psum_tensor(n_, sh, dt))
        SG = [eps2(f"SG{i}", [128, 1024], F32) for i in range(2)]
        CP = [eps2(f"CP{i}", [128, 512], F32) for i in range(2)]
        ZP = [eps2(f"ZP{i}", [1, 512], F32) for i in range(2)]

        @block.sync
        def _(s):
            for nm, vv in gen_waits:
                s.wait_ge(P.sem(nm), vv)
            for m in range(2 * NH):
                h, half = divmod(m, 2)
                if m >= 2:
                    P.wait(s, "e_cd", m - 1)
                P.inc(s.dma_start(out=KCB[m % 2][:],
                                  in_=kc_scr.ap()[h, :, half * 4096:(half + 1) * 4096]),
                      "e_ld" + str(m % 2), 16)
                P.inc(s.dma_start(out=KRB[m % 2][0:RD, :],
                                  in_=kr_scr.ap()[h, :, half * 4096:(half + 1) * 4096]),
                      "e_ld" + str(m % 2), 16)
                P.inc(s.dma_start(out=KRB[m % 2][RD:128, :],
                                  in_=kr_scr.ap()[h, :, half * 4096:(half + 1) * 4096]),
                      "e_ld" + str(m % 2), 16)
                P.inc(s.dma_start(out=VB[m % 2][:],
                                  in_=v_scr.ap()[h, half * 32:(half + 1) * 32].rearrange(
                                      "kc p d -> p kc d")),
                      "e_ld" + str(m % 2), 16)
                if half == 0:
                    P.inc(s.dma_start(out=QCB[h % 2][:], in_=qcs_scr.ap()[h]),
                          "e_ld" + str(m % 2), 16)
                    P.inc(s.dma_start(out=QRB[h % 2][:], in_=qrs_scr.ap()[h]),
                          "e_ld" + str(m % 2), 16)
                    P.inc(s.dma_start(out=QRB2[h % 2][64:128, :], in_=qrs_scr.ap()[h]),
                          "e_ld" + str(m % 2), 16)

        for h in range(NH):
            @block.tensor
            def _(t, h=h):
                if h == 0:
                    P.wait(t, "bh", bh_total)
                for half in range(2):
                    m = 2 * h + half
                    t.wait_ge(P.sem("e_ld" + str(m % 2)),
                              (112 if m % 2 == 0 else 64) * (h + 1))
                    for qt in range(2):
                        if half == 0:
                            t.wait_ge(P.sem("en"), 8 * h)
                        pend = []
                        for g in range(16):
                            egi = (m * 2 + qt) * 16 + g
                            if egi >= 2:
                                t.wait_ge(P.sem("e_ex"), egi - 1)
                            kc0, kc1 = 2 * g, 2 * g + 1
                            t.matmul(SG[egi % 2][:, 0:512],
                                     KCB[m % 2][:, kc0 * 128:(kc0 + 1) * 128],
                                     QCB[h % 2][:, qt * 512:(qt + 1) * 512],
                                     start=True, stop=False)
                            t.matmul(SG[egi % 2][:, 512:1024],
                                     KCB[m % 2][:, kc1 * 128:(kc1 + 1) * 128],
                                     QCB[h % 2][:, qt * 512:(qt + 1) * 512],
                                     start=True, stop=False)
                            t.matmul(SG[egi % 2][:, 0:512],
                                     KRB[m % 2][0:RD, kc0 * 128:(kc0 + 1) * 128],
                                     QRB[h % 2][:, qt * 512:(qt + 1) * 512],
                                     start=False, stop=True)
                            li = t.matmul(SG[egi % 2][:, 512:1024],
                                          KRB[m % 2][RD:128, kc1 * 128:(kc1 + 1) * 128],
                                          QRB2[h % 2][64:128, qt * 512:(qt + 1) * 512],
                                          start=False, stop=True)
                            P.inc(li, "e_sm", 1)
                            pend.append((egi, g))
                            if len(pend) >= 2:
                                pegi, pg = pend.pop(0)
                                t.wait_ge(P.sem("e_ex"), pegi + 1)
                                li2 = None
                                for i in range(2):
                                    kc_ = 2 * pg + i
                                    st = (half == 0 and pg == 0 and i == 0)
                                    sp = (half == 1 and pg == 15 and i == 1)
                                    li2 = t.matmul(CP[qt][:],
                                                   VB[m % 2][:, kc_, :],
                                                   EG[pegi % 2][:, i * 512:(i + 1) * 512],
                                                   start=st, stop=sp)
                                    li2 = t.matmul(ZP[qt][:], ONES[:],
                                                   EG[pegi % 2][:, i * 512:(i + 1) * 512],
                                                   start=st, stop=sp)
                                P.inc(li2, "e_cz", 1)
                        pegi, pg = pend.pop(0)
                        t.wait_ge(P.sem("e_ex"), pegi + 1)
                        li2 = None
                        for i in range(2):
                            kc_ = 2 * pg + i
                            st = (half == 0 and pg == 0 and i == 0)
                            sp = (half == 1 and pg == 15 and i == 1)
                            li2 = t.matmul(CP[qt][:], VB[m % 2][:, kc_, :],
                                           EG[pegi % 2][:, i * 512:(i + 1) * 512],
                                           start=st, stop=sp)
                            li2 = t.matmul(ZP[qt][:], ONES[:],
                                           EG[pegi % 2][:, i * 512:(i + 1) * 512],
                                           start=st, stop=sp)
                        P.inc(li2, "e_cz", 1)
                    P.inc(t.nop(), "e_cd", 1)

            @block.scalar
            def _(sc, h=h):
                P.wait(sc, "bh", bh_total)
                for half in range(2):
                    m = 2 * h + half
                    for qt in range(2):
                        for g in range(16):
                            egi = (m * 2 + qt) * 16 + g
                            sc.wait_ge(P.sem("e_sm"), egi + 1)
                            if egi >= 2:
                                sc.wait_ge(P.sem("e_cz"), egi - 1)
                            P.inc(sc.activation(EG[egi % 2][:], SG[egi % 2][:],
                                                AF.Exp, bias=BHS[:, h:h + 1],
                                                scale=SCALE), "e_ex", 1)

            m1 = 2 * h + 1
            ecz_after = m1 * 32 + 32
            P.ser_begin()
            P.ser("vector", lambda e: e.reciprocal(RZROW[:, 0:512], ZP[0][:]))
            P.ser("vector", lambda e: e.reciprocal(RZROW[:, 512:1024], ZP[1][:]))
            P.ser("vector", lambda e: e.tensor_copy(RZB[:], RZROW[:]))
            P.ser("tensor", lambda e: e.matmul(SG[0][:, 0:512], ONESM[:],
                                               RZB[:, 0:512], start=True,
                                               stop=True))
            P.ser("tensor", lambda e: e.matmul(SG[0][:, 512:1024], ONESM[:],
                                               RZB[:, 512:1024], start=True,
                                               stop=True))
            P.ser("scalar", lambda e: e.activation(RZS[:], SG[0][:, 0:1024],
                                                   AF.Copy))
            P.ser("vector", lambda e, h=h: e.tensor_tensor(
                CTX[:, h, 0:512], CP[0][:], RZS[:, 0:512], OP.mult))
            P.ser("vector", lambda e, h=h: e.tensor_tensor(
                CTX[:, h, 512:1024], CP[1][:], RZS[:, 512:1024], OP.mult))
            P.ser_emit(block, "en", pred=("e_cz", ecz_after))

        en_total = P.val("en")
        gep.close()

        WOE = [ge.enter_context(nc.sbuf_tensor(f"WOE{i}", [128, FC, 128], BF16))
               for i in range(2)]
        POF = [ge.enter_context(nc.psum_tensor(f"POF{i}", [128, 512], F32))
               for i in range(2)]

        @block.gpsimd
        def _(gp):
            P.wait(gp, "wo_s", 16)
            P.wait(gp, "en", en_total)
            for ec in range(FC):
                if ec >= 2:
                    P.wait(gp, "f_pm", 2 * (ec - 1))
                P.inc(gp.dma_start(
                    out=WOE[ec % 2][:],
                    in_=wt_o[:, ec * 128:(ec + 1) * 128].rearrange(
                        "(hc p) e -> p hc e", p=128)), "f_wl" + str(ec % 2), 16)

        @block.tensor
        def _(t):
            P.wait(t, "en", en_total)
            for ec in range(FC):
                t.wait_ge(P.sem("f_wl" + str(ec % 2)), 16 * (ec // 2 + 1))
                for tt in range(2):
                    n = ec * 2 + tt
                    if n >= 2:
                        t.wait_ge(P.sem("f_rs"), n - 1)
                    li = None
                    for hc in range(FC):
                        li = t.matmul(POF[n % 2][:], WOE[ec % 2][:, hc, :],
                                      CTX[:, hc, tt * 512:(tt + 1) * 512],
                                      start=(hc == 0), stop=(hc == FC - 1))
                    P.inc(li, "f_pm", 1)

        @block.vector
        def _(v):
            P.wait(v, "wo_f", 1)
            for ec in range(FC):
                for tt in range(2):
                    n = ec * 2 + tt
                    v.wait_ge(P.sem("f_pm"), n + 1)
                    P.inc(v.scalar_tensor_tensor(
                        XT[:, ec, tt * 512:(tt + 1) * 512], POF[n % 2][:],
                        COUT[:, ec:ec + 1],
                        XT[:, ec, tt * 512:(tt + 1) * 512], OP.add, OP.add),
                        "f_rs", 1)

        if sub == "ctx":
            tap_ctx = nc.dram_tensor("tap_ctx", [128, NH, T], BF16,
                                     kind="ExternalOutput").ap()
            HDBG = 0
            tap_kc = nc.dram_tensor("tap_kc", [128, SK], BF16, kind="ExternalOutput").ap()
            tap_kr = nc.dram_tensor("tap_kr", [RD, SK], BF16, kind="ExternalOutput").ap()
            tap_v = nc.dram_tensor("tap_v", [KC, 128, HD], BF16, kind="ExternalOutput").ap()
            tap_qc = nc.dram_tensor("tap_qc", [128, T], BF16, kind="ExternalOutput").ap()
            tap_qr = nc.dram_tensor("tap_qr", [RD, T], BF16, kind="ExternalOutput").ap()
            tap_bh = nc.dram_tensor("tap_bh", [128, NH], F32, kind="ExternalOutput").ap()
            tap_ag = nc.dram_tensor("tap_ag", [NCORES * KV, T], BF16, kind="ExternalOutput").ap()
            tap_agi = nc.dram_tensor("tap_agi", [KV, T], BF16, kind="ExternalOutput").ap()
            tap_co = nc.dram_tensor("tap_co", [128, FC], F32, kind="ExternalOutput").ap()
            tap_cv = nc.dram_tensor("tap_cv", [128, D], F32, kind="ExternalOutput").ap()

            @block.sync
            def _(s):
                P.wait(s, "en", P.val("en"))
                P.inc(s.dma_start(out=tap_ctx, in_=CTX[:]), "dbg3", 16)
                P.inc(s.dma_start(out=tap_kc, in_=kc_scr.ap()[HDBG]), "dbg3", 16)
                P.inc(s.dma_start(out=tap_kr, in_=kr_scr.ap()[HDBG]), "dbg3", 16)
                P.inc(s.dma_start(out=tap_v, in_=v_scr.ap()[HDBG]), "dbg3", 16)
                P.inc(s.dma_start(out=tap_qc, in_=qcs_scr.ap()[HDBG]), "dbg3", 16)
                P.inc(s.dma_start(out=tap_qr, in_=qrs_scr.ap()[HDBG]), "dbg3", 16)
                P.inc(s.dma_start(out=tap_ag, in_=ag_out.ap()), "dbg3", 16)
                P.inc(s.dma_start(out=tap_agi, in_=ag_in.ap()), "dbg3", 16)
                P.wait(s, "dbg3")

            @block.gpsimd
            def _(gp):
                P.wait(gp, "en", P.val("en"))
                P.inc(gp.dma_start(out=tap_bh, in_=BHS[:]), "dbg4", 16)
                P.inc(gp.dma_start(out=tap_co, in_=COUT[:]), "dbg4", 16)
                P.wait(gp, "dbg4")
            ge.close()
            return ("en", P.val("en"))
        done = ("f_rs", P.val("f_rs"))
        ge.close()
        return done

    gfw1 = ExitStack()
    f1W = ([gfw1.enter_context(nc.sbuf_tensor(f"f1W0B{i}", [128, FC, 128], FP8))
            for i in range(2)],
           [gfw1.enter_context(nc.sbuf_tensor(f"f1W1B{i}", [128, FC, 128], FP8))
            for i in range(2)])
    ln1_done = emit_ln("l1", XLN8, p0_done)
    ffn1_done = emit_ffn("f1", ln1_done, pred_ld=p0_done, wbufs=f1W)
    gfw1.close()

    if stop_after == "ffn1":
        final_pred = ffn1_done
    else:
        sub = stop_after if stop_after in ("ln2", "bphase", "prep", "gen", "ctx") else None
        attn_done = emit_attention(ffn1_done, sub=sub)
        if stop_after in ("attn", "ln2", "bphase", "prep", "gen", "ctx"):
            final_pred = attn_done
        else:
            gfw3 = ExitStack()
            f3W = ([gfw3.enter_context(nc.sbuf_tensor(f"f3W0B{i}", [128, FC, 128], FP8))
                    for i in range(2)],
                   [gfw3.enter_context(nc.sbuf_tensor(f"f3W1B{i}", [128, FC, 128], FP8))
                    for i in range(2)])
            ln3_done = emit_ln("l3", XLN8, attn_done)
            final_pred = emit_ffn("f3", ln3_done, pred_ld=attn_done, wbufs=f3W)
            gfw3.close()

    fin = ExitStack()
    OSTG = [fin.enter_context(nc.sbuf_tensor(f"OSTG{i}", [128, D], F32))
            for i in range(2)]
    PTO = [fin.enter_context(nc.psum_tensor(f"PTO{i}", [128, 128], F32))
           for i in range(2)]

    @block.tensor
    def _(t):
        P.wait(t, final_pred[0], final_pred[1])
        for tc in range(8):
            for fc in range(FC):
                n = tc * FC + fc
                if n >= 2:
                    t.wait_ge(P.sem("o_ev"), n - 1)
                P.inc(t.transpose(PTO[n % 2][:],
                                  XT[:, fc, tc * 128:(tc + 1) * 128], IDF[:]),
                      "o_tp", 1)

    @block.vector
    def _(v):
        for tc in range(8):
            for fc in range(FC):
                n = tc * FC + fc
                v.wait_ge(P.sem("o_tp"), n + 1)
                if fc == 0 and tc >= 2:
                    P.wait(v, f"o_st{tc % 2}", 16 * (tc // 2))
                P.inc(v.tensor_copy(OSTG[tc % 2][:, fc * 128:(fc + 1) * 128],
                                    PTO[n % 2][:]), "o_ev", 1)

    @block.sync
    def _(s):
        for tc in range(8):
            s.wait_ge(P.sem("o_ev"), (tc + 1) * FC)
            P.inc(s.dma_start(out=out[tc * 128:(tc + 1) * 128, :],
                              in_=OSTG[tc % 2][:]), f"o_st{tc % 2}", 16)
        s.wait_ge(P.sem("o_st0"), P.val("o_st0"))
        s.wait_ge(P.sem("o_st1"), P.val("o_st1"))

    fin.close()
    blk_cm.__exit__(None, None, None)
    ctx.close()
    return nc, {}


def make_inmaps(inp):
    NHRD = NH * RD
    inv_freq = 1.0 / 10000.0 ** (np.arange(0, RD, 2, dtype=np.float64) / RD)
    Rfull = np.zeros((NHRD, NHRD), dtype=np.float32)
    for h in range(NH):
        ang = h * inv_freq
        c, s = np.cos(ang), np.sin(ang)
        for m in range(RD // 2):
            b0 = h * RD + 2 * m
            Rfull[b0, b0] = c[m]; Rfull[b0, b0 + 1] = -s[m]
            Rfull[b0 + 1, b0] = s[m]; Rfull[b0 + 1, b0 + 1] = c[m]
    RT_ = Rfull.T
    RfT = np.zeros((8, 128, 128), dtype=ml_dtypes.bfloat16)
    for i in range(8):
        RfT[i] = RT_[i * 128:(i + 1) * 128, i * 128:(i + 1) * 128].astype(
            ml_dtypes.bfloat16)

    com = dict(
        w_kvd=inp["L0_attn_kv_down_w"],
        b_kvd_c=inp["L0_attn_kv_down_b"].reshape(CC, 128),
        w_ku=inp["L0_attn_key_up_w"],
        b_ku=inp["L0_attn_key_up_b"].reshape(FC, 128),
        w_vu=inp["L0_attn_value_up_w"],
        b_vu=inp["L0_attn_value_up_b"].reshape(1, D),
        w_kr=inp["L0_attn_key_rope_w"],
        b_kr=inp["L0_attn_key_rope_b"].reshape(8, 128),
        w_qd=inp["L0_attn_query_down_w"],
        b_qd=inp["L0_attn_query_down_b"].reshape(QCC, 128),
        w_qu=inp["L0_attn_query_up_w"],
        b_qu=inp["L0_attn_query_up_b"].reshape(FC, 128),
        w_qr=inp["L0_attn_query_rope_w"],
        b_qr=inp["L0_attn_query_rope_b"].reshape(8, 128),
        w_out=inp["L0_attn_out_w"],
        b_out=inp["L0_attn_out_b"].reshape(FC, 128),
        w_g0=np.ascontiguousarray(inp["L0_ffn_w_gating"][0]),
        w_g1=np.ascontiguousarray(inp["L0_ffn_w_gating"][1]),
        w_l=inp["L0_ffn_w_linear"],
        identf=np.eye(128, dtype=np.float32),
        identb=np.eye(128, dtype=ml_dtypes.bfloat16),
        rt=RfT,
    )
    x = np.asarray(inp["L0_attn_x"][0])
    maps = []
    for c in range(NCORES):
        m = dict(com)
        m["x"] = np.ascontiguousarray(x[c * T:(c + 1) * T, :])
        maps.append(m)
    return maps


def kernel(**inputs):
    from concourse.bass_utils import run_bass_kernel_spmd
    nc, _ = build()
    maps = make_inmaps(inputs)
    res = run_bass_kernel_spmd(nc, maps, core_ids=list(range(NCORES)))
    out = np.concatenate([res.results[c]["out"] for c in range(NCORES)], axis=0)
    return out.reshape(1, NCORES * T, D)


# version 04 (final)
# speedup vs baseline: 1.2499x; 1.1817x over previous
import numpy as np
import ml_dtypes
import concourse.bass as bass
import concourse.mybir as mybir
from contextlib import ExitStack

F32 = mybir.dt.float32
BF16 = mybir.dt.bfloat16
FP8 = mybir.dt.float8e4
AF = mybir.ActivationFunctionType
OP = mybir.AluOpType

NCORES = 8
T = 1024
D = 2048
FC = 16
J = 8192
JC = 64
NH, HD, RD = 16, 128, 64
KV, CC = 512, 4
QC, QCC = 1024, 8
SK, KC = 8192, 64
SCALE = float(1.0 / np.sqrt(HD + RD))
C44715 = float(np.float32(ml_dtypes.bfloat16(0.044715)))
SQ2PI = float(np.float32(np.sqrt(2 / np.pi).astype(ml_dtypes.bfloat16)))
EPS = 1e-5


class Prog:
    def __init__(self, nc, ctx):
        self.nc, self.ctx = nc, ctx
        self.sems, self.cnt = {}, {}

    def sem(self, name):
        if name not in self.sems:
            self.sems[name] = self.ctx.enter_context(self.nc.semaphore(name))
            self.cnt[name] = 0
        return self.sems[name]

    def inc(self, instr, name, n=1):
        instr.then_inc(self.sem(name), n)
        self.cnt[name] += n
        return instr

    def val(self, name):
        self.sem(name)
        return self.cnt[name]

    def wait(self, eng, name, val=None):
        eng.wait_ge(self.sem(name), self.cnt[name] if val is None else val)

    def ser_begin(self):
        self._ser = []

    def ser(self, eng_name, fn, inc=1):
        self._ser.append((eng_name, fn, inc))

    def ser_emit(self, block, sem_name, pred=None):
        bodies = {"sync": block.sync, "tensor": block.tensor,
                  "vector": block.vector, "scalar": block.scalar,
                  "gpsimd": block.gpsimd}
        ops = self._ser
        sm = self.sem(sem_name)
        starts = []
        v = self.cnt[sem_name]
        for (_, _, inc) in ops:
            starts.append(v)
            v += inc
        self.cnt[sem_name] = v
        for ename in ("sync", "tensor", "vector", "scalar", "gpsimd"):
            todo = [(i, f, inc) for i, (en, f, inc) in enumerate(ops)
                    if en == ename]
            if not todo:
                continue

            def mk(todo_list):
                def _body(eng):
                    if pred is not None:
                        eng.wait_ge(self.sem(pred[0]), pred[1])
                    for i, f, inc in todo_list:
                        eng.wait_ge(sm, starts[i])
                        f(eng).then_inc(sm, inc)
                return _body
            bodies[ename](mk(todo))
        return (sem_name, v)


def build(stop_after=None):
    nc = bass.Bass("TRN2", target_bir_lowering=False)
    ctx = ExitStack()
    P = Prog(nc, ctx)

    def dri(name, shape, dtype=F32):
        return nc.dram_tensor(name, shape, dtype, kind="ExternalInput").ap()

    x_in = dri("x", [T, D])
    w_kvd = dri("w_kvd", [KV, D])
    b_kvd_c = dri("b_kvd_c", [CC, 128])
    w_ku = dri("w_ku", [D, KV])
    b_ku = dri("b_ku", [FC, 128])
    w_vu = dri("w_vu", [D, KV])
    b_vu = dri("b_vu", [1, D])
    w_kr = dri("w_kr", [QC, KV])
    b_kr = dri("b_kr", [8, 128])
    w_qd = dri("w_qd", [QC, D])
    b_qd = dri("b_qd", [QCC, 128])
    w_qu = dri("w_qu", [D, QC])
    b_qu = dri("b_qu", [FC, 128])
    w_qr = dri("w_qr", [QC, QC])
    b_qr = dri("b_qr", [8, 128])
    w_out = dri("w_out", [D, D])
    b_out = dri("b_out", [FC, 128])
    w_g0 = dri("w_g0", [D, J])
    w_g1 = dri("w_g1", [D, J])
    w_l = dri("w_l", [J, D])
    identf = dri("identf", [128, 128])
    identb = dri("identb", [128, 128], BF16)
    rt = dri("rt", [8, 128, 128], BF16)

    out = nc.dram_tensor("out", [T, D], F32, kind="ExternalOutput").ap()

    ag_in = nc.dram_tensor("ag_in", [KV, T], BF16)
    ag_out = nc.dram_tensor("ag_out", [NCORES * KV, T], BF16)
    kc_scr = nc.dram_tensor("kc_scr", [NH, 128, SK], BF16)
    kr_scr = nc.dram_tensor("kr_scr", [NH, RD, SK], BF16)
    v_scr = nc.dram_tensor("v_scr", [NH, KC, 128, HD], BF16)
    qcs_scr = nc.dram_tensor("qcs_scr", [NH, 128, T], BF16)
    qrs_scr = nc.dram_tensor("qrs_scr", [NH, RD, T], BF16)
    wt_k = nc.dram_tensor("wt_k", [KV, D], BF16)
    wt_v = nc.dram_tensor("wt_v", [KV, D], BF16)
    wt_kr = nc.dram_tensor("wt_kr", [KV, QC], BF16)
    wt_q = nc.dram_tensor("wt_q", [QC, D], BF16)
    wt_qr = nc.dram_tensor("wt_qr", [QC, QC], BF16)
    wt_o = nc.dram_tensor("wt_o", [D, D], BF16)
    w0q = nc.dram_tensor("w0q", [JC // 2, 128, FC, 256], FP8)
    w1q = nc.dram_tensor("w1q", [JC // 2, 128, FC, 256], FP8)
    wlq = nc.dram_tensor("wlq", [8, 128, JC, 256], FP8)

    def sbp(n_, sh, dt):
        return ctx.enter_context(nc.sbuf_tensor(n_, sh, dt))

    XT = sbp("XT", [128, FC, T], F32)
    IDF = sbp("IDF", [128, 128], F32)
    IDB = sbp("IDB", [128, 128], BF16)
    RTS = sbp("RTS", [128, 8, 128], BF16)
    ONES = sbp("ONES", [128, 1], BF16)
    ONESM = sbp("ONESM", [1, 128], BF16)
    XLN8 = sbp("XLN8", [128, FC, T], FP8)
    QDB = sbp("QDB", [128, QCC], F32)
    QUB = sbp("QUB", [128, FC], F32)
    BQR = sbp("BQR", [128, 8], F32)
    CKC = sbp("CKC", [128, FC], F32)
    CKR = sbp("CKR", [128, 8], F32)
    COUT = sbp("COUT", [128, FC], F32)
    BHS = sbp("BHS", [128, NH], F32)


    blk_cm = nc.Block()
    block = blk_cm.__enter__()

    p0 = ExitStack()
    STG = [p0.enter_context(nc.sbuf_tensor(f"STG{i}", [128, D], F32)) for i in range(2)]
    PT = [p0.enter_context(nc.psum_tensor(f"PT{i}", [128, 128], F32)) for i in range(2)]

    @block.sync
    def _(s):
        P.inc(s.dma_start(out=IDF[:], in_=identf), "c_ld", 16)
        P.inc(s.dma_start(out=IDB[:], in_=identb), "c_ld", 16)
        P.inc(s.dma_start(out=RTS[:], in_=rt.rearrange("c p n -> p c n")), "c_ld", 16)
        for tc in range(8):
            if tc >= 2:
                P.wait(s, "x_tp", (tc - 1) * FC)
            P.inc(s.dma_start(out=STG[tc % 2][:],
                              in_=x_in[tc * 128:(tc + 1) * 128, :]),
                  f"x_ld{tc % 2}", 16)

    @block.vector
    def _(v):
        v.memset(ONES[:], 1.0)
        P.inc(v.memset(ONESM[:], 1.0), "c_init", 1)

    @block.tensor
    def _(t):
        t.wait_ge(P.sem("c_ld"), 48)
        for tc in range(8):
            t.wait_ge(P.sem(f"x_ld{tc % 2}"), 16 * (tc // 2 + 1))
            for fc in range(FC):
                n = tc * FC + fc
                if n >= 2:
                    t.wait_ge(P.sem("x_ev"), n - 1)
                P.inc(t.transpose(PT[n % 2][:],
                                  STG[tc % 2][:, fc * 128:(fc + 1) * 128],
                                  IDF[:]), "x_tp", 1)

    @block.vector
    def _(v):
        for tc in range(8):
            for fc in range(FC):
                n = tc * FC + fc
                v.wait_ge(P.sem("x_tp"), n + 1)
                P.inc(v.tensor_copy(XT[:, fc, tc * 128:(tc + 1) * 128],
                                    PT[n % 2][:]), "x_ev", 1)

    p0_done = ("x_ev", P.val("x_ev"))
    p0.close()

    def emit_ln(tag, OUTT, pred):
        g = ExitStack()
        nsb = lambda n_, sh, dt: g.enter_context(nc.sbuf_tensor(tag + n_, sh, dt))
        nps = lambda n_, sh, dt: g.enter_context(nc.psum_tensor(tag + n_, sh, dt))
        NB = 2
        SQ = nsb("SQ", [128, NB, T], F32)
        RA = nsb("RA", [128, NB, T], F32)
        HB = nsb("HB", [128, NB, T], BF16)
        LB = nsb("LB", [128, NB, T], BF16)
        L2B = nsb("L2B", [128, NB, T], BF16)
        ROWS = nsb("ROWS", [1, 6, T], F32)
        RWB = nsb("RWB", [1, 6, T], BF16)
        MRS = nsb("MRS", [128, T], F32)
        RRS = nsb("RRS", [128, T], F32)
        XMS = nsb("XMS", [128, T], F32)
        PSX = nps("PSX", [1, T], F32)
        PSXX = nps("PSXX", [1, T], F32)
        PM = nps("PM", [128, T], F32)
        PR = nps("PR", [128, T], F32)
        NBAT = FC // NB

        def row(i):
            return ROWS[:, i, :]

        sS, sM, sQ_, sF, sR, sO = ("ln_sp", "ln_mm", "ln_sq", "ln_fin",
                                   "ln_rep", "ln_out")
        bS, bM, bQ, bF, bR, bO = (P.val(sS), P.val(sM), P.val(sQ_),
                                  P.val(sF), P.val(sR), P.val(sO))

        @block.scalar
        def _(sc):
            P.wait(sc, pred[0], pred[1])
            for b in range(NBAT):
                if b > 0:
                    P.wait(sc, sS, bS + NBAT + b)
                P.inc(sc.activation(SQ[:], XT[:, NB * b:NB * (b + 1), :],
                                    AF.Square), sQ_, 1)

        @block.vector
        def _(v):
            P.wait(v, pred[0], pred[1])
            for b in range(2 * NBAT):
                if b < NBAT:
                    srcx = XT[:, NB * b:NB * (b + 1), :]
                else:
                    P.wait(v, sQ_, bQ + b - NBAT + 1)
                    srcx = SQ[:]
                if b >= 1:
                    P.wait(v, sM, bM + b)
                v.tensor_copy(HB[:], srcx)
                v.tensor_tensor(RA[:], srcx, HB[:], OP.subtract)
                v.tensor_copy(LB[:], RA[:])
                v.tensor_tensor(RA[:], RA[:], LB[:], OP.subtract)
                P.inc(v.tensor_copy(L2B[:], RA[:]), sS, 1)

        @block.tensor
        def _(t):
            P.wait(t, "c_init", 1)
            for b in range(2 * NBAT):
                ps = PSX if b < NBAT else PSXX
                t.wait_ge(P.sem(sS), bS + b + 1)
                li = None
                for buf in (HB, LB, L2B):
                    for fi in range(NB):
                        first = (b % NBAT == 0 and buf is HB and fi == 0)
                        last = (b % NBAT == NBAT - 1 and buf is L2B
                                and fi == NB - 1)
                        for hh in range(2):
                            li = t.matmul(ps[:, hh * 512:(hh + 1) * 512],
                                          ONES[:],
                                          buf[:, fi, hh * 512:(hh + 1) * 512],
                                          start=first, stop=last)
                P.inc(li, sM, 1)

        @block.scalar
        def _(sc):
            P.wait(sc, sM, bM + NBAT)
            P.inc(sc.activation(row(0), PSX[:], AF.Copy, scale=1.0 / D), sF, 1)
            P.wait(sc, sM, bM + 2 * NBAT)
            P.inc(sc.activation(row(1), PSXX[:], AF.Copy, scale=1.0 / D), sF, 1)

        @block.vector
        def _(v):
            P.wait(v, sF, bF + 2)
            v.tensor_tensor(row(2), row(0), row(0), OP.mult)
            v.tensor_tensor(row(1), row(1), row(2), OP.subtract)
            P.inc(v.tensor_scalar_add(row(1), row(1), EPS), sF, 1)

        @block.scalar
        def _(sc):
            P.wait(sc, sF, bF + 3)
            P.inc(sc.activation(row(3), row(1), AF.Sqrt), sF, 1)

        @block.vector
        def _(v):
            P.wait(v, sF, bF + 4)
            v.reciprocal(row(4), row(3))
            v.tensor_tensor(row(2), row(1), row(4), OP.mult)
            v.tensor_tensor(row(2), row(3), row(2), OP.add)
            v.tensor_scalar_mul(row(2), row(2), 0.5)
            v.reciprocal(row(4), row(2))
            v.tensor_tensor(row(3), row(2), row(4), OP.mult)
            v.tensor_scalar(row(3), row(3), 2.0, -1.0, OP.subtract, OP.mult)
            v.tensor_tensor(row(5), row(4), row(3), OP.mult)
            li = None
            for i, ridx in ((0, 0), (3, 5)):
                srcr = row(ridx)
                v.tensor_copy(RWB[:, i, :], srcr)
                v.tensor_tensor(row(3), srcr, RWB[:, i, :], OP.subtract)
                v.tensor_copy(RWB[:, i + 1, :], row(3))
                v.tensor_tensor(row(3), row(3), RWB[:, i + 1, :], OP.subtract)
                li = v.tensor_copy(RWB[:, i + 2, :], row(3))
            P.inc(li, sF, 1)

        @block.tensor
        def _(t):
            P.wait(t, sF, bF + 5)
            for ps, base in ((PM, 0), (PR, 3)):
                li = None
                for i in range(3):
                    for hh in range(2):
                        li = t.matmul(ps[:, hh * 512:(hh + 1) * 512], ONESM[:],
                                      RWB[:, base + i, hh * 512:(hh + 1) * 512],
                                      start=(i == 0), stop=(i == 2))
                P.inc(li, sR, 1)

        @block.vector
        def _(v):
            P.wait(v, sR, bR + 2)
            v.tensor_copy(MRS[:], PM[:])
            v.tensor_copy(RRS[:], PR[:])
            for fc in range(FC):
                v.tensor_tensor(XMS[:], XT[:, fc, :], MRS[:], OP.subtract)
                P.inc(v.tensor_tensor(OUTT[:, fc, :], XMS[:], RRS[:], OP.mult),
                      sO, 1)

        done = (sO, P.val(sO))
        g.close()
        return done

    def emit_ffn(tag, pred, mode="tee"):
        g = ExitStack()
        nsb = lambda n_, sh, dt: g.enter_context(nc.sbuf_tensor(tag + n_, sh, dt))
        W0B = [nsb(f"W0B{i}", [128, FC, 256], FP8) for i in range(2)]
        W1B = [nsb(f"W1B{i}", [128, FC, 256], FP8) for i in range(2)]
        GATE = [nsb(f"GATE{i}", [128, 1024], BF16) for i in range(2)]
        FF1B = [nsb(f"FF1B{i}", [128, 1024], BF16) for i in range(2)]
        ACT8 = nsb("ACT8", [128, JC * T], FP8)
        WLE = [nsb(f"WLE{i}", [128, JC, 256], FP8) for i in range(2)]
        gtm = ExitStack()
        TM1 = gtm.enter_context(nc.sbuf_tensor(tag + "TM1", [128, 1024], BF16))
        TM2 = gtm.enter_context(nc.sbuf_tensor(tag + "TM2", [128, 1024], BF16))

        sWL, sMM, sEV, sGD, sTH, sTI, sWE, sPM, sOB, sRS = (
            "ff_wl", "ff_mm", "ff_ev", "ff_gd", "ff_th", "ff_ti", "ff_we",
            "ff_pm", "ff_ob", "ff_rs")
        sTE, sTW = "ff_te", "ff_tw"
        bWL = [P.val(sWL + "0"), P.val(sWL + "1")]
        bMM, bEV, bGD, bTH, bTI = (P.val(sMM), P.val(sEV), P.val(sGD),
                                   P.val(sTH), P.val(sTI))
        bWE = [P.val(sWE + "0"), P.val(sWE + "1")]
        bPM, bOB, bRS = P.val(sPM), P.val(sOB), P.val(sRS)
        bTE = [P.val(sTE + "0"), P.val(sTE + "1")]
        bTW = [P.val(sTW + "0"), P.val(sTW + "1")]
        NPR = JC // 2
        w0r = w_g0.rearrange("(fc p) j -> p fc j", p=128)
        w1r = w_g1.rearrange("(fc p) j -> p fc j", p=128)
        wlr = w_l.rearrange("(kc p) e -> p kc e", p=128)

        @block.gpsimd
        def _(gp):
            P.wait(gp, pred[0], pred[1])
            if mode == "from8":
                for pp in range(2):
                    P.wait(gp, sTE + str(pp), P.val(sTE + str(pp)))
                    P.wait(gp, sTW + str(pp), P.val(sTW + str(pp)))

            def tee(q):
                P.wait(gp, sWL + str(q % 2), bWL[q % 2] + 32 * (q // 2 + 1))
                P.inc(gp.dma_start(out=w0q.ap()[q], in_=W0B[q % 2][:]),
                      sTE + str(q % 2), 16)
                P.inc(gp.dma_start(out=w1q.ap()[q], in_=W1B[q % 2][:]),
                      sTE + str(q % 2), 16)
            for pr in range(NPR):
                if pr >= 2:
                    P.wait(gp, sMM, bMM + 2 * pr - 2)
                    if mode == "tee":
                        P.wait(gp, sTE + str(pr % 2), bTE[pr % 2] + 32 * (pr // 2))
                if mode == "tee":
                    s0 = w0r[:, :, pr * 256:(pr + 1) * 256]
                    s1 = w1r[:, :, pr * 256:(pr + 1) * 256]
                else:
                    s0 = w0q.ap()[pr]
                    s1 = w1q.ap()[pr]
                P.inc(gp.dma_start(out=W0B[pr % 2][:], in_=s0), sWL + str(pr % 2), 16)
                P.inc(gp.dma_start(out=W1B[pr % 2][:], in_=s1), sWL + str(pr % 2), 16)
                if mode == "tee" and pr >= 1:
                    tee(pr - 1)
            if mode == "tee":
                tee(NPR - 1)

        g1 = ExitStack()
        PG = [g1.enter_context(nc.psum_tensor(tag + f"PG{i}", [128, 1024], F32))
              for i in range(2)]
        PF = [g1.enter_context(nc.psum_tensor(tag + f"PF{i}", [128, 1024], F32))
              for i in range(2)]

        @block.tensor
        def _(t):
            P.wait(t, pred[0], pred[1])
            for pr in range(NPR):
                t.wait_ge(P.sem(sWL + str(pr % 2)), bWL[pr % 2] + 32 * (pr // 2 + 1))
                for jco in range(2):
                    jc = 2 * pr + jco
                    if jc >= 2:
                        t.wait_ge(P.sem(sEV), bEV + 4 * (jc - 1))
                    li = None
                    for dst, wb in ((PG, W0B), (PF, W1B)):
                        for tt in range(2):
                            for fc in range(FC):
                                li = t.matmul(
                                    dst[jc % 2][:, tt * 512:(tt + 1) * 512],
                                    wb[pr % 2][:, fc, jco * 128:jco * 128 + 128],
                                    XLN8[:, fc, tt * 512:(tt + 1) * 512],
                                    start=(fc == 0), stop=(fc == FC - 1))
                    P.inc(li, sMM, 1)

        @block.scalar
        def _(sc):
            for jc in range(JC):
                gbuf = jc % 2
                P.wait(sc, sMM, bMM + jc + 1)
                if jc >= 2:
                    P.wait(sc, sGD, bGD + jc - 1)
                for tt in range(2):
                    P.inc(sc.activation(GATE[gbuf][:, tt * 512:(tt + 1) * 512],
                                        PG[jc % 2][:, tt * 512:(tt + 1) * 512],
                                        AF.Copy), sEV, 1)
                    P.inc(sc.activation(FF1B[gbuf][:, tt * 512:(tt + 1) * 512],
                                        PF[jc % 2][:, tt * 512:(tt + 1) * 512],
                                        AF.Copy), sEV, 1)
                P.wait(sc, sTI, bTI + jc + 1)
                P.inc(sc.activation(TM2[:], TM1[:], AF.Tanh), sTH, 1)

        @block.vector
        def _(v):
            for gi in range(JC):
                gbuf = gi % 2
                P.wait(v, sEV, bEV + (gi + 1) * 4)
                xg = GATE[gbuf][:]
                v.tensor_tensor(TM1[:], xg, xg, OP.mult)
                v.tensor_tensor(TM2[:], TM1[:], xg, OP.mult)
                v.tensor_scalar_mul(TM1[:], TM2[:], C44715)
                v.tensor_tensor(TM2[:], xg, TM1[:], OP.add)
                P.inc(v.tensor_scalar_mul(TM1[:], TM2[:], SQ2PI), sTI, 1)
                P.wait(v, sTH, bTH + gi + 1)
                v.tensor_scalar(TM1[:], TM2[:], 1.0, 0.5, OP.add, OP.mult)
                v.tensor_tensor(TM2[:], xg, TM1[:], OP.mult)
                v.tensor_tensor(TM1[:], TM2[:], FF1B[gbuf][:], OP.mult)
                P.inc(v.tensor_copy(ACT8[:, gi * 1024:(gi + 1) * 1024], TM1[:]),
                      sGD, 1)

        g1.close()
        gtm.close()
        OBF = [nsb(f"OBF{i}", [128, 512], BF16) for i in range(2)]
        PO = [g.enter_context(nc.psum_tensor(tag + f"PO{i}", [128, 512], F32))
              for i in range(3)]

        @block.gpsimd
        def _(gp):
            def teew(q):
                P.wait(gp, sWE + str(q % 2), bWE[q % 2] + 16 * (q // 2 + 1))
                P.inc(gp.dma_start(out=wlq.ap()[q], in_=WLE[q % 2][:]),
                      sTW + str(q % 2), 16)
            for p in range(8):
                if p >= 2:
                    P.wait(gp, sPM, bPM + 4 * p - 4)
                    if mode == "tee":
                        P.wait(gp, sTW + str(p % 2), bTW[p % 2] + 16 * (p // 2))
                if mode == "tee":
                    src = wlr[:, :, p * 256:(p + 1) * 256]
                else:
                    src = wlq.ap()[p]
                P.inc(gp.dma_start(out=WLE[p % 2][:], in_=src), sWE + str(p % 2), 16)
                if mode == "tee" and p >= 1:
                    teew(p - 1)
            if mode == "tee":
                teew(7)

        @block.tensor
        def _(t):
            P.wait(t, sGD, bGD + JC)
            for ec in range(FC):
                p = ec // 2
                t.wait_ge(P.sem(sWE + str(p % 2)), bWE[p % 2] + 16 * (p // 2 + 1))
                for tt in range(2):
                    n = ec * 2 + tt
                    if n >= 3:
                        t.wait_ge(P.sem(sOB), bOB + n - 2)
                    li = None
                    for jc in range(JC):
                        li = t.matmul(
                            PO[n % 3][:],
                            WLE[p % 2][:, jc, (ec % 2) * 128:(ec % 2) * 128 + 128],
                            ACT8[:, jc * 1024 + tt * 512:jc * 1024 + tt * 512 + 512],
                            start=(jc == 0), stop=(jc == JC - 1))
                    P.inc(li, sPM, 1)

        @block.scalar
        def _(sc):
            for ec in range(FC):
                for tt in range(2):
                    n = ec * 2 + tt
                    P.wait(sc, sPM, bPM + n + 1)
                    if n >= 2:
                        P.wait(sc, sRS, bRS + n - 1)
                    P.inc(sc.activation(OBF[n % 2][:], PO[n % 3][:], AF.Copy),
                          sOB, 1)

        @block.vector
        def _(v):
            for ec in range(FC):
                for tt in range(2):
                    n = ec * 2 + tt
                    P.wait(v, sOB, bOB + n + 1)
                    P.inc(v.scalar_tensor_tensor(
                        XT[:, ec, tt * 512:(tt + 1) * 512], OBF[n % 2][:], 0.5,
                        XT[:, ec, tt * 512:(tt + 1) * 512], OP.mult, OP.add),
                        sRS, 1)

        done = (sRS, P.val(sRS))
        g.close()
        return done

    def emit_attention(pred, sub=None):
        gat = ExitStack()
        QCL = gat.enter_context(nc.sbuf_tensor("QCL", [128, QCC, T], BF16))
        ga = ExitStack()
        XLN2 = ga.enter_context(nc.sbuf_tensor("XLN2", [128, FC, T], BF16))
        ln2_done = emit_ln("l2", XLN2, pred)
        if sub == "ln2":
            ga.close(); gat.close()
            return ln2_done

        gb = ExitStack()
        nsb = lambda n_, sh, dt: gb.enter_context(nc.sbuf_tensor(n_, sh, dt))
        nps = lambda n_, sh, dt: gb.enter_context(nc.psum_tensor(n_, sh, dt))
        KV0L = nsb("KV0L", [128, CC, T], BF16)
        BQDN = nsb("BQDN", [QCC, 128], F32)
        TPB = [nps(f"TPB{i}", [128, 128], BF16) for i in range(2)]
        TPFb = nps("TPFb", [128, 128], F32)
        PKV = [nps(f"PKV{i}", [128, 512], F32) for i in range(2)]

        @block.sync
        def _(s):
            P.wait(s, ln2_done[0], ln2_done[1])
            P.inc(s.dma_start(out=BQDN[:], in_=b_qd), "at_bl", 16)

        gb1 = ExitStack()
        WKVN = gb1.enter_context(nc.sbuf_tensor("WKVN", [128, CC, D], BF16))
        WKVT = gb1.enter_context(nc.sbuf_tensor("WKVT", [128, FC, KV], BF16))

        @block.gpsimd
        def _(gp):
            P.wait(gp, ln2_done[0], ln2_done[1])
            P.inc(gp.dma_start(out=WKVN[:], in_=w_kvd.rearrange(
                "(c p) f -> p c f", p=128)), "at_wl", 16)

        @block.tensor
        def _(t):
            P.wait(t, "at_wl", 16)
            n = 0
            for cc in range(CC):
                for fc in range(FC):
                    if n >= 2:
                        P.wait(t, "at_we", n - 1)
                    P.inc(t.transpose(TPB[n % 2][:],
                                      WKVN[:, cc, fc * 128:(fc + 1) * 128],
                                      IDB[:]), "at_wt", 1)
                    n += 1

        @block.vector
        def _(v):
            n = 0
            for cc in range(CC):
                for fc in range(FC):
                    v.wait_ge(P.sem("at_wt"), n + 1)
                    P.inc(v.tensor_copy(WKVT[:, fc, cc * 128:(cc + 1) * 128],
                                        TPB[n % 2][:]), "at_we", 1)
                    n += 1

        @block.tensor
        def _(t):
            P.wait(t, ln2_done[0], ln2_done[1])
            P.wait(t, "at_we", 64)
            n = 0
            for cc in range(CC):
                for tt in range(2):
                    if n >= 2:
                        P.wait(t, "kv_ev", n - 1)
                    li = None
                    for fc in range(FC):
                        li = t.matmul(PKV[n % 2][:],
                                      WKVT[:, fc, cc * 128:(cc + 1) * 128],
                                      XLN2[:, fc, tt * 512:(tt + 1) * 512],
                                      start=(fc == 0), stop=(fc == FC - 1))
                    P.inc(li, "kv_mm", 1)
                    n += 1

        @block.scalar
        def _(sc):
            n = 0
            for cc in range(CC):
                for tt in range(2):
                    P.wait(sc, "kv_mm", n + 1)
                    P.inc(sc.activation(KV0L[:, cc, tt * 512:(tt + 1) * 512],
                                        PKV[n % 2][:], AF.Copy), "kv_ev", 1)
                    n += 1

        @block.gpsimd
        def _(gp):
            P.wait(gp, "kv_ev", 8)
            P.inc(gp.dma_start(out=ag_in.ap().rearrange("(c p) t -> p c t", p=128),
                               in_=KV0L[:]), "ag_w", 16)
            gp.wait_ge(P.sem("ag_w"), 16)
            P.inc(gp.collective_compute(
                "AllGather", OP.bypass,
                replica_groups=[list(range(NCORES))],
                ins=[ag_in.ap().opt()], outs=[ag_out.ap().opt()]), "ag_c", 1)

        gb1.close()

        gb2 = ExitStack()
        WQDN = gb2.enter_context(nc.sbuf_tensor("WQDN", [128, QCC, D], BF16))
        WQDT = gb2.enter_context(nc.sbuf_tensor("WQDT", [128, FC, QC], BF16))

        @block.gpsimd
        def _(gp):
            P.wait(gp, "kv_mm", 8)
            P.inc(gp.dma_start(out=WQDN[:], in_=w_qd.rearrange(
                "(c p) f -> p c f", p=128)), "at_wl", 32)

        @block.tensor
        def _(t):
            t.wait_ge(P.sem("at_wl"), 48)
            n = 64
            for qcc in range(QCC):
                for fc in range(FC):
                    if n >= 2:
                        P.wait(t, "at_we", n - 1)
                    P.inc(t.transpose(TPB[n % 2][:],
                                      WQDN[:, qcc, fc * 128:(fc + 1) * 128],
                                      IDB[:]), "at_wt", 1)
                    n += 1
            P.wait(t, "at_bl", 16)
            P.inc(t.transpose(TPFb[:, 0:QCC], BQDN[:], IDF[0:QCC, 0:QCC]), "at_bt", 1)

        @block.vector
        def _(v):
            n = 64
            for qcc in range(QCC):
                for fc in range(FC):
                    v.wait_ge(P.sem("at_wt"), n + 1)
                    P.inc(v.tensor_copy(WQDT[:, fc, qcc * 128:(qcc + 1) * 128],
                                        TPB[n % 2][:]), "at_we", 1)
                    n += 1
            P.wait(v, "at_bt", 1)
            P.inc(v.tensor_copy(QDB[:], TPFb[:, 0:QCC]), "at_be", 1)

        @block.tensor
        def _(t):
            P.wait(t, "at_we", 192)
            P.wait(t, "at_be", 1)
            n = 8
            for qcc in range(QCC):
                for tt in range(2):
                    if n >= 2:
                        P.wait(t, "kv_ev", n - 1)
                    li = None
                    for fc in range(FC):
                        li = t.matmul(PKV[n % 2][:],
                                      WQDT[:, fc, qcc * 128:(qcc + 1) * 128],
                                      XLN2[:, fc, tt * 512:(tt + 1) * 512],
                                      start=(fc == 0), stop=(fc == FC - 1))
                    P.inc(li, "kv_mm", 1)
                    n += 1

        @block.scalar
        def _(sc):
            n = 8
            for qcc in range(QCC):
                for tt in range(2):
                    P.wait(sc, "kv_mm", n + 1)
                    P.inc(sc.activation(QCL[:, qcc, tt * 512:(tt + 1) * 512],
                                        PKV[n % 2][:], AF.Identity,
                                        bias=QDB[:, qcc:qcc + 1]), "kv_ev", 1)
                    n += 1

        gb2.close()

        b_done = ("kv_ev", P.val("kv_ev"))
        gb.close()
        ga.close()
        if sub == "bphase":
            gat.close()
            return b_done

        gco = ExitStack()
        osb = lambda n_, sh, dt: gco.enter_context(nc.sbuf_tensor(n_, sh, dt))
        ops_ = lambda n_, sh, dt: gco.enter_context(nc.psum_tensor(n_, sh, dt))
        CVREP = osb("CVREP", [128, D], F32)
        gcr = ExitStack()
        rsb = lambda n_, sh, dt: gcr.enter_context(nc.sbuf_tensor(n_, sh, dt))
        CROW = rsb("CROW", [1, 1, D], F32)
        CRB = rsb("CRB", [1, 3, D], BF16)
        SMLF = rsb("SMLF", [16, 128], F32)
        SMLB = rsb("SMLB", [CC, 3, 128], BF16)
        BKVT = rsb("BKVT", [128, 3, CC], BF16)
        BVU = rsb("BVU", [1, D], F32)
        BRB = rsb("BRB", [128, 8], BF16)
        gcp = ExitStack()
        cps_ = lambda n_, sh, dt: gcp.enter_context(nc.psum_tensor(n_, sh, dt))
        CPT = [cps_(f"CPT{i}", [128, 128], BF16) for i in range(3)]
        CPQ = cps_("CPQ", [128, D], F32)
        TPX = cps_("TPX", [128, 128], F32)

        P.ser_begin()
        P.ser("sync", lambda e: e.dma_start(out=SMLF[0:CC, :], in_=b_kvd_c), 16)
        P.ser("sync", lambda e: e.dma_start(out=BVU[:], in_=b_vu), 16)
        P.ser("vector", lambda e: e.tensor_copy(SMLB[:, 0, :], SMLF[0:CC, :]))
        P.ser("vector", lambda e: e.tensor_tensor(SMLF[0:CC, :], SMLF[0:CC, :],
                                                  SMLB[:, 0, :], OP.subtract))
        P.ser("vector", lambda e: e.tensor_copy(SMLB[:, 1, :], SMLF[0:CC, :]))
        P.ser("vector", lambda e: e.tensor_tensor(SMLF[0:CC, :], SMLF[0:CC, :],
                                                  SMLB[:, 1, :], OP.subtract))
        P.ser("vector", lambda e: e.tensor_copy(SMLB[:, 2, :], SMLF[0:CC, :]))
        for sp in range(3):
            P.ser("tensor", lambda e, sp=sp: e.transpose(
                CPT[0][:, 0:CC], SMLB[:, sp, :], IDB[0:CC, 0:CC]))
            P.ser("vector", lambda e, sp=sp: e.tensor_copy(
                BKVT[:, sp, :], CPT[0][:, 0:CC]))
        for (bsrc, bdst) in ((b_qu, QUB), (b_ku, CKC), (b_out, COUT)):
            P.ser("sync", lambda e, bsrc=bsrc: e.dma_start(out=SMLF[:], in_=bsrc), 16)
            P.ser("tensor", lambda e: e.transpose(TPX[:, 0:FC], SMLF[:], IDF[0:FC, 0:FC]))
            P.ser("vector", lambda e, bdst=bdst: e.tensor_copy(bdst[:], TPX[:, 0:FC]))
        for (bsrc, bdst) in ((b_qr, BQR), (b_kr, CKR)):
            P.ser("sync", lambda e, bsrc=bsrc: e.dma_start(out=SMLF[0:8, :], in_=bsrc), 16)
            P.ser("tensor", lambda e: e.transpose(TPX[:, 0:8], SMLF[0:8, :], IDF[0:8, 0:8]))
            P.ser("vector", lambda e: e.tensor_copy(BRB[:], TPX[:, 0:8]))
            for i in range(8):
                P.ser("tensor", lambda e, i=i: e.matmul(
                    TPX[:, i:i + 1], RTS[:, i, :], BRB[:, i:i + 1],
                    start=True, stop=True))
            P.ser("vector", lambda e, bdst=bdst: e.tensor_copy(bdst[:], TPX[:, 0:8]))
        cst_chain1 = P.ser_emit(block, "cst", pred=b_done)[1]

        gci = ExitStack()
        WNAT = gci.enter_context(nc.sbuf_tensor("WNAT", [128, FC, QC], BF16))
        WTS = gci.enter_context(nc.sbuf_tensor("WTS", [128, QCC, D], BF16))
        WEIGHTS = [
            ("ku", w_ku, FC, CC, CC * 128, wt_k, False, "ckc"),
            ("vu", w_vu, FC, CC, CC * 128, wt_v, False, "cv"),
            ("kr", w_kr, 8, CC, CC * 128, wt_kr, True, "ckr"),
            ("qu", w_qu, FC, QCC, QCC * 128, wt_q, False, None),
            ("qr", w_qr, 8, QCC, QCC * 128, wt_qr, True, None),
        ]
        prev = [None]
        for (nm, wsrc, nrc, ncc, wid, wdst, rot, cst) in WEIGHTS:
            pv = prev[0]
            bL = P.val("c_l"); bR_ = P.val("c_r"); bRC = P.val("c_rc")
            bT = P.val("c_t"); bE = P.val("c_e"); bSt = P.val("c_s")

            @block.gpsimd
            def _(gp, nm=nm, wsrc=wsrc, nrc=nrc, wid=wid, pv=pv):
                if pv is None:
                    P.wait(gp, b_done[0], b_done[1])
                    gp.wait_ge(P.sem("ag_w"), 16)
                else:
                    P.wait(gp, "c_t", pv[1])
                P.inc(gp.dma_start(out=WNAT[:, 0:nrc, 0:wid],
                                   in_=wsrc.rearrange("(rc p) c -> p rc c", p=128)),
                      "c_l", 16)

            if rot:
                @block.tensor
                def _(t, nm=nm, nrc=nrc, wid=wid, pv=pv):
                    t.wait_ge(P.sem("c_l"), bL + 16)
                    t.wait_ge(P.sem("cst"), pv[2] if pv is not None else cst_chain1)
                    for i in range(nrc):
                        if i >= 1:
                            P.wait(t, "c_rc", bRC + i)
                        li = None
                        for hq in range(wid // 512):
                            li = t.matmul(CPQ[:, hq * 512:(hq + 1) * 512],
                                          RTS[:, i, :],
                                          WNAT[:, i, hq * 512:(hq + 1) * 512],
                                          start=True, stop=True)
                        P.inc(li, "c_r", 1)

                @block.vector
                def _(v, nm=nm, nrc=nrc, wid=wid):
                    for i in range(nrc):
                        v.wait_ge(P.sem("c_r"), bR_ + i + 1)
                        P.inc(v.tensor_copy(WNAT[:, i, 0:wid], CPQ[:, 0:wid]),
                              "c_rc", 1)

            @block.tensor
            def _(t, nm=nm, nrc=nrc, ncc=ncc, rot=rot, pv=pv):
                if rot:
                    P.wait(t, "c_rc", bRC + nrc)
                else:
                    t.wait_ge(P.sem("c_l"), bL + 16)
                    t.wait_ge(P.sem("cst"),
                              pv[2] if pv is not None else cst_chain1)
                if pv is not None:
                    t.wait_ge(P.sem("c_s"), bSt)
                n = 0
                for rc in range(nrc):
                    for cc2 in range(ncc):
                        if n >= 3:
                            P.wait(t, "c_e", bE + n - 2)
                        P.inc(t.transpose(CPT[n % 3][:],
                                          WNAT[:, rc, cc2 * 128:(cc2 + 1) * 128],
                                          IDB[:]), "c_t", 1)
                        n += 1

            @block.vector
            def _(v, nm=nm, nrc=nrc, ncc=ncc):
                n = 0
                for rc in range(nrc):
                    for cc2 in range(ncc):
                        v.wait_ge(P.sem("c_t"), bT + n + 1)
                        P.inc(v.tensor_copy(WTS[:, cc2, rc * 128:(rc + 1) * 128],
                                            CPT[n % 3][:]), "c_e", 1)
                        n += 1

            @block.sync
            def _(s, nm=nm, nrc=nrc, ncc=ncc, wdst=wdst):
                P.wait(s, "c_e", P.val("c_e"))
                P.inc(s.dma_start(
                    out=wdst.rearrange("(cb p) d -> p cb d", p=128),
                    in_=WTS[:, 0:ncc, 0:nrc * 128]), "c_s", 16)

            if cst is not None:
                P.ser_begin()
                nq = (nrc * 128) // 512
                for cc2 in range(ncc):
                    for sp in range(3):
                        for hh in range(nq):
                            def mmf(e, cc2=cc2, sp=sp, hh=hh,
                                    fi=(cc2 == 0 and sp == 0),
                                    last=(cc2 == ncc - 1 and sp == 2)):
                                return e.matmul(
                                    CPQ[0:1, hh * 512:(hh + 1) * 512],
                                    BKVT[:, sp, cc2:cc2 + 1],
                                    WTS[:, cc2, hh * 512:(hh + 1) * 512],
                                    start=fi, stop=last)
                            P.ser("tensor", mmf)
                P.ser("vector", lambda e, nrc=nrc: e.tensor_copy(
                    CROW[:, 0, 0:nrc * 128], CPQ[0:1, 0:nrc * 128]))
                if cst == "cv":
                    P.ser("vector", lambda e: e.tensor_tensor(
                        CROW[:, 0, :], CROW[:, 0, :], BVU[:], OP.add))
                    P.ser("vector", lambda e: e.tensor_copy(CRB[:, 0, :], CROW[:, 0, :]))
                    P.ser("vector", lambda e: e.tensor_tensor(
                        CVREP[0:1, :], CROW[:, 0, :], CRB[:, 0, :], OP.subtract))
                    P.ser("vector", lambda e: e.tensor_copy(CRB[:, 1, :], CVREP[0:1, :]))
                    P.ser("vector", lambda e: e.tensor_tensor(
                        CVREP[0:1, :], CVREP[0:1, :], CRB[:, 1, :], OP.subtract))
                    P.ser("vector", lambda e: e.tensor_copy(CRB[:, 2, :], CVREP[0:1, :]))
                    for hh in range(4):
                        for sp in range(3):
                            P.ser("tensor", lambda e, sp=sp, hh=hh: e.matmul(
                                CPQ[:, hh * 512:(hh + 1) * 512], ONESM[:],
                                CRB[:, sp, hh * 512:(hh + 1) * 512],
                                start=(sp == 0), stop=(sp == 2)))
                    P.ser("vector", lambda e: e.tensor_copy(CVREP[:], CPQ[:]))
                else:
                    dstt = CKC if cst == "ckc" else CKR
                    nch = (nrc * 128) // 128
                    for i in range(nch):
                        P.ser("tensor", lambda e, i=i: e.transpose(
                            TPX[:, i:i + 1],
                            CROW[:, 0, i * 128:(i + 1) * 128], IDF[0:1, 0:1]))
                    P.ser("vector", lambda e, dstt=dstt, nch=nch: e.tensor_tensor(
                        dstt[:, 0:nch], dstt[:, 0:nch], TPX[:, 0:nch], OP.add))
                P.ser_emit(block, "cst", pred=("c_e", P.val("c_e")))
            prev[0] = (nm, P.val("c_t"), P.val("cst"))

        cs_total = P.val("c_s")
        wout_pred_ce = ("c_s", cs_total)
        cst_after_weights = P.val("cst")
        gci.close()

        gcr.close()
        gwo = ExitStack()
        WOTS = gwo.enter_context(nc.sbuf_tensor("WOTS", [128, FC, D], BF16))
        WONF = [gwo.enter_context(nc.sbuf_tensor(f"WONF{i}", [128, D], F32))
                for i in range(2)]
        WONB = gwo.enter_context(nc.sbuf_tensor("WONB", [128, D], BF16))
        SCRC = gwo.enter_context(nc.sbuf_tensor("SCRC", [128, D], BF16))
        CSCR = gwo.enter_context(nc.sbuf_tensor("CSCR", [128, FC], F32))

        @block.sync
        def _(s):
            P.wait(s, wout_pred_ce[0], wout_pred_ce[1])
            P.wait(s, "cst", cst_after_weights)
            for ec in range(FC):
                if ec >= 2:
                    P.wait(s, "wo_n", ec - 1)
                P.inc(s.dma_start(out=WONF[ec % 2][:],
                                  in_=w_out[ec * 128:(ec + 1) * 128, :]),
                      "wo_l" + str(ec % 2), 16)

        @block.vector
        def _(v):
            for ec in range(FC):
                v.wait_ge(P.sem("wo_l" + str(ec % 2)), 16 * (ec // 2 + 1))
                P.inc(v.scalar_tensor_tensor(SCRC[:], WONF[ec % 2][:], 1.0,
                                             CVREP[:], OP.mult, OP.mult,
                                             accum_out=CSCR[:, ec:ec + 1]),
                      "wo_c", 1)
                if ec >= 1:
                    P.wait(v, "wo_t", ec * FC)
                P.inc(v.tensor_copy(WONB[:], WONF[ec % 2][:]), "wo_n", 1)
                for rc in range(FC):
                    nn = ec * FC + rc
                    v.wait_ge(P.sem("wo_t"), nn + 1)
                    P.inc(v.tensor_copy(WOTS[:, rc, ec * 128:(ec + 1) * 128],
                                        CPT[nn % 2][:]), "wo_e", 1)
            P.wait(v, "wo_c", FC)
            P.inc(v.tensor_tensor(COUT[:], COUT[:], CSCR[:], OP.add), "wo_f", 1)

        @block.tensor
        def _(t):
            for ec in range(FC):
                P.wait(t, "wo_n", ec + 1)
                for rc in range(FC):
                    nn = ec * FC + rc
                    if nn >= 2:
                        P.wait(t, "wo_e", nn - 1)
                    P.inc(t.transpose(CPT[nn % 2][:],
                                      WONB[:, rc * 128:(rc + 1) * 128], IDB[:]),
                          "wo_t", 1)

        @block.sync
        def _(s):
            P.wait(s, "wo_e", FC * FC)
            P.inc(s.dma_start(out=wt_o.rearrange("(cb p) d -> p cb d", p=128),
                              in_=WOTS[:]), "wo_s", 16)

        if sub == "ctx":
            tap_cv2 = nc.dram_tensor("tap_cv2", [128, D], F32,
                                     kind="ExternalOutput").ap()
            tap_cs2 = nc.dram_tensor("tap_cs2", [128, FC], F32,
                                     kind="ExternalOutput").ap()
            tap_wts = nc.dram_tensor("tap_wts", [KV, D], BF16,
                                     kind="ExternalOutput").ap()

            @block.gpsimd
            def _(gp):
                P.wait(gp, "wo_f", 1)
                P.wait(gp, "wo_s", 16)
                P.inc(gp.dma_start(out=tap_cv2, in_=CVREP[:]), "dbg5", 16)
                P.inc(gp.dma_start(out=tap_cs2, in_=CSCR[:]), "dbg5", 16)
                P.inc(gp.dma_start(out=tap_wts, in_=wt_v.ap()), "dbg5", 16)
                P.wait(gp, "dbg5")
        gwo.close()
        cst_total = P.val("cst")
        gcp.close()
        gco.close()
        if sub == "prep":
            gat.close()
            return ("cst", cst_total)

        gd = ExitStack()
        dsb = lambda n_, sh, dt: gd.enter_context(nc.sbuf_tensor(n_, sh, dt))
        dps = lambda n_, sh, dt: gd.enter_context(nc.psum_tensor(n_, sh, dt))
        KV0SB = dsb("KV0SB", [128, CC, SK], BF16)
        WKH = [dsb(f"WKH{i}", [128, CC, 128], BF16) for i in range(2)]
        WVH = [dsb(f"WVH{i}", [128, CC, 128], BF16) for i in range(2)]
        WKRH = [dsb(f"WKRH{i}", [128, CC, RD], BF16) for i in range(2)]
        WQH = [dsb(f"WQH{i}", [128, QCC, 128], BF16) for i in range(2)]
        WQRH = [dsb(f"WQRH{i}", [128, QCC, RD], BF16) for i in range(2)]
        KSTG = [dsb(f"KSTG{i}", [128, 512], BF16) for i in range(2)]
        KRSTG = [dsb(f"KRSTG{i}", [RD, 512], BF16) for i in range(2)]
        VSTG = [dsb(f"VSTG{i}", [128, 512], BF16) for i in range(2)]
        QSTG = dsb("QSTG", [128, T], BF16)
        QRSTG = dsb("QRSTG", [RD, T], BF16)
        MXALL = dsb("MXALL", [128, NH], F32)
        MXH = dsb("MXH", [16, 2], F32)
        MXB = dsb("MXB", [1, 16], BF16)
        PKC = [dps(f"PKC{i}", [128, 512], F32) for i in range(2)]
        PKR = [dps(f"PKR{i}", [RD, 512], F32) for i in range(2)]
        PV = [dps(f"PV{i}", [128, 512], F32) for i in range(2)]
        PSS = [dps(f"PSS{i}", [128, 512], F32) for i in range(2)]

        @block.gpsimd
        def _(gp):
            gp.wait_ge(P.sem("ag_c"), 1)
            P.wait(gp, "wo_s", 16)
            P.wait(gp, "wo_f", 1)
            for cc in range(CC):
                for r in range(NCORES):
                    P.inc(gp.dma_start(
                        out=KV0SB[:, cc, r * T:(r + 1) * T],
                        in_=ag_out.ap()[r * KV + cc * 128:r * KV + (cc + 1) * 128, :]),
                        "kv0_l", 16)

        NT2 = 16

        for h in range(NH):
            hb = (h % 2) * RD
            hc2 = h // 2

            @block.sync
            def _(s, h=h):
                if h >= 2:
                    P.wait(s, "g_hd", h - 1)
                if h == 0:
                    P.wait(s, "c_s", cs_total)
                    P.wait(s, "wo_s", 16)
                    P.wait(s, "wo_f", 1)
                for (buf, wsrc, wid2) in ((WKH, wt_k, 128), (WVH, wt_v, 128),
                                          (WKRH, wt_kr, RD), (WQH, wt_q, 128),
                                          (WQRH, wt_qr, RD)):
                    P.inc(s.dma_start(
                        out=buf[h % 2][:],
                        in_=wsrc[:, h * wid2:(h + 1) * wid2].rearrange(
                            "(cc p) d -> p cc d", p=128)),
                        "dw" + str(h % 2), 16)

            @block.tensor
            def _(t, h=h):
                t.wait_ge(P.sem("dw" + str(h % 2)), 80 * (h // 2 + 1))
                if h == 0:
                    P.wait(t, "kv0_l", 16 * CC * NCORES)
                    P.wait(t, "wo_s", 16)
                    P.wait(t, "wo_f", 1)
                for ti in range(2):
                    if h >= 1 or ti >= 1:
                        pass
                    if h >= 1:
                        t.wait_ge(P.sem("g_ke"), NT2 * (h - 1) + 15 + ti)
                    li = None
                    for qcc in range(QCC):
                        li = t.matmul(PKC[ti][:], WQH[h % 2][:, qcc, :],
                                      QCL[:, qcc, ti * 512:(ti + 1) * 512],
                                      start=(qcc == 0), stop=(qcc == QCC - 1))
                    P.inc(li, "g_qm", 1)
                for ti in range(2):
                    if h >= 1:
                        t.wait_ge(P.sem("g_re"), NT2 * (h - 1) + 15 + ti)
                    li = None
                    for qcc in range(QCC):
                        li = t.matmul(PKR[ti][:], WQRH[h % 2][:, qcc, :],
                                      QCL[:, qcc, ti * 512:(ti + 1) * 512],
                                      start=(qcc == 0), stop=(qcc == QCC - 1))
                    P.inc(li, "g_qm", 1)

            @block.scalar
            def _(sc, h=h, hb=hb, hc2=hc2):
                if h == 0:
                    P.wait(sc, "cst", cst_total)
                    P.wait(sc, "wo_s", 16)
                    P.wait(sc, "wo_f", 1)
                sc.wait_ge(P.sem("sq"), 16 * h)
                sc.wait_ge(P.sem("g_sm"), h)
                for ti in range(2):
                    sc.wait_ge(P.sem("g_qm"), 4 * h + ti + 1)
                    P.inc(sc.activation(QSTG[:, ti * 512:(ti + 1) * 512],
                                        PKC[ti][:], AF.Identity,
                                        bias=QUB[:, h:h + 1]), "g_qe", 1)
                sc.wait_ge(P.sem("sqr"), 16 * h)
                for ti in range(2):
                    sc.wait_ge(P.sem("g_qm"), 4 * h + ti + 3)
                    P.inc(sc.activation(QRSTG[0:RD, ti * 512:(ti + 1) * 512],
                                        PKR[ti][0:RD, :], AF.Identity,
                                        bias=BQR[hb:hb + RD, hc2:hc2 + 1]),
                          "g_qe", 1)

            @block.sync
            def _(s, h=h):
                s.wait_ge(P.sem("g_qe"), 4 * h + 2)
                P.inc(s.dma_start(out=qcs_scr.ap()[h], in_=QSTG[:]), "sq", 16)
                s.wait_ge(P.sem("g_qe"), 4 * h + 4)
                P.inc(s.dma_start(out=qrs_scr.ap()[h], in_=QRSTG[:]), "sqr", 16)

            @block.tensor
            def _(t, h=h):
                t.wait_ge(P.sem("g_qe"), 4 * h + 1)
                li = None
                for cc in range(CC):
                    li = t.matmul(PKC[0][:], WKH[h % 2][:, cc, :],
                                  KV0SB[:, cc, 0:512], start=(cc == 0),
                                  stop=(cc == CC - 1))
                P.inc(li, "g_km", 1)
                t.wait_ge(P.sem("g_qe"), 4 * h + 3)
                li = None
                for cc in range(CC):
                    li = t.matmul(PKR[0][:], WKRH[h % 2][:, cc, :],
                                  KV0SB[:, cc, 0:512], start=(cc == 0),
                                  stop=(cc == CC - 1))
                P.inc(li, "g_rm", 1)

            @block.scalar
            def _(sc, h=h, hb=hb, hc2=hc2):
                sc.wait_ge(P.sem("g_km"), NT2 * h + 1)
                sc.wait_ge(P.sem("skc0"), 16 * 8 * h)
                P.inc(sc.activation(KSTG[0][:], PKC[0][:], AF.Identity,
                                    bias=CKC[:, h:h + 1]), "g_ke", 1)
                sc.wait_ge(P.sem("g_rm"), NT2 * h + 1)
                sc.wait_ge(P.sem("skr0"), 16 * 8 * h)
                P.inc(sc.activation(KRSTG[0][0:RD, :], PKR[0][0:RD, :],
                                    AF.Identity,
                                    bias=CKR[hb:hb + RD, hc2:hc2 + 1]),
                      "g_re", 1)

            @block.tensor
            def _(t, h=h):
                t.wait_ge(P.sem("g_re"), NT2 * h + 1)
                if h >= 2:
                    t.wait_ge(P.sem("g_mx"), h - 1)
                t.matmul(PSS[h % 2][:], KSTG[0][:, 0:128], QSTG[:, 0:512],
                         start=True, stop=False)
                P.inc(t.matmul(PSS[h % 2][:], KRSTG[0][:, 0:128],
                               QRSTG[:, 0:512], start=False, stop=True),
                      "g_sm", 1)

            @block.vector
            def _(v, h=h):
                P.wait(v, "g_sm", h + 1)
                P.inc(v.reduce_max(MXALL[:, h:h + 1], PSS[h % 2][:],
                                   mybir.AxisListType.X), "g_mx", 1)

            @block.tensor
            def _(t, h=h):
                def vmm(kt):
                    t.wait_ge(P.sem("g_ve"),
                              NT2 * h + kt - 1 if kt >= 2 else 16 * h)
                    li = None
                    for ci in range(4):
                        for cc in range(CC):
                            li = t.matmul(
                                PV[kt % 2][:, ci * 128:(ci + 1) * 128],
                                KV0SB[:, cc, (kt * 4 + ci) * 128:(kt * 4 + ci + 1) * 128],
                                WVH[h % 2][:, cc, :],
                                start=(cc == 0), stop=(cc == CC - 1))
                    P.inc(li, "g_vm", 1)

                vmm(0)
                for kt in range(1, NT2):
                    if kt == 1:
                        t.wait_ge(P.sem("g_qe"), 4 * h + 2)
                    else:
                        t.wait_ge(P.sem("g_ke"), NT2 * h + kt - 1)
                    li = None
                    for cc in range(CC):
                        li = t.matmul(PKC[kt % 2][:], WKH[h % 2][:, cc, :],
                                      KV0SB[:, cc, kt * 512:(kt + 1) * 512],
                                      start=(cc == 0), stop=(cc == CC - 1))
                    P.inc(li, "g_km", 1)
                    if kt == 1:
                        t.wait_ge(P.sem("g_qe"), 4 * h + 4)
                    else:
                        t.wait_ge(P.sem("g_re"), NT2 * h + kt - 1)
                    li = None
                    for cc in range(CC):
                        li = t.matmul(PKR[kt % 2][:], WKRH[h % 2][:, cc, :],
                                      KV0SB[:, cc, kt * 512:(kt + 1) * 512],
                                      start=(cc == 0), stop=(cc == CC - 1))
                    P.inc(li, "g_rm", 1)
                    vmm(kt)
                P.inc(t.nop(), "g_hd", 1)

            @block.scalar
            def _(sc, h=h, hb=hb, hc2=hc2):
                def vev(kt):
                    sc.wait_ge(P.sem("g_vm"), NT2 * h + kt + 1)
                    sc.wait_ge(P.sem("sv" + str(kt % 2)),
                               16 * (8 * h + kt // 2))
                    P.inc(sc.activation(VSTG[kt % 2][:], PV[kt % 2][:],
                                        AF.Copy), "g_ve", 1)

                vev(0)
                for kt in range(1, NT2):
                    sc.wait_ge(P.sem("g_km"), NT2 * h + kt + 1)
                    sc.wait_ge(P.sem("skc" + str(kt % 2)),
                               16 * (8 * h + kt // 2))
                    if kt == 2:
                        sc.wait_ge(P.sem("g_sm"), h + 1)
                    P.inc(sc.activation(KSTG[kt % 2][:], PKC[kt % 2][:],
                                        AF.Identity, bias=CKC[:, h:h + 1]),
                          "g_ke", 1)
                    sc.wait_ge(P.sem("g_rm"), NT2 * h + kt + 1)
                    sc.wait_ge(P.sem("skr" + str(kt % 2)),
                               16 * (8 * h + kt // 2))
                    if kt == 2:
                        sc.wait_ge(P.sem("g_sm"), h + 1)
                    P.inc(sc.activation(KRSTG[kt % 2][0:RD, :],
                                        PKR[kt % 2][0:RD, :], AF.Identity,
                                        bias=CKR[hb:hb + RD, hc2:hc2 + 1]),
                          "g_re", 1)
                    vev(kt)

            @block.sync
            def _(s, h=h):
                def vst(kt):
                    s.wait_ge(P.sem("g_ve"), NT2 * h + kt + 1)
                    P.inc(s.dma_start(
                        out=v_scr.ap()[h, kt * 4:(kt + 1) * 4].rearrange(
                            "kc p d -> p kc d"),
                        in_=VSTG[kt % 2][:].rearrange("p (kc d) -> p kc d", kc=4)),
                        "sv" + str(kt % 2), 16)

                vst(0)
                s.wait_ge(P.sem("g_ke"), NT2 * h + 1)
                P.inc(s.dma_start(
                    out=kc_scr.ap()[h, :, 0:512],
                    in_=KSTG[0][:]), "skc0", 16)
                s.wait_ge(P.sem("g_re"), NT2 * h + 1)
                P.inc(s.dma_start(
                    out=kr_scr.ap()[h, :, 0:512],
                    in_=KRSTG[0][:]), "skr0", 16)
                for kt in range(1, NT2):
                    s.wait_ge(P.sem("g_ke"), NT2 * h + kt + 1)
                    P.inc(s.dma_start(
                        out=kc_scr.ap()[h, :, kt * 512:(kt + 1) * 512],
                        in_=KSTG[kt % 2][:]), "skc" + str(kt % 2), 16)
                    s.wait_ge(P.sem("g_re"), NT2 * h + kt + 1)
                    P.inc(s.dma_start(
                        out=kr_scr.ap()[h, :, kt * 512:(kt + 1) * 512],
                        in_=KRSTG[kt % 2][:]), "skr" + str(kt % 2), 16)
                    vst(kt)

        P.ser_begin()
        _gke_t = P.val("g_ke"); _gre_t = P.val("g_re"); _gve_t = P.val("g_ve")

        def _bh_first(e):
            e.wait_ge(P.sem("g_ke"), _gke_t)
            e.wait_ge(P.sem("g_re"), _gre_t)
            e.wait_ge(P.sem("g_ve"), _gve_t)
            return e.transpose(PKR[0][0:16, 0:128], MXALL[:], IDF[:])
        P.ser("tensor", _bh_first)
        P.ser("vector", lambda e: e.reduce_max(MXH[:, 0:1], PKR[0][0:16, 0:128],
                                               mybir.AxisListType.X))
        P.ser("vector", lambda e: e.tensor_scalar_mul(MXH[:, 1:2], MXH[:, 0:1],
                                                      -SCALE))
        P.ser("tensor", lambda e: e.transpose(PKC[0][0:1, 0:16], MXH[:, 1:2],
                                              IDF[0:16, 0:16]))
        P.ser("vector", lambda e: e.tensor_copy(MXB[:], PKC[0][0:1, 0:16]))
        P.ser("tensor", lambda e: e.matmul(PKC[1][:, 0:16], ONESM[:], MXB[:],
                                           start=True, stop=True))
        P.ser("vector", lambda e: e.tensor_copy(BHS[:], PKC[1][:, 0:16]))
        P.ser_emit(block, "bh", pred=("g_mx", NH))

        gen_waits = [(nm, P.val(nm)) for nm in
                     ("skc0", "skc1", "skr0", "skr1", "sv0", "sv1", "sq", "sqr")]
        bh_total = P.val("bh")
        gd.close()
        gat.close()
        if sub == "gen":
            tap_kc = nc.dram_tensor("tap_kc", [128, SK], BF16, kind="ExternalOutput").ap()
            tap_kr = nc.dram_tensor("tap_kr", [RD, SK], BF16, kind="ExternalOutput").ap()
            tap_v = nc.dram_tensor("tap_v", [KC, 128, HD], BF16, kind="ExternalOutput").ap()
            tap_qc = nc.dram_tensor("tap_qc", [128, T], BF16, kind="ExternalOutput").ap()
            tap_qr = nc.dram_tensor("tap_qr", [RD, T], BF16, kind="ExternalOutput").ap()
            tap_ag = nc.dram_tensor("tap_ag", [NCORES * KV, T], BF16, kind="ExternalOutput").ap()
            tap_bh = nc.dram_tensor("tap_bh", [128, NH], F32, kind="ExternalOutput").ap()
            tap_ck = nc.dram_tensor("tap_ck", [128, FC], F32, kind="ExternalOutput").ap()
            tap_co = nc.dram_tensor("tap_co", [128, FC], F32, kind="ExternalOutput").ap()
            HDBG = 3

            @block.sync
            def _(s):
                for nm_, vv_ in [("skc0", None), ("skc1", None), ("skr0", None),
                                 ("skr1", None), ("sv0", None), ("sv1", None),
                                 ("sq", None), ("sqr", None)]:
                    P.wait(s, nm_)
                P.wait(s, "bh", P.val("bh"))
                P.inc(s.dma_start(out=tap_kc, in_=kc_scr.ap()[HDBG]), "dbg", 16)
                P.inc(s.dma_start(out=tap_kr, in_=kr_scr.ap()[HDBG]), "dbg", 16)
                P.inc(s.dma_start(out=tap_v, in_=v_scr.ap()[HDBG]), "dbg", 16)
                P.inc(s.dma_start(out=tap_qc, in_=qcs_scr.ap()[HDBG]), "dbg", 16)
                P.inc(s.dma_start(out=tap_qr, in_=qrs_scr.ap()[HDBG]), "dbg", 16)
                P.inc(s.dma_start(out=tap_ag, in_=ag_out.ap()), "dbg", 16)
                P.wait(s, "dbg")

            @block.gpsimd
            def _(gp):
                P.wait(gp, "bh", P.val("bh"))
                P.inc(gp.dma_start(out=tap_bh, in_=BHS[:]), "dbg2", 16)
                P.inc(gp.dma_start(out=tap_ck, in_=CKC[:]), "dbg2", 16)
                P.inc(gp.dma_start(out=tap_co, in_=COUT[:]), "dbg2", 16)
                P.wait(gp, "dbg2")
            return ("g_hd", P.val("g_hd"))

        ge = ExitStack()
        esb = lambda n_, sh, dt: ge.enter_context(nc.sbuf_tensor(n_, sh, dt))
        eps_ = lambda n_, sh, dt: ge.enter_context(nc.psum_tensor(n_, sh, dt))
        CTX = esb("CTX", [128, NH, T], BF16)
        KCB = [esb(f"KCB{i}", [128, SK // 2], BF16) for i in range(2)]
        KRB = [esb(f"KRB{i}", [128, SK // 2], BF16) for i in range(2)]
        VB = [esb(f"VB{i}", [128, 32, 128], BF16) for i in range(2)]
        QCB = [esb(f"QCB{i}", [128, T], BF16) for i in range(2)]
        QRB = [esb(f"QRB{i}", [RD, T], BF16) for i in range(2)]
        QRB2 = [esb(f"QRB2{i}", [128, T], BF16) for i in range(2)]
        EG = [esb(f"EG{i}", [128, 1024], BF16) for i in range(2)]
        RZROW = esb("RZROW", [1, 1024], F32)
        RZB = esb("RZB", [1, 1024], BF16)
        RZS = esb("RZS", [128, 1024], BF16)
        gep = ExitStack()
        eps2 = lambda n_, sh, dt: gep.enter_context(nc.psum_tensor(n_, sh, dt))
        SG = [eps2(f"SG{i}", [128, 1024], F32) for i in range(2)]
        CP = [eps2(f"CP{i}", [128, 512], F32) for i in range(2)]
        ZP = [eps2(f"ZP{i}", [1, 512], F32) for i in range(2)]

        @block.sync
        def _(s):
            for nm, vv in gen_waits:
                s.wait_ge(P.sem(nm), vv)
            for m in range(2 * NH):
                h, half = divmod(m, 2)
                if m >= 2:
                    P.wait(s, "e_cd", m - 1)
                P.inc(s.dma_start(out=KCB[m % 2][:],
                                  in_=kc_scr.ap()[h, :, half * 4096:(half + 1) * 4096]),
                      "e_ld" + str(m % 2), 16)
                P.inc(s.dma_start(out=KRB[m % 2][0:RD, :],
                                  in_=kr_scr.ap()[h, :, half * 4096:(half + 1) * 4096]),
                      "e_ld" + str(m % 2), 16)
                P.inc(s.dma_start(out=KRB[m % 2][RD:128, :],
                                  in_=kr_scr.ap()[h, :, half * 4096:(half + 1) * 4096]),
                      "e_ld" + str(m % 2), 16)
                P.inc(s.dma_start(out=VB[m % 2][:],
                                  in_=v_scr.ap()[h, half * 32:(half + 1) * 32].rearrange(
                                      "kc p d -> p kc d")),
                      "e_ld" + str(m % 2), 16)
                if half == 0:
                    P.inc(s.dma_start(out=QCB[h % 2][:], in_=qcs_scr.ap()[h]),
                          "e_ld" + str(m % 2), 16)
                    P.inc(s.dma_start(out=QRB[h % 2][:], in_=qrs_scr.ap()[h]),
                          "e_ld" + str(m % 2), 16)
                    P.inc(s.dma_start(out=QRB2[h % 2][64:128, :], in_=qrs_scr.ap()[h]),
                          "e_ld" + str(m % 2), 16)

        for h in range(NH):
            @block.tensor
            def _(t, h=h):
                if h == 0:
                    P.wait(t, "bh", bh_total)
                for half in range(2):
                    m = 2 * h + half
                    t.wait_ge(P.sem("e_ld" + str(m % 2)),
                              (112 if m % 2 == 0 else 64) * (h + 1))
                    for qt in range(2):
                        if half == 0:
                            t.wait_ge(P.sem("en"), 8 * h)
                        pend = []
                        for g in range(16):
                            egi = (m * 2 + qt) * 16 + g
                            if egi >= 2:
                                t.wait_ge(P.sem("e_ex"), egi - 1)
                            kc0, kc1 = 2 * g, 2 * g + 1
                            t.matmul(SG[egi % 2][:, 0:512],
                                     KCB[m % 2][:, kc0 * 128:(kc0 + 1) * 128],
                                     QCB[h % 2][:, qt * 512:(qt + 1) * 512],
                                     start=True, stop=False)
                            t.matmul(SG[egi % 2][:, 512:1024],
                                     KCB[m % 2][:, kc1 * 128:(kc1 + 1) * 128],
                                     QCB[h % 2][:, qt * 512:(qt + 1) * 512],
                                     start=True, stop=False)
                            t.matmul(SG[egi % 2][:, 0:512],
                                     KRB[m % 2][0:RD, kc0 * 128:(kc0 + 1) * 128],
                                     QRB[h % 2][:, qt * 512:(qt + 1) * 512],
                                     start=False, stop=True)
                            li = t.matmul(SG[egi % 2][:, 512:1024],
                                          KRB[m % 2][RD:128, kc1 * 128:(kc1 + 1) * 128],
                                          QRB2[h % 2][64:128, qt * 512:(qt + 1) * 512],
                                          start=False, stop=True)
                            P.inc(li, "e_sm", 1)
                            pend.append((egi, g))
                            if len(pend) >= 2:
                                pegi, pg = pend.pop(0)
                                t.wait_ge(P.sem("e_ex"), pegi + 1)
                                li2 = None
                                for i in range(2):
                                    kc_ = 2 * pg + i
                                    st = (half == 0 and pg == 0 and i == 0)
                                    sp = (half == 1 and pg == 15 and i == 1)
                                    li2 = t.matmul(CP[qt][:],
                                                   VB[m % 2][:, kc_, :],
                                                   EG[pegi % 2][:, i * 512:(i + 1) * 512],
                                                   start=st, stop=sp)
                                    li2 = t.matmul(ZP[qt][:], ONES[:],
                                                   EG[pegi % 2][:, i * 512:(i + 1) * 512],
                                                   start=st, stop=sp)
                                P.inc(li2, "e_cz", 1)
                        pegi, pg = pend.pop(0)
                        t.wait_ge(P.sem("e_ex"), pegi + 1)
                        li2 = None
                        for i in range(2):
                            kc_ = 2 * pg + i
                            st = (half == 0 and pg == 0 and i == 0)
                            sp = (half == 1 and pg == 15 and i == 1)
                            li2 = t.matmul(CP[qt][:], VB[m % 2][:, kc_, :],
                                           EG[pegi % 2][:, i * 512:(i + 1) * 512],
                                           start=st, stop=sp)
                            li2 = t.matmul(ZP[qt][:], ONES[:],
                                           EG[pegi % 2][:, i * 512:(i + 1) * 512],
                                           start=st, stop=sp)
                        P.inc(li2, "e_cz", 1)
                    P.inc(t.nop(), "e_cd", 1)

            @block.scalar
            def _(sc, h=h):
                P.wait(sc, "bh", bh_total)
                for half in range(2):
                    m = 2 * h + half
                    for qt in range(2):
                        for g in range(16):
                            egi = (m * 2 + qt) * 16 + g
                            sc.wait_ge(P.sem("e_sm"), egi + 1)
                            if egi >= 2:
                                sc.wait_ge(P.sem("e_cz"), egi - 1)
                            P.inc(sc.activation(EG[egi % 2][:], SG[egi % 2][:],
                                                AF.Exp, bias=BHS[:, h:h + 1],
                                                scale=SCALE), "e_ex", 1)

            m1 = 2 * h + 1
            ecz_after = m1 * 32 + 32
            P.ser_begin()
            P.ser("vector", lambda e: e.reciprocal(RZROW[:, 0:512], ZP[0][:]))
            P.ser("vector", lambda e: e.reciprocal(RZROW[:, 512:1024], ZP[1][:]))
            P.ser("vector", lambda e: e.tensor_copy(RZB[:], RZROW[:]))
            P.ser("tensor", lambda e: e.matmul(SG[0][:, 0:512], ONESM[:],
                                               RZB[:, 0:512], start=True,
                                               stop=True))
            P.ser("tensor", lambda e: e.matmul(SG[0][:, 512:1024], ONESM[:],
                                               RZB[:, 512:1024], start=True,
                                               stop=True))
            P.ser("scalar", lambda e: e.activation(RZS[:], SG[0][:, 0:1024],
                                                   AF.Copy))
            P.ser("vector", lambda e, h=h: e.tensor_tensor(
                CTX[:, h, 0:512], CP[0][:], RZS[:, 0:512], OP.mult))
            P.ser("vector", lambda e, h=h: e.tensor_tensor(
                CTX[:, h, 512:1024], CP[1][:], RZS[:, 512:1024], OP.mult))
            P.ser_emit(block, "en", pred=("e_cz", ecz_after))

        en_total = P.val("en")
        gep.close()

        WOE = [ge.enter_context(nc.sbuf_tensor(f"WOE{i}", [128, FC, 128], BF16))
               for i in range(2)]
        POF = [ge.enter_context(nc.psum_tensor(f"POF{i}", [128, 512], F32))
               for i in range(2)]

        @block.gpsimd
        def _(gp):
            P.wait(gp, "wo_s", 16)
            P.wait(gp, "en", en_total)
            for ec in range(FC):
                if ec >= 2:
                    P.wait(gp, "f_pm", 2 * (ec - 1))
                P.inc(gp.dma_start(
                    out=WOE[ec % 2][:],
                    in_=wt_o[:, ec * 128:(ec + 1) * 128].rearrange(
                        "(hc p) e -> p hc e", p=128)), "f_wl" + str(ec % 2), 16)

        @block.tensor
        def _(t):
            P.wait(t, "en", en_total)
            for ec in range(FC):
                t.wait_ge(P.sem("f_wl" + str(ec % 2)), 16 * (ec // 2 + 1))
                for tt in range(2):
                    n = ec * 2 + tt
                    if n >= 2:
                        t.wait_ge(P.sem("f_rs"), n - 1)
                    li = None
                    for hc in range(FC):
                        li = t.matmul(POF[n % 2][:], WOE[ec % 2][:, hc, :],
                                      CTX[:, hc, tt * 512:(tt + 1) * 512],
                                      start=(hc == 0), stop=(hc == FC - 1))
                    P.inc(li, "f_pm", 1)

        @block.vector
        def _(v):
            P.wait(v, "wo_f", 1)
            for ec in range(FC):
                for tt in range(2):
                    n = ec * 2 + tt
                    v.wait_ge(P.sem("f_pm"), n + 1)
                    P.inc(v.scalar_tensor_tensor(
                        XT[:, ec, tt * 512:(tt + 1) * 512], POF[n % 2][:],
                        COUT[:, ec:ec + 1],
                        XT[:, ec, tt * 512:(tt + 1) * 512], OP.add, OP.add),
                        "f_rs", 1)

        if sub == "ctx":
            tap_ctx = nc.dram_tensor("tap_ctx", [128, NH, T], BF16,
                                     kind="ExternalOutput").ap()
            HDBG = 0
            tap_kc = nc.dram_tensor("tap_kc", [128, SK], BF16, kind="ExternalOutput").ap()
            tap_kr = nc.dram_tensor("tap_kr", [RD, SK], BF16, kind="ExternalOutput").ap()
            tap_v = nc.dram_tensor("tap_v", [KC, 128, HD], BF16, kind="ExternalOutput").ap()
            tap_qc = nc.dram_tensor("tap_qc", [128, T], BF16, kind="ExternalOutput").ap()
            tap_qr = nc.dram_tensor("tap_qr", [RD, T], BF16, kind="ExternalOutput").ap()
            tap_bh = nc.dram_tensor("tap_bh", [128, NH], F32, kind="ExternalOutput").ap()
            tap_ag = nc.dram_tensor("tap_ag", [NCORES * KV, T], BF16, kind="ExternalOutput").ap()
            tap_agi = nc.dram_tensor("tap_agi", [KV, T], BF16, kind="ExternalOutput").ap()
            tap_co = nc.dram_tensor("tap_co", [128, FC], F32, kind="ExternalOutput").ap()
            tap_cv = nc.dram_tensor("tap_cv", [128, D], F32, kind="ExternalOutput").ap()

            @block.sync
            def _(s):
                P.wait(s, "en", P.val("en"))
                P.inc(s.dma_start(out=tap_ctx, in_=CTX[:]), "dbg3", 16)
                P.inc(s.dma_start(out=tap_kc, in_=kc_scr.ap()[HDBG]), "dbg3", 16)
                P.inc(s.dma_start(out=tap_kr, in_=kr_scr.ap()[HDBG]), "dbg3", 16)
                P.inc(s.dma_start(out=tap_v, in_=v_scr.ap()[HDBG]), "dbg3", 16)
                P.inc(s.dma_start(out=tap_qc, in_=qcs_scr.ap()[HDBG]), "dbg3", 16)
                P.inc(s.dma_start(out=tap_qr, in_=qrs_scr.ap()[HDBG]), "dbg3", 16)
                P.inc(s.dma_start(out=tap_ag, in_=ag_out.ap()), "dbg3", 16)
                P.inc(s.dma_start(out=tap_agi, in_=ag_in.ap()), "dbg3", 16)
                P.wait(s, "dbg3")

            @block.gpsimd
            def _(gp):
                P.wait(gp, "en", P.val("en"))
                P.inc(gp.dma_start(out=tap_bh, in_=BHS[:]), "dbg4", 16)
                P.inc(gp.dma_start(out=tap_co, in_=COUT[:]), "dbg4", 16)
                P.wait(gp, "dbg4")
            ge.close()
            return ("en", P.val("en"))
        done = ("f_rs", P.val("f_rs"))
        ge.close()
        return done

    ln1_done = emit_ln("l1", XLN8, p0_done)
    ffn1_done = emit_ffn("f1", ln1_done, mode="tee")

    if stop_after == "ffn1":
        final_pred = ffn1_done
    else:
        sub = stop_after if stop_after in ("ln2", "bphase", "prep", "gen", "ctx") else None
        attn_done = emit_attention(ffn1_done, sub=sub)
        if stop_after in ("attn", "ln2", "bphase", "prep", "gen", "ctx"):
            final_pred = attn_done
        else:
            ln3_done = emit_ln("l3", XLN8, attn_done)
            final_pred = emit_ffn("f3", ln3_done, mode="from8")

    fin = ExitStack()
    OSTG = [fin.enter_context(nc.sbuf_tensor(f"OSTG{i}", [128, D], F32))
            for i in range(2)]
    PTO = [fin.enter_context(nc.psum_tensor(f"PTO{i}", [128, 128], F32))
           for i in range(2)]

    @block.tensor
    def _(t):
        P.wait(t, final_pred[0], final_pred[1])
        for tc in range(8):
            for fc in range(FC):
                n = tc * FC + fc
                if n >= 2:
                    t.wait_ge(P.sem("o_ev"), n - 1)
                P.inc(t.transpose(PTO[n % 2][:],
                                  XT[:, fc, tc * 128:(tc + 1) * 128], IDF[:]),
                      "o_tp", 1)

    @block.vector
    def _(v):
        for tc in range(8):
            for fc in range(FC):
                n = tc * FC + fc
                v.wait_ge(P.sem("o_tp"), n + 1)
                if fc == 0 and tc >= 2:
                    P.wait(v, f"o_st{tc % 2}", 16 * (tc // 2))
                P.inc(v.tensor_copy(OSTG[tc % 2][:, fc * 128:(fc + 1) * 128],
                                    PTO[n % 2][:]), "o_ev", 1)

    @block.sync
    def _(s):
        for tc in range(8):
            s.wait_ge(P.sem("o_ev"), (tc + 1) * FC)
            P.inc(s.dma_start(out=out[tc * 128:(tc + 1) * 128, :],
                              in_=OSTG[tc % 2][:]), f"o_st{tc % 2}", 16)
        s.wait_ge(P.sem("o_st0"), P.val("o_st0"))
        s.wait_ge(P.sem("o_st1"), P.val("o_st1"))

    fin.close()
    blk_cm.__exit__(None, None, None)
    ctx.close()
    return nc, {}


def make_inmaps(inp):
    NHRD = NH * RD
    inv_freq = 1.0 / 10000.0 ** (np.arange(0, RD, 2, dtype=np.float64) / RD)
    Rfull = np.zeros((NHRD, NHRD), dtype=np.float32)
    for h in range(NH):
        ang = h * inv_freq
        c, s = np.cos(ang), np.sin(ang)
        for m in range(RD // 2):
            b0 = h * RD + 2 * m
            Rfull[b0, b0] = c[m]; Rfull[b0, b0 + 1] = -s[m]
            Rfull[b0 + 1, b0] = s[m]; Rfull[b0 + 1, b0 + 1] = c[m]
    RT_ = Rfull.T
    RfT = np.zeros((8, 128, 128), dtype=ml_dtypes.bfloat16)
    for i in range(8):
        RfT[i] = RT_[i * 128:(i + 1) * 128, i * 128:(i + 1) * 128].astype(
            ml_dtypes.bfloat16)

    com = dict(
        w_kvd=inp["L0_attn_kv_down_w"],
        b_kvd_c=inp["L0_attn_kv_down_b"].reshape(CC, 128),
        w_ku=inp["L0_attn_key_up_w"],
        b_ku=inp["L0_attn_key_up_b"].reshape(FC, 128),
        w_vu=inp["L0_attn_value_up_w"],
        b_vu=inp["L0_attn_value_up_b"].reshape(1, D),
        w_kr=inp["L0_attn_key_rope_w"],
        b_kr=inp["L0_attn_key_rope_b"].reshape(8, 128),
        w_qd=inp["L0_attn_query_down_w"],
        b_qd=inp["L0_attn_query_down_b"].reshape(QCC, 128),
        w_qu=inp["L0_attn_query_up_w"],
        b_qu=inp["L0_attn_query_up_b"].reshape(FC, 128),
        w_qr=inp["L0_attn_query_rope_w"],
        b_qr=inp["L0_attn_query_rope_b"].reshape(8, 128),
        w_out=inp["L0_attn_out_w"],
        b_out=inp["L0_attn_out_b"].reshape(FC, 128),
        w_g0=np.ascontiguousarray(inp["L0_ffn_w_gating"][0]),
        w_g1=np.ascontiguousarray(inp["L0_ffn_w_gating"][1]),
        w_l=inp["L0_ffn_w_linear"],
        identf=np.eye(128, dtype=np.float32),
        identb=np.eye(128, dtype=ml_dtypes.bfloat16),
        rt=RfT,
    )
    x = np.asarray(inp["L0_attn_x"][0])
    maps = []
    for c in range(NCORES):
        m = dict(com)
        m["x"] = np.ascontiguousarray(x[c * T:(c + 1) * T, :])
        maps.append(m)
    return maps


def kernel(**inputs):
    from concourse.bass_utils import run_bass_kernel_spmd
    nc, _ = build()
    maps = make_inmaps(inputs)
    res = run_bass_kernel_spmd(nc, maps, core_ids=list(range(NCORES)))
    out = np.concatenate([res.results[c]["out"] for c in range(NCORES)], axis=0)
    return out.reshape(1, NCORES * T, D)
